# version 05
# baseline (speedup 1.0000x reference)
import numpy as np
import ml_dtypes
import concourse.bass as bass
import concourse.mybir as mybir
from concourse.bass_utils import run_bass_kernel_spmd

F32 = mybir.dt.float32
BF16 = mybir.dt.bfloat16
I32 = mybir.dt.int32
AF = mybir.ActivationFunctionType
ALU = mybir.AluOpType

T = 4096
D = 2048
H = 16
DH = 128
NK = D // 128
NTAU = T // 512
NST = T // 128
RMS_EPS = 1e-6
LN_EPS = 1e-5
SOFTCAP = 50.0
SCL = (DH ** -0.5) / SOFTCAP
MAGIC = 0x5F3759DF

ENGS = ("sync", "tensor", "scalar", "vector", "gpsimd")
DEBUG_SCRATCH = False
_last_results = None


class Prog:
    def __init__(self):
        self.ops = {e: [] for e in ENGS}
        self.cnt = {}

    def wait(self, eng, sem, val):
        if val and val > 0:
            self.ops[eng].append(lambda e, s=sem, v=val: e.wait_ge(s, v))

    def do(self, eng, fn, inc=None, n=1):
        if inc is not None:
            self.cnt[inc] = self.cnt.get(inc, 0) + n
            tot = self.cnt[inc]
            self.ops[eng].append(lambda e, f=fn, s=inc, k=n: f(e).then_inc(s, k))
            return tot
        self.ops[eng].append(fn)
        return None

    def count(self, sem):
        return self.cnt.get(sem, 0)


def build_nc():
    nc = bass.Bass("TRN2", target_bir_lowering=False)

    xt = nc.dram_tensor("xt", [D, T], F32, kind="ExternalInput").ap()
    wqt = nc.dram_tensor("wqt", [D, D], F32, kind="ExternalInput").ap()
    wkt = nc.dram_tensor("wkt", [D, D], F32, kind="ExternalInput").ap()
    wvt = nc.dram_tensor("wvt", [D, D], F32, kind="ExternalInput").ap()
    wot = nc.dram_tensor("wot", [D, D], F32, kind="ExternalInput").ap()
    wft = nc.dram_tensor("wft", [D, D], F32, kind="ExternalInput").ap()
    cfin = nc.dram_tensor("cfin", [128, 50], F32, kind="ExternalInput").ap()
    maskin = nc.dram_tensor("maskin", [128, 896], BF16, kind="ExternalInput").ap()
    identin = nc.dram_tensor("identin", [128, 128], F32, kind="ExternalInput").ap()
    maskwin = nc.dram_tensor("maskwin", [128, 128], F32, kind="ExternalInput").ap()
    onesfin = nc.dram_tensor("onesfin", [1, 128], F32, kind="ExternalInput").ap()
    onesbin = nc.dram_tensor("onesbin", [128, 1], BF16, kind="ExternalInput").ap()
    yt = nc.dram_tensor("yt", [D, T], F32, kind="ExternalOutput").ap()

    kw = {"kind": "ExternalOutput"} if DEBUG_SCRATCH else {}
    q_scr = nc.dram_tensor("q_scr", [H, 128, T], BF16, **kw).ap()
    k_scr = nc.dram_tensor("k_scr", [H, 128, T], BF16, **kw).ap()
    v_scr = nc.dram_tensor("v_scr", [H, NST, 128, 128], BF16, **kw).ap()
    o_scr = nc.dram_tensor("o_scr", [H, 128, T], BF16, **kw).ap()
    y1_scr = nc.dram_tensor("y1_scr", [NK, 128, T], F32, **kw).ap()
    wt_scr = nc.dram_tensor("wt_scr", [NK, 128, D], BF16, **kw).ap()
    r_scr = nc.dram_tensor("r_scr", [2 * H, T], F32, **kw).ap()
    at_scr = nc.dram_tensor("at_scr", [128, 64], F32).ap()
    thr_scr = nc.dram_tensor("thr_scr", [2048], F32, **kw).ap()
    rec_scr = nc.dram_tensor("rec_scr", [2, 512], F32).ap()
    ln_scr = nc.dram_tensor("ln_scr", [2, 1024], F32).ap()
    u_dbg = nc.dram_tensor("u_dbg", [2 * H, 128, 32], F32, **kw).ap()
    r2_dbg = nc.dram_tensor("r2_dbg", [2 * H, 128, 32], F32, **kw).ap()
    sd_dbg = nc.dram_tensor("sd_dbg", [2 * H, 128, 32], F32, **kw).ap()
    it_dbg = nc.dram_tensor("it_dbg", [2 * H, 128, 32], F32, **kw).ap()
    z_dbg = nc.dram_tensor("z_dbg", [128, 8192], F32, **kw).ap()
    osb_dbg = nc.dram_tensor("osb_dbg", [128, 8192], F32, **kw).ap()
    xf_dbg = nc.dram_tensor("xf_dbg", [128, 8192], F32, **kw).ap()
    rall_dbg = nc.dram_tensor("rall_dbg", [128, 4096], BF16, **kw).ap()

    p = Prog()
    from contextlib import ExitStack
    es = ExitStack()
    with es:
        A = es.enter_context(nc.sbuf_tensor("A", [128, 65536], BF16))
        FS = es.enter_context(nc.sbuf_tensor("FS", [128, 8192], F32))
        Rall = es.enter_context(nc.sbuf_tensor("Rall", [128, 32 * 128], BF16))
        cf = es.enter_context(nc.sbuf_tensor("cf", [128, 50], F32))
        maskb = es.enter_context(nc.sbuf_tensor("maskb", [128, 896], BF16))
        identf = es.enter_context(nc.sbuf_tensor("identf", [128, 128], F32))
        maskw = es.enter_context(nc.sbuf_tensor("maskw", [128, 128], F32))
        onesrow = es.enter_context(nc.sbuf_tensor("onesrow", [1, 128], F32))
        onescol = es.enter_context(nc.sbuf_tensor("onescol", [128, 1], BF16))
        onescolf = es.enter_context(nc.sbuf_tensor("onescolf", [128, 1], F32))
        Atile = es.enter_context(nc.sbuf_tensor("Atile", [128, 64], F32))
        rtmp1 = es.enter_context(nc.sbuf_tensor("rtmp1", [128, 128], BF16))
        rtmp2 = es.enter_context(nc.sbuf_tensor("rtmp2", [128, 128], BF16))
        rbuf = es.enter_context(nc.sbuf_tensor("rbuf", [128, 32], F32))
        ntb = es.enter_context(nc.sbuf_tensor("ntb", [128, 32], F32))
        hhb = es.enter_context(nc.sbuf_tensor("hhb", [128, 32], F32))
        rT = es.enter_context(nc.sbuf_tensor("rT", [32, 128], F32))
        rbc = es.enter_context(nc.sbuf_tensor("rbc", [128, 1024], F32))
        abc = es.enter_context(nc.sbuf_tensor("abc", [128, 512], F32))
        bbc = es.enter_context(nc.sbuf_tensor("bbc", [128, 512], F32))
        qraw = es.enter_context(nc.sbuf_tensor("qraw", [128, 1024], BF16))
        sqbf = es.enter_context(nc.sbuf_tensor("sqbf", [128, 1024], BF16))
        qhat = es.enter_context(nc.sbuf_tensor("qhat", [128, 1024], BF16))
        recrow = es.enter_context(nc.sbuf_tensor("recrow", [1, 512], F32))
        bcsb = es.enter_context(nc.sbuf_tensor("bcsb", [128, 512], F32))
        zbf = es.enter_context(nc.sbuf_tensor("zbf", [128, 1024], BF16))
        sqz = es.enter_context(nc.sbuf_tensor("sqz", [128, 1024], BF16))
        y1t = es.enter_context(nc.sbuf_tensor("y1t", [128, 1024], F32))
        srow = es.enter_context(nc.sbuf_tensor("srow", [1, 2048], F32))
        PS = es.enter_context(nc.psum_tensor("PS", [128, 4096], F32))

        s_ld = es.enter_context(nc.semaphore("s_ld"))
        s_st = es.enter_context(nc.semaphore("s_st"))
        s_te = es.enter_context(nc.semaphore("s_te"))
        s_act = es.enter_context(nc.semaphore("s_act"))
        s_dve = es.enter_context(nc.semaphore("s_dve"))

        def bank(i, rows=128, width=512):
            return PS[0:rows, 512 * i: 512 * i + width]

        tS = FS[:, 0:2048]
        tP = FS[:, 2048:3072].bitcast(BF16)
        tPm = FS[:, 3072:4096].bitcast(BF16)
        obf = FS[:, 4096:4608].bitcast(BF16)
        gtv = A[:, 40960:43008]
        ltv = A[:, 43008:45056]
        absf = A[:, 45056:49152].bitcast(F32)
        vevac = FS[:, 5120:6144].bitcast(BF16)
        throw_ = FS[0:1, 6144:8192]

        LD = lambda fn: p.do("sync", fn, inc=s_ld, n=16)
        ST = lambda fn: p.do("gpsimd", fn, inc=s_st, n=16)
        TE = lambda fn, inc=False: p.do("tensor", fn, inc=s_te if inc else None)
        ACT = lambda fn, inc=True: p.do("scalar", fn, inc=s_act if inc else None)
        DVE = lambda fn, inc=False: p.do("vector", fn, inc=s_dve if inc else None)

        def newton_cols(src_ap, scale_mean, eps, dbg=None, dbg2=None, dbg3=None):
            def F(fn):
                c = DVE(fn, inc=True)
                p.wait("vector", s_dve, c)
                return c
            c_u = F(lambda e, s=src_ap, sm=scale_mean, ep=eps: e.tensor_scalar(
                out=ntb[:, 0:32], in0=s, scalar1=sm, scalar2=ep,
                op0=ALU.mult, op1=ALU.add))
            if dbg is not None:
                p.wait("gpsimd", s_dve, c_u)
                ST(lambda e, d=dbg: e.dma_start(d, ntb[:, 0:32]))
            F(lambda e: e.tensor_scalar(
                out=rbuf[:, 0:32].bitcast(I32), in0=ntb[:, 0:32].bitcast(I32),
                scalar1=1, scalar2=None, op0=ALU.arith_shift_right))
            F(lambda e: e.tensor_scalar(
                out=rbuf[:, 0:32].bitcast(I32), in0=rbuf[:, 0:32].bitcast(I32),
                scalar1=-1, scalar2=MAGIC, op0=ALU.mult, op1=ALU.add))
            for it_ in range(4):
                F(lambda e: e.tensor_tensor(hhb[:, 0:32], rbuf[:, 0:32],
                                            rbuf[:, 0:32], ALU.mult))
                F(lambda e: e.tensor_tensor(hhb[:, 0:32], hhb[:, 0:32],
                                            ntb[:, 0:32], ALU.mult))
                F(lambda e: e.tensor_scalar(out=hhb[:, 0:32], in0=hhb[:, 0:32],
                                            scalar1=-0.5, scalar2=1.5,
                                            op0=ALU.mult, op1=ALU.add))
                F(lambda e: e.tensor_tensor(rbuf[:, 0:32], rbuf[:, 0:32],
                                            hhb[:, 0:32], ALU.mult))
            return F(lambda e: e.tensor_copy(rbuf[:, 0:32], rbuf[:, 0:32]))

        def newton_row(src_ap, dst_ap, eps, w=512):
            u = srow[0:1, 512:512 + w]
            t = srow[0:1, 1024:1024 + w]
            def F(fn):
                c = DVE(fn, inc=True)
                p.wait("vector", s_dve, c)
                return c
            F(lambda e, s=src_ap, uu=u, ep=eps: e.tensor_scalar(
                out=uu, in0=s, scalar1=1.0, scalar2=ep, op0=ALU.mult, op1=ALU.add))
            F(lambda e, uu=u, y=dst_ap: e.tensor_scalar(
                out=y.bitcast(I32), in0=uu.bitcast(I32), scalar1=1, scalar2=None,
                op0=ALU.arith_shift_right))
            F(lambda e, y=dst_ap: e.tensor_scalar(
                out=y.bitcast(I32), in0=y.bitcast(I32), scalar1=-1, scalar2=MAGIC,
                op0=ALU.mult, op1=ALU.add))
            for _ in range(4):
                F(lambda e, y=dst_ap, tt=t: e.tensor_tensor(tt, y, y, ALU.mult))
                F(lambda e, uu=u, tt=t: e.tensor_tensor(tt, tt, uu, ALU.mult))
                F(lambda e, tt=t: e.tensor_scalar(out=tt, in0=tt, scalar1=-0.5,
                                                  scalar2=1.5, op0=ALU.mult,
                                                  op1=ALU.add))
                F(lambda e, y=dst_ap, tt=t: e.tensor_tensor(y, y, tt, ALU.mult))

        def phase_barrier():
            snap = dict(p.cnt)
            for eng in ENGS:
                for sem in (s_ld, s_st, s_te, s_act, s_dve):
                    p.wait(eng, sem, snap.get(sem, 0))

        LD(lambda e: e.dma_start(cf[:], cfin))
        LD(lambda e: e.dma_start(maskb[:], maskin))
        LD(lambda e: e.dma_start(identf[:], identin))
        LD(lambda e: e.dma_start(maskw[:], maskwin))
        LD(lambda e: e.dma_start(onesrow[:], onesfin))
        LD(lambda e: e.dma_start(onescol[:], onesbin))
        c_consts = LD(lambda e: e.dma_start(
            onescolf[:], onesfin.rearrange("o f -> f o")))

        p.wait("vector", s_ld, c_consts)
        DVE(lambda e: e.tensor_scalar(out=Atile[:, 0:16], in0=cf[:, 2:18],
                                      scalar1=cf[:, 0:1], scalar2=None, op0=ALU.mult))
        DVE(lambda e: e.tensor_scalar(out=Atile[:, 16:32], in0=cf[:, 18:34],
                                      scalar1=cf[:, 0:1], scalar2=None, op0=ALU.mult))
        DVE(lambda e: e.tensor_scalar(out=Atile[:, 32:48], in0=cf[:, 2:18],
                                      scalar1=cf[:, 1:2], scalar2=None, op0=ALU.mult))
        c_at = DVE(lambda e: e.tensor_scalar(out=Atile[:, 48:64], in0=cf[:, 18:34],
                                             scalar1=cf[:, 1:2], scalar2=None,
                                             op0=ALU.mult), inc=True)
        p.wait("gpsimd", s_dve, c_at)
        c_atst = ST(lambda e: e.dma_start(at_scr, Atile[:]))
        p.wait("sync", s_st, c_atst)
        c_atld = LD(lambda e: e.dma_start(Atile[:], at_scr))
        p.wait("vector", s_ld, c_atld)
        for hd in range(2 * H):
            base = 0 if hd < H else 32
            hc = hd % H
            DVE(lambda e, b=base, h_=hc: e.tensor_scalar(
                out=rtmp1[:], in0=identf[:], scalar1=Atile[:, b + h_:b + h_ + 1],
                scalar2=None, op0=ALU.mult))
            DVE(lambda e, b=base, h_=hc: e.tensor_scalar(
                out=rtmp2[:], in0=maskw[:], scalar1=Atile[:, b + 16 + h_:b + 17 + h_],
                scalar2=None, op0=ALU.mult))
            DVE(lambda e, hd_=hd: e.tensor_tensor(
                Rall[:, hd_ * 128:(hd_ + 1) * 128], rtmp1[:], rtmp2[:], ALU.add))

        if DEBUG_SCRATCH:
            c_rall = DVE(lambda e: e.tensor_copy(rtmp1[:, 0:1], rtmp1[:, 0:1]),
                         inc=True)
            p.wait("gpsimd", s_dve, c_rall)
            ST(lambda e: e.dma_start(rall_dbg, Rall[:]))
        te_abs = {}
        act_abs = {}
        res_hist = {}
        for k in range(NK):
            par = k % 2
            if k >= 2:
                p.wait("sync", s_act, act_abs[k - 2])
            c = LD(lambda e, kk=k, pr=par: e.dma_start(
                FS[:, pr * 2048:(pr + 1) * 2048], wft[kk * 128:(kk + 1) * 128, :]))
            p.wait("scalar", s_ld, c)
            if k >= 1:
                p.wait("scalar", s_te, te_abs[k - 1])
                p.wait("scalar", s_dve, res_hist[k - 1])
            act_abs[k] = ACT(lambda e, pr=par: e.activation(
                out=absf[:, 0:2048], in_=FS[:, pr * 2048:(pr + 1) * 2048],
                func=AF.Abs))
            p.wait("vector", s_act, act_abs[k])
            if k >= 1:
                p.wait("vector", s_te, te_abs[k - 1])
            c_hi = DVE(lambda e: e.tensor_copy(gtv[:, 0:2048], absf[:, 0:2048]),
                       inc=True)
            p.wait("vector", s_dve, c_hi)
            c_res = DVE(lambda e: e.tensor_tensor(ltv[:, 0:2048], absf[:, 0:2048],
                                                  gtv[:, 0:2048], ALU.subtract),
                        inc=True)
            res_hist[k] = c_res
            p.wait("tensor", s_dve, c_res)
            for g in range(4):
                TE(lambda e, gg=g, kk=k: e.matmul(
                    bank(gg, rows=1), onescol[:], gtv[:, gg * 512:(gg + 1) * 512],
                    start=(kk == 0), stop=(kk == NK - 1)), inc=False)
                TE(lambda e, gg=g, kk=k: e.matmul(
                    bank(4 + gg, rows=1), onescol[:], ltv[:, gg * 512:(gg + 1) * 512],
                    start=(kk == 0), stop=(kk == NK - 1)), inc=(g == 3))
            te_abs[k] = p.count(s_te)
        p.wait("vector", s_te, te_abs[NK - 1])
        thrbc = A[:, 57344:61440].bitcast(F32)
        nthrbc = A[:, 61440:65536].bitcast(F32)
        for g in range(4):
            DVE(lambda e, gg=g: e.tensor_scalar(
                out=srow[0:1, gg * 512:(gg + 1) * 512],
                in0=PS[0:1, gg * 512:(gg + 1) * 512],
                scalar1=0.7 / D, scalar2=None, op0=ALU.mult))
        c_s1 = DVE(lambda e: e.tensor_copy(srow[0:1, 0:1], srow[0:1, 0:1]), inc=True)
        p.wait("vector", s_dve, c_s1)
        for g in range(4):
            DVE(lambda e, gg=g: e.scalar_tensor_tensor(
                out=throw_[0:1, gg * 512:(gg + 1) * 512],
                in0=PS[0:1, 2048 + gg * 512:2048 + (gg + 1) * 512],
                scalar=0.7 / D, in1=srow[0:1, gg * 512:(gg + 1) * 512],
                op0=ALU.mult, op1=ALU.add))
        c_thr = DVE(lambda e: e.tensor_copy(srow[0:1, 0:1], srow[0:1, 0:1]), inc=True)
        p.wait("gpsimd", s_dve, c_thr)
        c_thrst = ST(lambda e: e.dma_start(thr_scr.rearrange("(o f) -> o f", o=1),
                                           throw_[0:1, :]))
        p.wait("sync", s_st, c_thrst)
        c_thrld = LD(lambda e: e.dma_start(thrbc[:, 0:2048],
                                           thr_scr.partition_broadcast(128)))
        p.wait("vector", s_ld, c_thrld)
        c_nthr = DVE(lambda e: e.tensor_scalar(out=nthrbc[:, 0:2048],
                                               in0=thrbc[:, 0:2048], scalar1=-1.0,
                                               scalar2=None, op0=ALU.mult), inc=True)
        wt_tile = FS[:, 4096:6144].bitcast(BF16)
        dve_q = {}
        for k in range(NK):
            par = k % 2
            if k >= 2:
                p.wait("sync", s_dve, dve_q[k - 2])
            c = LD(lambda e, kk=k, pr=par: e.dma_start(
                FS[:, pr * 2048:(pr + 1) * 2048], wft[kk * 128:(kk + 1) * 128, :]))
            p.wait("vector", s_ld, c)
            if k == 0:
                p.wait("vector", s_dve, c_nthr)
            DVE(lambda e, pr=par: e.tensor_tensor(
                gtv[:, 0:2048], FS[:, pr * 2048:(pr + 1) * 2048],
                thrbc[:, 0:2048], ALU.is_gt))
            DVE(lambda e, pr=par: e.tensor_tensor(
                ltv[:, 0:2048], FS[:, pr * 2048:(pr + 1) * 2048],
                nthrbc[:, 0:2048], ALU.is_lt))
            if k >= 2:
                p.wait("vector", s_st, p.count(s_st) - 16)
            dve_q[k] = DVE(lambda e, pr=par: e.tensor_tensor(
                wt_tile[:, pr * 2048:(pr + 1) * 2048], gtv[:, 0:2048],
                ltv[:, 0:2048], ALU.subtract), inc=True)
            p.wait("gpsimd", s_dve, dve_q[k])
            ST(lambda e, kk=k, pr=par: e.dma_start(
                wt_scr[kk], wt_tile[:, pr * 2048:(pr + 1) * 2048]))
        p.wait("sync", s_dve, dve_q[NK - 1])
        p.wait("sync", s_st, p.count(s_st))
        dve_x = {}
        for k in range(NK):
            par = k % 2
            if k >= 2:
                p.wait("sync", s_dve, dve_x[k - 2])
            c = LD(lambda e, kk=k, pr=par: e.dma_start(
                FS[:, pr * 4096:(pr + 1) * 4096], xt[kk * 128:(kk + 1) * 128, :]))
            p.wait("vector", s_ld, c)
            dve_x[k] = DVE(lambda e, kk=k, pr=par: e.tensor_copy(
                A[:, kk * 4096:(kk + 1) * 4096], FS[:, pr * 4096:(pr + 1) * 4096]),
                inc=True)
        phase_barrier()

        wvq = FS[:, 0:4096].bitcast(BF16)
        wvf = FS[:, 4096:5120]
        st_evac = {}
        dve_evac = {}
        gidx = 0
        dve_wv_all = {}
        for g4 in range(4):
            dve_wv = {}
            for k in range(NK):
                par = k % 2
                prev = dve_wv_all.get(g4 * NK + k - 2)
                if prev:
                    p.wait("sync", s_dve, prev)
                c = LD(lambda e, kk=k, g=g4, pr=par: e.dma_start(
                    wvf[:, pr * 512:(pr + 1) * 512],
                    wvt[kk * 128:(kk + 1) * 128, g * 512:(g + 1) * 512]))
                p.wait("vector", s_ld, c)
                if g4 > 0 and k == 0:
                    p.wait("vector", s_te, p.count(s_te))
                dve_wv[k] = DVE(lambda e, kk=k, pr=par: e.tensor_copy(
                    wvq[:, kk * 512:(kk + 1) * 512], wvf[:, pr * 512:(pr + 1) * 512]),
                    inc=True)
                dve_wv_all[g4 * NK + k] = dve_wv[k]
            p.wait("tensor", s_dve, dve_wv[NK - 1])
            for oct_ in range(4):
                if gidx >= 8:
                    p.wait("tensor", s_dve, dve_evac[gidx - 1])
                for k in range(NK):
                    for s8 in range(8):
                        st_ = oct_ * 8 + s8
                        TE(lambda e, kk=k, s=st_, b=s8: e.matmul(
                            bank(b), A[:, kk * 4096 + s * 128:kk * 4096 + (s + 1) * 128],
                            wvq[:, kk * 512:(kk + 1) * 512],
                            start=(kk == 0), stop=(kk == NK - 1)),
                           inc=(k == NK - 1 and s8 == 7))
                c_vmm = p.count(s_te)
                p.wait("vector", s_te, c_vmm)
                for s8 in range(8):
                    st_ = oct_ * 8 + s8
                    if gidx >= 2:
                        p.wait("vector", s_st, st_evac[gidx - 2])
                    dve_evac[gidx] = DVE(lambda e, b=s8: e.tensor_copy(
                        vevac[:, (b % 2) * 1024:(b % 2) * 1024 + 512], bank(b)), inc=True)
                    p.wait("gpsimd", s_dve, dve_evac[gidx])
                    for j in range(4):
                        ST(lambda e, hj=4 * g4 + j, s=st_, b=s8, jj=j: e.dma_start(
                            v_scr[hj, s],
                            vevac[:, (b % 2) * 1024 + jj * 128:(b % 2) * 1024 + (jj + 1) * 128]))
                    st_evac[gidx] = p.count(s_st)
                    gidx += 1
        phase_barrier()

        wpf = FS[:, 0:4096]
        wpb = FS[:, 4096:6144].bitcast(BF16)
        stg = FS[:, 6144:8192].bitcast(BF16)
        act_hist = {}
        dve_stage = {}
        te_rope = {}
        te_bc = {}
        dve_fin = {}
        st_fin = {}
        te_hd_end = {}
        cwb_hist = {}
        c_tr_hist = {}
        c_rst_hist = {}
        g_tau = 0
        g_fin = 0
        for hd in range(2 * H):
            proj = 0 if hd < H else 1
            h = hd % H
            wsrc = wqt if proj == 0 else wkt
            scr = q_scr if proj == 0 else k_scr
            wslot = hd % 2
            if hd >= 2:
                p.wait("sync", s_dve, cwb_hist[hd - 2])
            cw = LD(lambda e, w=wsrc, h_=h, sl=wslot: e.dma_start(
                wpf[:, sl * 2048:(sl + 1) * 2048].rearrange("p (k c) -> p k c", c=128),
                w[:, h_ * 128:(h_ + 1) * 128].rearrange("(k p) c -> p k c", p=128)))
            p.wait("vector", s_ld, cw)
            if hd >= 2:
                p.wait("vector", s_te, te_hd_end[hd - 2])
            cwb = DVE(lambda e, sl=wslot: e.tensor_copy(
                wpb[:, sl * 2048:(sl + 1) * 2048], wpf[:, sl * 2048:(sl + 1) * 2048]),
                inc=True)
            cwb_hist[hd] = cwb
            p.wait("tensor", s_dve, cwb)
            te_proj = {}

            def emit_proj(tau):
                pr = tau % 2
                gi = g_tau + tau
                if gi >= 2:
                    p.wait("tensor", s_act, act_hist[gi - 2])
                for k in range(NK):
                    TE(lambda e, kk=k, sl=wslot, t=tau, b=pr: e.matmul(
                        bank(b), wpb[:, sl * 2048 + kk * 128:sl * 2048 + (kk + 1) * 128],
                        A[:, kk * 4096 + t * 512:kk * 4096 + (t + 1) * 512],
                        start=(kk == 0), stop=(kk == NK - 1)), inc=(k == NK - 1))
                te_proj[tau] = p.count(s_te)

            emit_proj(0)
            for tau in range(NTAU):
                pr = tau % 2
                gi = g_tau + tau
                if tau + 1 < NTAU:
                    emit_proj(tau + 1)
                p.wait("scalar", s_te, te_proj[tau])
                ACT(lambda e, b=pr: e.activation(
                    out=qraw[:, b * 512:(b + 1) * 512], in_=bank(b), func=AF.Copy),
                    inc=False)
                act_hist[gi] = ACT(lambda e, b=pr: e.activation(
                    out=sqbf[:, b * 512:(b + 1) * 512], in_=bank(b), func=AF.Square))
                p.wait("tensor", s_act, act_hist[gi])
                if gi >= 2:
                    p.wait("tensor", s_dve, dve_stage[gi - 2])
                TE(lambda e, hd_=hd, b=pr: e.matmul(
                    bank(2 + b), Rall[:, hd_ * 128:(hd_ + 1) * 128],
                    qraw[:, b * 512:(b + 1) * 512], start=True, stop=True))
                for i4 in range(4):
                    col = 4 * tau + i4
                    TE(lambda e, b=pr, cc=col: e.matmul(
                        PS[:, 2048 + cc:2048 + cc + 1],
                        sqbf[:, b * 512 + (cc % 4) * 128:b * 512 + ((cc % 4) + 1) * 128],
                        onescol[:], start=True, stop=True), inc=(i4 == 3))
                te_rope[gi] = p.count(s_te)
                p.wait("vector", s_te, te_rope[gi])
                dve_stage[gi] = DVE(lambda e, b=pr, t=tau: e.tensor_copy(
                    stg[:, t * 512:(t + 1) * 512], bank(2 + b)), inc=True)
            g_tau += NTAU
            p.wait("vector", s_te, te_rope[g_tau - 1])
            if hd >= 1:
                p.wait("vector", s_te, c_tr_hist[hd - 1])
                p.wait("vector", s_st, c_rst_hist[hd - 1])
            c_newt = newton_cols(PS[:, 2048:2080], 1.0 / DH, RMS_EPS,
                                 dbg=(u_dbg[hd] if DEBUG_SCRATCH else None),
                                 dbg2=(sd_dbg[hd] if DEBUG_SCRATCH else None),
                                 dbg3=(it_dbg[hd] if DEBUG_SCRATCH else None))
            if DEBUG_SCRATCH:
                p.wait("gpsimd", s_dve, c_newt)
                ST(lambda e, hd_=hd: e.dma_start(r2_dbg[hd_], rbuf[:, 0:32]))
            p.wait("tensor", s_dve, c_newt)
            c_tr = TE(lambda e: e.transpose(PS[0:32, 3584:3712], rbuf[:, 0:32],
                                            identf[:]), inc=True)
            c_tr_hist[hd] = c_tr
            p.wait("vector", s_te, c_tr)
            c_trc = DVE(lambda e: e.tensor_copy(rT[:], PS[0:32, 3584:3712]), inc=True)
            p.wait("gpsimd", s_dve, c_trc)
            c_rst = ST(lambda e, hd_=hd: e.dma_start(
                r_scr[hd_].rearrange("(a b) -> a b", a=32), rT[:]))
            c_rst_hist[hd] = c_rst
            p.wait("vector", s_dve, dve_stage[g_tau - 1])
            for tau in range(NTAU):
                pr = tau % 2
                gf = g_fin + tau
                if gf >= 2:
                    p.wait("sync", s_dve, dve_fin[gf - 2])
                p.wait("sync", s_st, c_rst)
                crr = LD(lambda e, hd_=hd, t=tau, b=pr: e.dma_start(
                    rbc[:, b * 512:(b + 1) * 512],
                    r_scr[hd_, t * 512:(t + 1) * 512].partition_broadcast(128)))
                p.wait("vector", s_ld, crr)
                if gf >= 2:
                    p.wait("vector", s_st, st_fin[gf - 2])
                dve_fin[gf] = DVE(lambda e, t=tau, b=pr: e.tensor_tensor(
                    qhat[:, b * 512:(b + 1) * 512], stg[:, t * 512:(t + 1) * 512],
                    rbc[:, b * 512:(b + 1) * 512], ALU.mult), inc=True)
                p.wait("gpsimd", s_dve, dve_fin[gf])
                ST(lambda e, h_=h, t=tau, b=pr, sc=scr: e.dma_start(
                    sc[h_][:, t * 512:(t + 1) * 512], qhat[:, b * 512:(b + 1) * 512]))
                st_fin[gf] = p.count(s_st)
            g_fin += NTAU
            te_hd_end[hd] = p.count(s_te)
        phase_barrier()

        tanh_hist = {}
        exp_hist = {}
        pv_hist = {}
        mask_hist = {}
        dve_on = {}
        st_on = {}
        st_rec = {}
        ld_rec = {}
        rec_hist = {}
        te_head_done = {}
        ld_head = {}
        gp_ = 0
        gJ = 0
        for h in range(H):
            hp = h % 2
            if h >= 2:
                p.wait("sync", s_te, te_head_done[h - 2])
            LD(lambda e, h_=h, b=hp: e.dma_start(
                A[:, b * 4096:(b + 1) * 4096], q_scr[h_]))
            LD(lambda e, h_=h, b=hp: e.dma_start(
                A[:, 8192 + b * 4096:8192 + (b + 1) * 4096], k_scr[h_]))
            for i in range(NST):
                LD(lambda e, h_=h, b=hp, ii=i: e.dma_start(
                    A[:, 16384 + b * 4096 + ii * 128:16384 + b * 4096 + (ii + 1) * 128],
                    v_scr[h_, ii]))
            ld_head[h] = p.count(s_ld)
            qb = A[:, hp * 4096:(hp + 1) * 4096]
            kb = A[:, 8192 + hp * 4096:8192 + (hp + 1) * 4096]
            vb = A[:, 16384 + hp * 4096:16384 + (hp + 1) * 4096]
            p.wait("tensor", s_ld, ld_head[h])
            for J in range(NTAU):
                npair = 2 * J + 2
                s_hist = {}

                def emit_S(m, J_=J):
                    pr = m % 2
                    g = gp_ + m
                    if g >= 2:
                        p.wait("tensor", s_act, tanh_hist[g - 2])
                    for ii, b in ((2 * m, 2 * pr), (2 * m + 1, 2 * pr + 1)):
                        TE(lambda e, i_=ii, b_=b, Jx=J_, q_=qb, k_=kb: e.matmul(
                            bank(b_), k_[:, i_ * 128:(i_ + 1) * 128],
                            q_[:, Jx * 512:(Jx + 1) * 512], start=True, stop=True),
                           inc=(b % 2 == 1))
                    s_hist[g] = p.count(s_te)

                emit_S(0)
                for m in range(npair):
                    pr = m % 2
                    g = gp_ + m
                    if m + 1 < npair:
                        emit_S(m + 1)
                    p.wait("scalar", s_te, s_hist[g])
                    if g >= 2:
                        p.wait("scalar", s_te, pv_hist[g - 2])
                    tanh_hist[g] = ACT(lambda e, b=pr: e.activation(
                        out=tS[:, b * 1024:(b + 1) * 1024],
                        in_=PS[:, b * 1024:(b + 1) * 1024], func=AF.Tanh, scale=SCL))
                    p.wait("scalar", s_act, tanh_hist[g])
                    exp_hist[g] = ACT(lambda e, b=pr: e.activation(
                        out=tP[:, b * 1024:(b + 1) * 1024],
                        in_=tS[:, b * 1024:(b + 1) * 1024], func=AF.Exp, scale=SOFTCAP))
                    diag = (m >= 2 * J)
                    use = tP
                    if diag:
                        d0 = 2 * m - 4 * J
                        p.wait("vector", s_act, exp_hist[g])
                        if g >= 2 and (g - 2) in mask_hist:
                            p.wait("vector", s_te, pv_hist[g - 2])
                        DVE(lambda e, b=pr, d=d0: e.tensor_tensor(
                            tPm[:, b * 1024:b * 1024 + 512],
                            tP[:, b * 1024:b * 1024 + 512],
                            maskb[:, 384 - 128 * d:896 - 128 * d], ALU.mult))
                        mask_hist[g] = DVE(lambda e, b=pr, d=d0 + 1: e.tensor_tensor(
                            tPm[:, b * 1024 + 512:b * 1024 + 1024],
                            tP[:, b * 1024 + 512:b * 1024 + 1024],
                            maskb[:, 384 - 128 * d:896 - 128 * d], ALU.mult), inc=True)
                        use = tPm
                    p.wait("tensor", s_act, exp_hist[g])
                    if diag:
                        p.wait("tensor", s_dve, mask_hist[g])
                    if m == 0:
                        if gJ >= 2:
                            p.wait("tensor", s_dve, dve_on[gJ - 2])
                            p.wait("tensor", s_dve, rec_hist[gJ - 2])
                    last = (m == npair - 1)
                    ob = 4 + (gJ % 2)
                    sb = 6 + (gJ % 2)
                    for half, ii in ((0, 2 * m), (1, 2 * m + 1)):
                        TE(lambda e, i_=ii, b=pr, hf=half, v_=vb, u_=use, m_=m,
                           ob_=ob: e.matmul(
                            bank(ob_), v_[:, i_ * 128:(i_ + 1) * 128],
                            u_[:, b * 1024 + hf * 512:b * 1024 + (hf + 1) * 512],
                            start=(m_ == 0 and hf == 0), stop=False))
                        TE(lambda e, i_=ii, b=pr, hf=half, u_=use, m_=m,
                           sb_=sb, lt=(last and half == 1): e.matmul(
                            bank(sb_, rows=1), onescol[:],
                            u_[:, b * 1024 + hf * 512:b * 1024 + (hf + 1) * 512],
                            start=(m_ == 0 and hf == 0), stop=lt),
                           inc=(half == 1))
                    pv_hist[g] = p.count(s_te)
                gp_ += npair
                c_attn = pv_hist[gp_ - 1]
                p.wait("vector", s_te, c_attn)
                if gJ >= 1:
                    p.wait("vector", s_st, st_rec[gJ - 1])
                rec_hist[gJ] = DVE(lambda e, sb=6 + (gJ % 2): e.reciprocal(
                    recrow[:], PS[0:1, sb * 512:sb * 512 + 512]), inc=True)
                p.wait("gpsimd", s_dve, rec_hist[gJ])
                if gJ >= 2:
                    p.wait("gpsimd", s_ld, ld_rec[gJ - 2])
                ST(lambda e, b=gJ % 2: e.dma_start(rec_scr[b].unsqueeze(0),
                                                   recrow[:]))
                st_rec[gJ] = p.count(s_st)
                p.wait("sync", s_st, st_rec[gJ])
                if gJ >= 1:
                    p.wait("sync", s_dve, dve_on[gJ - 1])
                ld_rec[gJ] = LD(lambda e, b=gJ % 2: e.dma_start(
                    bcsb[:], rec_scr[b].partition_broadcast(128)))
                p.wait("vector", s_ld, ld_rec[gJ])
                if gJ >= 2:
                    p.wait("vector", s_st, st_on[gJ - 2])
                dve_on[gJ] = DVE(lambda e, b=gJ % 2, ob=4 + (gJ % 2): e.tensor_tensor(
                    obf[:, b * 512:(b + 1) * 512], bank(ob), bcsb[:], ALU.mult),
                    inc=True)
                p.wait("gpsimd", s_dve, dve_on[gJ])
                ST(lambda e, h_=h, J_=J, b=gJ % 2: e.dma_start(
                    o_scr[h_][:, J_ * 512:(J_ + 1) * 512],
                    obf[:, b * 512:(b + 1) * 512]))
                st_on[gJ] = p.count(s_st)
                gJ += 1
            te_head_done[h] = p.count(s_te)
        phase_barrier()

        zview = A[:, 49152:65536].bitcast(F32)
        ln_state = {"i": 0, "ld": {}, "dve": {}, "st": {}, "stat": {}, "zlast": {},
                    "zall": {}, "mmlast": {}}

        def ln_block(tau, mm_emit, z_emit, store_emit):
            te_sums = {}
            dve_z = {}
            sq_hist = {}
            mm_hist = {}

            def emit_mms(c):
                if c >= 2:
                    p.wait("tensor", s_dve, dve_z[c - 2])
                mm_emit(c, c % 2)
                mm_hist[c] = p.count(s_te)

            emit_mms(0)
            for c in range(NK):
                b = c % 2
                if c + 1 < NK:
                    emit_mms(c + 1)
                p.wait("vector", s_te, mm_hist[c])
                z_emit(c, b)
                c_zf = p.count(s_dve)
                p.wait("vector", s_dve, c_zf)
                if c >= 2:
                    p.wait("vector", s_te, te_sums[c - 2])
                dve_z[c] = DVE(lambda e, cc=c, b_=b: e.tensor_copy(
                    zbf[:, b_ * 512:(b_ + 1) * 512],
                    zview[:, cc * 512:(cc + 1) * 512]), inc=True)
                p.wait("scalar", s_dve, dve_z[c])
                if c >= 2:
                    p.wait("scalar", s_te, te_sums[c - 2])
                sq_hist[c] = ACT(lambda e, cc=c, b_=b: e.activation(
                    out=sqz[:, b_ * 512:(b_ + 1) * 512],
                    in_=zview[:, cc * 512:(cc + 1) * 512], func=AF.Square))
                p.wait("tensor", s_act, sq_hist[c])
                if c == 0 and ln_state["i"] >= 1:
                    p.wait("tensor", s_dve, ln_state["stat"][ln_state["i"] - 1])
                TE(lambda e, cc=c, b_=b: e.matmul(
                    bank(2, rows=1), onescol[:], zbf[:, b_ * 512:(b_ + 1) * 512],
                    start=(cc == 0), stop=(cc == NK - 1)))
                TE(lambda e, cc=c, b_=b: e.matmul(
                    bank(3, rows=1), onescol[:], sqz[:, b_ * 512:(b_ + 1) * 512],
                    start=(cc == 0), stop=(cc == NK - 1)), inc=True)
                te_sums[c] = p.count(s_te)
            p.wait("vector", s_te, te_sums[NK - 1])
            if ln_state["i"] >= 1:
                p.wait("vector", s_st, ln_state["st"][ln_state["i"] - 1])
            DVE(lambda e: e.tensor_scalar(out=srow[0:1, 0:512], in0=PS[0:1, 1024:1536],
                                          scalar1=1.0 / D, scalar2=None, op0=ALU.mult))
            c_stat = DVE(lambda e: e.tensor_scalar(out=srow[0:1, 512:1024],
                                                   in0=PS[0:1, 1536:2048],
                                                   scalar1=1.0 / D, scalar2=None,
                                                   op0=ALU.mult), inc=True)
            p.wait("vector", s_dve, c_stat)
            ln_state["stat"][ln_state["i"]] = c_stat
            DVE(lambda e: e.tensor_tensor(srow[0:1, 1536:2048], srow[0:1, 0:512],
                                          srow[0:1, 0:512], ALU.mult))
            DVE(lambda e: e.tensor_tensor(srow[0:1, 512:1024], srow[0:1, 512:1024],
                                          srow[0:1, 1536:2048], ALU.subtract))
            newton_row(srow[0:1, 512:1024], recrow[0:1, :], LN_EPS)
            DVE(lambda e: e.tensor_tensor(srow[0:1, 0:512], srow[0:1, 0:512],
                                          recrow[0:1, :], ALU.mult))
            c_st2 = DVE(lambda e: e.tensor_scalar(out=srow[0:1, 0:512],
                                                  in0=srow[0:1, 0:512], scalar1=-1.0,
                                                  scalar2=None, op0=ALU.mult), inc=True)
            p.wait("gpsimd", s_dve, c_st2)
            lslot = ln_state["i"] % 2
            if ln_state["i"] >= 2:
                p.wait("gpsimd", s_ld, ln_state["ld"][ln_state["i"] - 2])
            ST(lambda e, b=lslot: e.dma_start(ln_scr[b, 0:512].unsqueeze(0),
                                              recrow[:]))
            c_lst = ST(lambda e, b=lslot: e.dma_start(ln_scr[b, 512:1024].unsqueeze(0),
                                                      srow[0:1, 0:512]))
            p.wait("sync", s_st, c_lst)
            if ln_state["i"] >= 1:
                p.wait("sync", s_dve, ln_state["dve"][ln_state["i"] - 1])
            LD(lambda e, b=lslot: e.dma_start(
                abc[:], ln_scr[b, 0:512].partition_broadcast(128)))
            c_lld = LD(lambda e, b=lslot: e.dma_start(
                bbc[:], ln_scr[b, 512:1024].partition_broadcast(128)))
            ln_state["ld"][ln_state["i"]] = c_lld
            ln_state["st"][ln_state["i"]] = c_lst
            p.wait("vector", s_ld, c_lld)
            for c in range(NK):
                b = c % 2
                if c >= 2:
                    p.wait("vector", s_st, p.count(s_st) - 16)
                DVE(lambda e, cc=c, b_=b: e.tensor_tensor(
                    y1t[:, b_ * 512:(b_ + 1) * 512],
                    zview[:, cc * 512:(cc + 1) * 512], abc[:], ALU.mult))
                c_y = DVE(lambda e, b_=b: e.tensor_tensor(
                    y1t[:, b_ * 512:(b_ + 1) * 512],
                    y1t[:, b_ * 512:(b_ + 1) * 512], bbc[:], ALU.add), inc=True)
                p.wait("gpsimd", s_dve, c_y)
                store_emit(c, b)
            ln_state["dve"][ln_state["i"]] = p.count(s_dve)
            ln_state["zlast"][ln_state["i"]] = dve_z[NK - 1]
            ln_state["zall"][ln_state["i"]] = dict(dve_z)
            ln_state["mmlast"][ln_state["i"]] = mm_hist[NK - 1]
            ln_state["i"] += 1

        dve_wo = {}
        for k in range(NK):
            par = k % 2
            if k >= 2:
                p.wait("sync", s_dve, dve_wo[k - 2])
            c = LD(lambda e, kk=k, pr=par: e.dma_start(
                FS[:, pr * 2048:(pr + 1) * 2048], wot[kk * 128:(kk + 1) * 128, :]))
            p.wait("vector", s_ld, c)
            dve_wo[k] = DVE(lambda e, kk=k, pr=par: e.tensor_copy(
                A[:, kk * 2048:(kk + 1) * 2048], FS[:, pr * 2048:(pr + 1) * 2048]),
                inc=True)
        p.wait("tensor", s_dve, dve_wo[NK - 1])
        for tau in range(NTAU):
            hp = tau % 2
            if tau >= 2:
                p.wait("sync", s_te, ln_state["mmlast"][ln_state["i"] - 1])
            for hc in range(H):
                LD(lambda e, h_=hc, t=tau, b=hp: e.dma_start(
                    A[:, 32768 + b * 8192 + h_ * 512:32768 + b * 8192 + (h_ + 1) * 512],
                    o_scr[h_][:, t * 512:(t + 1) * 512]))
            for k in range(NK):
                if tau >= 1:
                    p.wait("sync", s_dve, ln_state["zall"][ln_state["i"] - 1][k])
                LD(lambda e, kk=k, t=tau: e.dma_start(
                    FS[:, kk * 512:(kk + 1) * 512],
                    xt[kk * 128:(kk + 1) * 128, t * 512:(t + 1) * 512]))
            c_lds = p.count(s_ld)
            p.wait("tensor", s_ld, c_lds)
            p.wait("vector", s_ld, c_lds)

            def pc_mm(c, b, hp_=hp):
                for hc2 in range(NK):
                    TE(lambda e, cc=c, b_=b, h_=hc2, bb=hp_: e.matmul(
                        bank(b_), A[:, h_ * 2048 + cc * 128:h_ * 2048 + (cc + 1) * 128],
                        A[:, 32768 + bb * 8192 + h_ * 512:
                          32768 + bb * 8192 + (h_ + 1) * 512],
                        start=(h_ == 0), stop=(h_ == NK - 1)), inc=(hc2 == NK - 1))

            def pc_z(c, b):
                DVE(lambda e, cc=c, b_=b: e.tensor_tensor(
                    zview[:, cc * 512:(cc + 1) * 512], bank(b_),
                    FS[:, cc * 512:(cc + 1) * 512], ALU.add), inc=True)

            def pc_store(c, b, t=tau):
                ST(lambda e, cc=c, b_=b, t_=t: e.dma_start(
                    y1_scr[cc][:, t_ * 512:(t_ + 1) * 512],
                    y1t[:, b_ * 512:(b_ + 1) * 512]))

            ln_block(tau, pc_mm, pc_z, pc_store)
        phase_barrier()

        for k in range(NK):
            LD(lambda e, kk=k: e.dma_start(A[:, kk * 2048:(kk + 1) * 2048], wt_scr[kk]))
        c_wt = p.count(s_ld)
        p.wait("tensor", s_ld, c_wt)
        y1bf = A[:, 32768:40960]
        for tau in range(NTAU):
            if tau >= 1:
                p.wait("sync", s_te, ln_state["mmlast"][ln_state["i"] - 1])
            for k in range(NK):
                if tau >= 1:
                    p.wait("sync", s_dve, ln_state["zall"][ln_state["i"] - 1][k])
                LD(lambda e, kk=k, t=tau: e.dma_start(
                    FS[:, kk * 512:(kk + 1) * 512],
                    y1_scr[kk][:, t * 512:(t + 1) * 512]))
            c_lds = p.count(s_ld)
            p.wait("vector", s_ld, c_lds)
            for k in range(NK):
                DVE(lambda e, kk=k: e.tensor_copy(
                    y1bf[:, kk * 512:(kk + 1) * 512], FS[:, kk * 512:(kk + 1) * 512]),
                    inc=(kk := k) == NK - 1)
            c_cast = p.count(s_dve)
            p.wait("tensor", s_dve, c_cast)

            def pd_mm(c, b):
                for k2 in range(NK):
                    TE(lambda e, cc=c, b_=b, kk=k2: e.matmul(
                        bank(b_), A[:, kk * 2048 + cc * 128:kk * 2048 + (cc + 1) * 128],
                        y1bf[:, kk * 512:(kk + 1) * 512],
                        start=(kk == 0), stop=(kk == NK - 1)), inc=(k2 == NK - 1))

            def pd_z(c, b):
                DVE(lambda e, cc=c, b_=b: e.scalar_tensor_tensor(
                    out=zview[:, cc * 512:(cc + 1) * 512], in0=bank(b_),
                    scalar=cf[:, 34 + cc:35 + cc],
                    in1=FS[:, cc * 512:(cc + 1) * 512], op0=ALU.add, op1=ALU.add),
                    inc=True)

            def pd_store(c, b, t=tau):
                ST(lambda e, cc=c, b_=b, t_=t: e.dma_start(
                    yt[cc * 128:(cc + 1) * 128, t_ * 512:(t_ + 1) * 512],
                    y1t[:, b_ * 512:(b_ + 1) * 512]))

            ln_block(tau, pd_mm, pd_z, pd_store)

        p.wait("gpsimd", s_st, p.count(s_st))
        p.wait("gpsimd", s_ld, p.count(s_ld))

        block = es.enter_context(nc.Block())

        @block.sync
        def _(eng):
            for f in p.ops["sync"]:
                f(eng)

        @block.tensor
        def _(eng):
            for f in p.ops["tensor"]:
                f(eng)

        @block.scalar
        def _(eng):
            for f in p.ops["scalar"]:
                f(eng)

        @block.vector
        def _(eng):
            for f in p.ops["vector"]:
                f(eng)

        @block.gpsimd
        def _(eng):
            for f in p.ops["gpsimd"]:
                f(eng)

    return nc


def _perm():
    pr = np.arange(DH).reshape(DH // 2, 2).T.reshape(-1)
    return np.concatenate([h * DH + pr for h in range(H)])


def kernel(L0_attn_x, L0_attn_wq, L0_attn_wk, L0_attn_wv, L0_attn_wo,
           L0_attn_q_norm_w, L0_attn_k_norm_w, L0_attn_rope_cos, L0_attn_rope_sin,
           L0_ffn_weight, L0_ffn_bias):
    x = np.asarray(L0_attn_x, dtype=np.float32)
    wq = np.asarray(L0_attn_wq, dtype=np.float32)
    wk = np.asarray(L0_attn_wk, dtype=np.float32)
    wv = np.asarray(L0_attn_wv, dtype=np.float32)
    wo = np.asarray(L0_attn_wo, dtype=np.float32)
    qnw = np.asarray(L0_attn_q_norm_w, dtype=np.float32)
    knw = np.asarray(L0_attn_k_norm_w, dtype=np.float32)
    cos = np.asarray(L0_attn_rope_cos, dtype=np.float32)
    sin = np.asarray(L0_attn_rope_sin, dtype=np.float32)
    wf = np.asarray(L0_ffn_weight, dtype=np.float32)
    bias = np.asarray(L0_ffn_bias, dtype=np.float32)

    perm = _perm()
    prd = perm[:DH]
    wqt = np.ascontiguousarray(wq[perm, :].T)
    wkt = np.ascontiguousarray(wk[perm, :].T)
    wvt = np.ascontiguousarray(wv.T)
    wot = np.ascontiguousarray(wo.T)
    wft = np.ascontiguousarray(wf.T)

    cfin = np.zeros((128, 50), dtype=np.float32)
    cfin[:, 0] = qnw[prd]
    cfin[:, 1] = knw[prd]
    cosT = np.ascontiguousarray(cos[:H].T)
    sinT = np.ascontiguousarray(sin[:H].T)
    cfin[0:64, 2:18] = cosT
    cfin[64:128, 2:18] = cosT
    cfin[0:64, 18:34] = sinT
    cfin[64:128, 18:34] = sinT
    cfin[:, 34:50] = bias.reshape(NK, 128).T

    jj = np.arange(896)[None, :]
    pp = np.arange(128)[:, None]
    maskin = (pp <= jj - 384).astype(ml_dtypes.bfloat16)
    identin = np.eye(128, dtype=np.float32)
    maskwin = np.zeros((128, 128), dtype=np.float32)
    maskwin[np.arange(64), np.arange(64) + 64] = 1.0
    maskwin[np.arange(64) + 64, np.arange(64)] = -1.0
    onesfin = np.ones((1, 128), dtype=np.float32)
    onesbin = np.ones((128, 1), dtype=ml_dtypes.bfloat16)

    nc = build_nc()
    in_maps = []
    for b in range(8):
        in_maps.append({
            "xt": np.ascontiguousarray(x[b].T),
            "wqt": wqt, "wkt": wkt, "wvt": wvt, "wot": wot, "wft": wft,
            "cfin": cfin, "maskin": maskin, "identin": identin,
            "maskwin": maskwin, "onesfin": onesfin, "onesbin": onesbin,
        })
    res = run_bass_kernel_spmd(nc, in_maps, core_ids=list(range(8)))
    global _last_results
    if DEBUG_SCRATCH:
        _last_results = res.results
    out = np.stack([np.asarray(r["yt"]).T for r in res.results], axis=0)
    return np.ascontiguousarray(out.astype(np.float32))


# version 06
# speedup vs baseline: 1.0289x; 1.0289x over previous
import numpy as np
import ml_dtypes
import concourse.bass as bass
import concourse.mybir as mybir
from concourse.bass_utils import run_bass_kernel_spmd

F32 = mybir.dt.float32
BF16 = mybir.dt.bfloat16
I32 = mybir.dt.int32
AF = mybir.ActivationFunctionType
ALU = mybir.AluOpType

T = 4096
D = 2048
H = 16
DH = 128
NK = D // 128
NTAU = T // 512
NST = T // 128
RMS_EPS = 1e-6
LN_EPS = 1e-5
SOFTCAP = 50.0
SCL = (DH ** -0.5) / SOFTCAP
MAGIC = 0x5F3759DF

ENGS = ("sync", "tensor", "scalar", "vector", "gpsimd")
DEBUG_SCRATCH = False
_last_results = None


class Prog:
    def __init__(self):
        self.ops = {e: [] for e in ENGS}
        self.cnt = {}

    def wait(self, eng, sem, val):
        if val and val > 0:
            self.ops[eng].append(lambda e, s=sem, v=val: e.wait_ge(s, v))

    def do(self, eng, fn, inc=None, n=1):
        if inc is not None:
            self.cnt[inc] = self.cnt.get(inc, 0) + n
            tot = self.cnt[inc]
            self.ops[eng].append(lambda e, f=fn, s=inc, k=n: f(e).then_inc(s, k))
            return tot
        self.ops[eng].append(fn)
        return None

    def count(self, sem):
        return self.cnt.get(sem, 0)


def build_nc():
    nc = bass.Bass("TRN2", target_bir_lowering=False)

    xt = nc.dram_tensor("xt", [D, T], F32, kind="ExternalInput").ap()
    wqt = nc.dram_tensor("wqt", [D, D], F32, kind="ExternalInput").ap()
    wkt = nc.dram_tensor("wkt", [D, D], F32, kind="ExternalInput").ap()
    wvt = nc.dram_tensor("wvt", [D, D], F32, kind="ExternalInput").ap()
    wot = nc.dram_tensor("wot", [D, D], F32, kind="ExternalInput").ap()
    wft = nc.dram_tensor("wft", [D, D], F32, kind="ExternalInput").ap()
    cfin = nc.dram_tensor("cfin", [128, 50], F32, kind="ExternalInput").ap()
    maskin = nc.dram_tensor("maskin", [128, 896], BF16, kind="ExternalInput").ap()
    identin = nc.dram_tensor("identin", [128, 128], F32, kind="ExternalInput").ap()
    maskwin = nc.dram_tensor("maskwin", [128, 128], F32, kind="ExternalInput").ap()
    onesfin = nc.dram_tensor("onesfin", [1, 128], F32, kind="ExternalInput").ap()
    onesbin = nc.dram_tensor("onesbin", [128, 1], BF16, kind="ExternalInput").ap()
    yt = nc.dram_tensor("yt", [D, T], F32, kind="ExternalOutput").ap()

    kw = {"kind": "ExternalOutput"} if DEBUG_SCRATCH else {}
    q_scr = nc.dram_tensor("q_scr", [H, 128, T], BF16, **kw).ap()
    k_scr = nc.dram_tensor("k_scr", [H, 128, T], BF16, **kw).ap()
    v_scr = nc.dram_tensor("v_scr", [H, 128, T], BF16, **kw).ap()
    o_scr = nc.dram_tensor("o_scr", [NTAU, 128, H * 512], BF16, **kw).ap()
    y1_scr = nc.dram_tensor("y1_scr", [NK, 128, T], F32, **kw).ap()
    wt_scr = nc.dram_tensor("wt_scr", [NK, 128, D], BF16, **kw).ap()
    r_scr = nc.dram_tensor("r_scr", [2 * H, T], F32, **kw).ap()
    at_scr = nc.dram_tensor("at_scr", [128, 64], F32).ap()
    thr_scr = nc.dram_tensor("thr_scr", [2048], F32, **kw).ap()
    rec_scr = nc.dram_tensor("rec_scr", [2, 512], F32).ap()
    ln_scr = nc.dram_tensor("ln_scr", [2, 1024], F32).ap()
    u_dbg = nc.dram_tensor("u_dbg", [2 * H, 128, 32], F32, **kw).ap()
    r2_dbg = nc.dram_tensor("r2_dbg", [2 * H, 128, 32], F32, **kw).ap()
    sd_dbg = nc.dram_tensor("sd_dbg", [2 * H, 128, 32], F32, **kw).ap()
    it_dbg = nc.dram_tensor("it_dbg", [2 * H, 128, 32], F32, **kw).ap()
    z_dbg = nc.dram_tensor("z_dbg", [128, 8192], F32, **kw).ap()
    osb_dbg = nc.dram_tensor("osb_dbg", [128, 8192], F32, **kw).ap()
    xf_dbg = nc.dram_tensor("xf_dbg", [128, 8192], F32, **kw).ap()
    rall_dbg = nc.dram_tensor("rall_dbg", [128, 4096], BF16, **kw).ap()

    p = Prog()
    from contextlib import ExitStack
    es = ExitStack()
    with es:
        A = es.enter_context(nc.sbuf_tensor("A", [128, 65536], BF16))
        FS = es.enter_context(nc.sbuf_tensor("FS", [128, 8192], F32))
        Rall = es.enter_context(nc.sbuf_tensor("Rall", [128, 32 * 128], BF16))
        cf = es.enter_context(nc.sbuf_tensor("cf", [128, 50], F32))
        maskb = es.enter_context(nc.sbuf_tensor("maskb", [128, 896], BF16))
        identf = es.enter_context(nc.sbuf_tensor("identf", [128, 128], F32))
        maskw = es.enter_context(nc.sbuf_tensor("maskw", [128, 128], F32))
        onesrow = es.enter_context(nc.sbuf_tensor("onesrow", [1, 128], F32))
        onescol = es.enter_context(nc.sbuf_tensor("onescol", [128, 1], BF16))
        onescolf = es.enter_context(nc.sbuf_tensor("onescolf", [128, 1], F32))
        Atile = es.enter_context(nc.sbuf_tensor("Atile", [128, 64], F32))
        rtmp1 = es.enter_context(nc.sbuf_tensor("rtmp1", [128, 128], BF16))
        rtmp2 = es.enter_context(nc.sbuf_tensor("rtmp2", [128, 128], BF16))
        rbuf = es.enter_context(nc.sbuf_tensor("rbuf", [128, 32], F32))
        ntb = es.enter_context(nc.sbuf_tensor("ntb", [128, 32], F32))
        hhb = es.enter_context(nc.sbuf_tensor("hhb", [128, 32], F32))
        rT = es.enter_context(nc.sbuf_tensor("rT", [32, 128], F32))
        rbc = es.enter_context(nc.sbuf_tensor("rbc", [128, 1024], F32))
        abc = es.enter_context(nc.sbuf_tensor("abc", [128, 512], F32))
        bbc = es.enter_context(nc.sbuf_tensor("bbc", [128, 512], F32))
        qraw = es.enter_context(nc.sbuf_tensor("qraw", [128, 1024], BF16))
        sqbf = es.enter_context(nc.sbuf_tensor("sqbf", [128, 1024], BF16))
        qhat = es.enter_context(nc.sbuf_tensor("qhat", [128, 1024], BF16))
        recrow = es.enter_context(nc.sbuf_tensor("recrow", [1, 512], F32))
        bcsb = es.enter_context(nc.sbuf_tensor("bcsb", [128, 512], F32))
        zbf = es.enter_context(nc.sbuf_tensor("zbf", [128, 1024], BF16))
        sqz = es.enter_context(nc.sbuf_tensor("sqz", [128, 1024], BF16))
        y1t = es.enter_context(nc.sbuf_tensor("y1t", [128, 1024], F32))
        srow = es.enter_context(nc.sbuf_tensor("srow", [1, 2048], F32))
        PS = es.enter_context(nc.psum_tensor("PS", [128, 4096], F32))

        s_ld = es.enter_context(nc.semaphore("s_ld"))
        s_st = es.enter_context(nc.semaphore("s_st"))
        s_te = es.enter_context(nc.semaphore("s_te"))
        s_act = es.enter_context(nc.semaphore("s_act"))
        s_dve = es.enter_context(nc.semaphore("s_dve"))

        def bank(i, rows=128, width=512):
            return PS[0:rows, 512 * i: 512 * i + width]

        tS = FS[:, 0:2048]
        tP = FS[:, 2048:3072].bitcast(BF16)
        tPm = FS[:, 3072:4096].bitcast(BF16)
        obf = FS[:, 4096:4608].bitcast(BF16)
        gtv = A[:, 40960:43008]
        ltv = A[:, 43008:45056]
        absf = A[:, 45056:49152].bitcast(F32)
        vevac = FS[:, 5120:6144].bitcast(BF16)
        throw_ = FS[0:1, 6144:8192]

        LD = lambda fn: p.do("sync", fn, inc=s_ld, n=16)
        ST = lambda fn: p.do("gpsimd", fn, inc=s_st, n=16)
        TE = lambda fn, inc=False: p.do("tensor", fn, inc=s_te if inc else None)
        ACT = lambda fn, inc=True: p.do("scalar", fn, inc=s_act if inc else None)
        DVE = lambda fn, inc=False: p.do("vector", fn, inc=s_dve if inc else None)

        def newton_cols(src_ap, scale_mean, eps, dbg=None, dbg2=None, dbg3=None):
            def F(fn):
                c = DVE(fn, inc=True)
                p.wait("vector", s_dve, c)
                return c
            c_u = F(lambda e, s=src_ap, sm=scale_mean, ep=eps: e.tensor_scalar(
                out=ntb[:, 0:32], in0=s, scalar1=sm, scalar2=ep,
                op0=ALU.mult, op1=ALU.add))
            if dbg is not None:
                p.wait("gpsimd", s_dve, c_u)
                ST(lambda e, d=dbg: e.dma_start(d, ntb[:, 0:32]))
            F(lambda e: e.tensor_scalar(
                out=rbuf[:, 0:32].bitcast(I32), in0=ntb[:, 0:32].bitcast(I32),
                scalar1=1, scalar2=None, op0=ALU.arith_shift_right))
            F(lambda e: e.tensor_scalar(
                out=rbuf[:, 0:32].bitcast(I32), in0=rbuf[:, 0:32].bitcast(I32),
                scalar1=-1, scalar2=MAGIC, op0=ALU.mult, op1=ALU.add))
            for it_ in range(4):
                F(lambda e: e.tensor_tensor(hhb[:, 0:32], rbuf[:, 0:32],
                                            rbuf[:, 0:32], ALU.mult))
                F(lambda e: e.tensor_tensor(hhb[:, 0:32], hhb[:, 0:32],
                                            ntb[:, 0:32], ALU.mult))
                F(lambda e: e.tensor_scalar(out=hhb[:, 0:32], in0=hhb[:, 0:32],
                                            scalar1=-0.5, scalar2=1.5,
                                            op0=ALU.mult, op1=ALU.add))
                F(lambda e: e.tensor_tensor(rbuf[:, 0:32], rbuf[:, 0:32],
                                            hhb[:, 0:32], ALU.mult))
            return F(lambda e: e.tensor_copy(rbuf[:, 0:32], rbuf[:, 0:32]))

        def newton_row(src_ap, dst_ap, eps, w=512):
            u = srow[0:1, 512:512 + w]
            t = srow[0:1, 1024:1024 + w]
            def F(fn):
                c = DVE(fn, inc=True)
                p.wait("vector", s_dve, c)
                return c
            F(lambda e, s=src_ap, uu=u, ep=eps: e.tensor_scalar(
                out=uu, in0=s, scalar1=1.0, scalar2=ep, op0=ALU.mult, op1=ALU.add))
            F(lambda e, uu=u, y=dst_ap: e.tensor_scalar(
                out=y.bitcast(I32), in0=uu.bitcast(I32), scalar1=1, scalar2=None,
                op0=ALU.arith_shift_right))
            F(lambda e, y=dst_ap: e.tensor_scalar(
                out=y.bitcast(I32), in0=y.bitcast(I32), scalar1=-1, scalar2=MAGIC,
                op0=ALU.mult, op1=ALU.add))
            for _ in range(4):
                F(lambda e, y=dst_ap, tt=t: e.tensor_tensor(tt, y, y, ALU.mult))
                F(lambda e, uu=u, tt=t: e.tensor_tensor(tt, tt, uu, ALU.mult))
                F(lambda e, tt=t: e.tensor_scalar(out=tt, in0=tt, scalar1=-0.5,
                                                  scalar2=1.5, op0=ALU.mult,
                                                  op1=ALU.add))
                F(lambda e, y=dst_ap, tt=t: e.tensor_tensor(y, y, tt, ALU.mult))

        def phase_barrier():
            snap = dict(p.cnt)
            for eng in ENGS:
                for sem in (s_ld, s_st, s_te, s_act, s_dve):
                    p.wait(eng, sem, snap.get(sem, 0))

        LD(lambda e: e.dma_start(cf[:], cfin))
        LD(lambda e: e.dma_start(maskb[:], maskin))
        LD(lambda e: e.dma_start(identf[:], identin))
        LD(lambda e: e.dma_start(maskw[:], maskwin))
        LD(lambda e: e.dma_start(onesrow[:], onesfin))
        LD(lambda e: e.dma_start(onescol[:], onesbin))
        c_consts = LD(lambda e: e.dma_start(
            onescolf[:], onesfin.rearrange("o f -> f o")))

        p.wait("vector", s_ld, c_consts)
        DVE(lambda e: e.tensor_scalar(out=Atile[:, 0:16], in0=cf[:, 2:18],
                                      scalar1=cf[:, 0:1], scalar2=None, op0=ALU.mult))
        DVE(lambda e: e.tensor_scalar(out=Atile[:, 16:32], in0=cf[:, 18:34],
                                      scalar1=cf[:, 0:1], scalar2=None, op0=ALU.mult))
        DVE(lambda e: e.tensor_scalar(out=Atile[:, 32:48], in0=cf[:, 2:18],
                                      scalar1=cf[:, 1:2], scalar2=None, op0=ALU.mult))
        c_at = DVE(lambda e: e.tensor_scalar(out=Atile[:, 48:64], in0=cf[:, 18:34],
                                             scalar1=cf[:, 1:2], scalar2=None,
                                             op0=ALU.mult), inc=True)
        p.wait("gpsimd", s_dve, c_at)
        c_atst = ST(lambda e: e.dma_start(at_scr, Atile[:]))
        p.wait("sync", s_st, c_atst)
        c_atld = LD(lambda e: e.dma_start(Atile[:], at_scr))
        p.wait("vector", s_ld, c_atld)
        for hd in range(2 * H):
            base = 0 if hd < H else 32
            hc = hd % H
            DVE(lambda e, b=base, h_=hc: e.tensor_scalar(
                out=rtmp1[:], in0=identf[:], scalar1=Atile[:, b + h_:b + h_ + 1],
                scalar2=None, op0=ALU.mult))
            DVE(lambda e, b=base, h_=hc: e.tensor_scalar(
                out=rtmp2[:], in0=maskw[:], scalar1=Atile[:, b + 16 + h_:b + 17 + h_],
                scalar2=None, op0=ALU.mult))
            DVE(lambda e, hd_=hd: e.tensor_tensor(
                Rall[:, hd_ * 128:(hd_ + 1) * 128], rtmp1[:], rtmp2[:], ALU.add))

        if DEBUG_SCRATCH:
            c_rall = DVE(lambda e: e.tensor_copy(rtmp1[:, 0:1], rtmp1[:, 0:1]),
                         inc=True)
            p.wait("gpsimd", s_dve, c_rall)
            ST(lambda e: e.dma_start(rall_dbg, Rall[:]))
        te_abs = {}
        act_abs = {}
        res_hist = {}
        for k in range(NK):
            par = k % 2
            if k >= 2:
                p.wait("sync", s_act, act_abs[k - 2])
            c = LD(lambda e, kk=k, pr=par: e.dma_start(
                FS[:, pr * 2048:(pr + 1) * 2048], wft[kk * 128:(kk + 1) * 128, :]))
            p.wait("scalar", s_ld, c)
            if k >= 1:
                p.wait("scalar", s_te, te_abs[k - 1])
                p.wait("scalar", s_dve, res_hist[k - 1])
            act_abs[k] = ACT(lambda e, pr=par: e.activation(
                out=absf[:, 0:2048], in_=FS[:, pr * 2048:(pr + 1) * 2048],
                func=AF.Abs))
            p.wait("vector", s_act, act_abs[k])
            if k >= 1:
                p.wait("vector", s_te, te_abs[k - 1])
            c_hi = DVE(lambda e: e.tensor_copy(gtv[:, 0:2048], absf[:, 0:2048]),
                       inc=True)
            p.wait("vector", s_dve, c_hi)
            c_res = DVE(lambda e: e.tensor_tensor(ltv[:, 0:2048], absf[:, 0:2048],
                                                  gtv[:, 0:2048], ALU.subtract),
                        inc=True)
            res_hist[k] = c_res
            p.wait("tensor", s_dve, c_res)
            for g in range(4):
                TE(lambda e, gg=g, kk=k: e.matmul(
                    bank(gg, rows=1), onescol[:], gtv[:, gg * 512:(gg + 1) * 512],
                    start=(kk == 0), stop=(kk == NK - 1)), inc=False)
                TE(lambda e, gg=g, kk=k: e.matmul(
                    bank(4 + gg, rows=1), onescol[:], ltv[:, gg * 512:(gg + 1) * 512],
                    start=(kk == 0), stop=(kk == NK - 1)), inc=(g == 3))
            te_abs[k] = p.count(s_te)
        p.wait("vector", s_te, te_abs[NK - 1])
        thrbc = A[:, 57344:61440].bitcast(F32)
        nthrbc = A[:, 61440:65536].bitcast(F32)
        for g in range(4):
            DVE(lambda e, gg=g: e.tensor_scalar(
                out=srow[0:1, gg * 512:(gg + 1) * 512],
                in0=PS[0:1, gg * 512:(gg + 1) * 512],
                scalar1=0.7 / D, scalar2=None, op0=ALU.mult))
        c_s1 = DVE(lambda e: e.tensor_copy(srow[0:1, 0:1], srow[0:1, 0:1]), inc=True)
        p.wait("vector", s_dve, c_s1)
        for g in range(4):
            DVE(lambda e, gg=g: e.scalar_tensor_tensor(
                out=throw_[0:1, gg * 512:(gg + 1) * 512],
                in0=PS[0:1, 2048 + gg * 512:2048 + (gg + 1) * 512],
                scalar=0.7 / D, in1=srow[0:1, gg * 512:(gg + 1) * 512],
                op0=ALU.mult, op1=ALU.add))
        c_thr = DVE(lambda e: e.tensor_copy(srow[0:1, 0:1], srow[0:1, 0:1]), inc=True)
        p.wait("gpsimd", s_dve, c_thr)
        c_thrst = ST(lambda e: e.dma_start(thr_scr.rearrange("(o f) -> o f", o=1),
                                           throw_[0:1, :]))
        p.wait("sync", s_st, c_thrst)
        c_thrld = LD(lambda e: e.dma_start(thrbc[:, 0:2048],
                                           thr_scr.partition_broadcast(128)))
        p.wait("vector", s_ld, c_thrld)
        c_nthr = DVE(lambda e: e.tensor_scalar(out=nthrbc[:, 0:2048],
                                               in0=thrbc[:, 0:2048], scalar1=-1.0,
                                               scalar2=None, op0=ALU.mult), inc=True)
        wt_tile = FS[:, 4096:6144].bitcast(BF16)
        dve_q = {}
        for k in range(NK):
            par = k % 2
            if k >= 2:
                p.wait("sync", s_dve, dve_q[k - 2])
            c = LD(lambda e, kk=k, pr=par: e.dma_start(
                FS[:, pr * 2048:(pr + 1) * 2048], wft[kk * 128:(kk + 1) * 128, :]))
            p.wait("vector", s_ld, c)
            if k == 0:
                p.wait("vector", s_dve, c_nthr)
            DVE(lambda e, pr=par: e.tensor_tensor(
                gtv[:, 0:2048], FS[:, pr * 2048:(pr + 1) * 2048],
                thrbc[:, 0:2048], ALU.is_gt))
            DVE(lambda e, pr=par: e.tensor_tensor(
                ltv[:, 0:2048], FS[:, pr * 2048:(pr + 1) * 2048],
                nthrbc[:, 0:2048], ALU.is_lt))
            if k >= 2:
                p.wait("vector", s_st, p.count(s_st) - 16)
            dve_q[k] = DVE(lambda e, pr=par: e.tensor_tensor(
                wt_tile[:, pr * 2048:(pr + 1) * 2048], gtv[:, 0:2048],
                ltv[:, 0:2048], ALU.subtract), inc=True)
            p.wait("gpsimd", s_dve, dve_q[k])
            ST(lambda e, kk=k, pr=par: e.dma_start(
                wt_scr[kk], wt_tile[:, pr * 2048:(pr + 1) * 2048]))
        p.wait("sync", s_dve, dve_q[NK - 1])
        p.wait("sync", s_st, p.count(s_st))
        dve_x = {}
        for k in range(NK):
            par = k % 2
            if k >= 2:
                p.wait("sync", s_dve, dve_x[k - 2])
            c = LD(lambda e, kk=k, pr=par: e.dma_start(
                FS[:, pr * 4096:(pr + 1) * 4096], xt[kk * 128:(kk + 1) * 128, :]))
            p.wait("vector", s_ld, c)
            dve_x[k] = DVE(lambda e, kk=k, pr=par: e.tensor_copy(
                A[:, kk * 4096:(kk + 1) * 4096], FS[:, pr * 4096:(pr + 1) * 4096]),
                inc=True)
        phase_barrier()

        wvq = FS[:, 0:4096].bitcast(BF16)
        wvf = FS[:, 4096:5120]
        st_evac = {}
        dve_evac = {}
        gidx = 0
        dve_wv_all = {}
        for g4 in range(4):
            dve_wv = {}
            for k in range(NK):
                par = k % 2
                prev = dve_wv_all.get(g4 * NK + k - 2)
                if prev:
                    p.wait("sync", s_dve, prev)
                c = LD(lambda e, kk=k, g=g4, pr=par: e.dma_start(
                    wvf[:, pr * 512:(pr + 1) * 512],
                    wvt[kk * 128:(kk + 1) * 128, g * 512:(g + 1) * 512]))
                p.wait("vector", s_ld, c)
                if g4 > 0 and k == 0:
                    p.wait("vector", s_te, p.count(s_te))
                dve_wv[k] = DVE(lambda e, kk=k, pr=par: e.tensor_copy(
                    wvq[:, kk * 512:(kk + 1) * 512], wvf[:, pr * 512:(pr + 1) * 512]),
                    inc=True)
                dve_wv_all[g4 * NK + k] = dve_wv[k]
            p.wait("tensor", s_dve, dve_wv[NK - 1])
            for oct_ in range(4):
                if gidx >= 8:
                    p.wait("tensor", s_dve, dve_evac[gidx - 1])
                for k in range(NK):
                    for s8 in range(8):
                        st_ = oct_ * 8 + s8
                        TE(lambda e, kk=k, s=st_, b=s8: e.matmul(
                            bank(b), A[:, kk * 4096 + s * 128:kk * 4096 + (s + 1) * 128],
                            wvq[:, kk * 512:(kk + 1) * 512],
                            start=(kk == 0), stop=(kk == NK - 1)),
                           inc=(k == NK - 1 and s8 == 7))
                c_vmm = p.count(s_te)
                p.wait("vector", s_te, c_vmm)
                for s8 in range(8):
                    st_ = oct_ * 8 + s8
                    if gidx >= 2:
                        p.wait("vector", s_st, st_evac[gidx - 2])
                    dve_evac[gidx] = DVE(lambda e, b=s8: e.tensor_copy(
                        vevac[:, (b % 2) * 1024:(b % 2) * 1024 + 512], bank(b)), inc=True)
                    p.wait("gpsimd", s_dve, dve_evac[gidx])
                    for j in range(4):
                        ST(lambda e, g=g4, s=st_, b=s8, jj=j: e.dma_start(
                            v_scr[4 * g + jj, :, s * 128:(s + 1) * 128],
                            vevac[:, (b % 2) * 1024 + jj * 128:
                                  (b % 2) * 1024 + (jj + 1) * 128]))
                    st_evac[gidx] = p.count(s_st)
                    gidx += 1
        phase_barrier()

        wpf = FS[:, 0:4096]
        wpb = FS[:, 4096:6144].bitcast(BF16)
        stg = FS[:, 6144:8192].bitcast(BF16)
        act_hist = {}
        dve_stage = {}
        te_rope = {}
        te_bc = {}
        dve_fin = {}
        st_fin = {}
        te_hd_end = {}
        cwb_hist = {}
        c_tr_hist = {}
        c_rst_hist = {}
        g_tau = 0
        g_fin = 0
        for hd in range(2 * H):
            proj = 0 if hd < H else 1
            h = hd % H
            wsrc = wqt if proj == 0 else wkt
            scr = q_scr if proj == 0 else k_scr
            wslot = hd % 2
            if hd >= 2:
                p.wait("sync", s_dve, cwb_hist[hd - 2])
            cw = LD(lambda e, w=wsrc, h_=h, sl=wslot: e.dma_start(
                wpf[:, sl * 2048:(sl + 1) * 2048].rearrange("p (k c) -> p k c", c=128),
                w[:, h_ * 128:(h_ + 1) * 128].rearrange("(k p) c -> p k c", p=128)))
            p.wait("vector", s_ld, cw)
            if hd >= 2:
                p.wait("vector", s_te, te_hd_end[hd - 2])
            cwb = DVE(lambda e, sl=wslot: e.tensor_copy(
                wpb[:, sl * 2048:(sl + 1) * 2048], wpf[:, sl * 2048:(sl + 1) * 2048]),
                inc=True)
            cwb_hist[hd] = cwb
            p.wait("tensor", s_dve, cwb)
            te_proj = {}

            def emit_proj(tau):
                pr = tau % 2
                gi = g_tau + tau
                if gi >= 2:
                    p.wait("tensor", s_act, act_hist[gi - 2])
                for k in range(NK):
                    TE(lambda e, kk=k, sl=wslot, t=tau, b=pr: e.matmul(
                        bank(b), wpb[:, sl * 2048 + kk * 128:sl * 2048 + (kk + 1) * 128],
                        A[:, kk * 4096 + t * 512:kk * 4096 + (t + 1) * 512],
                        start=(kk == 0), stop=(kk == NK - 1)), inc=(k == NK - 1))
                te_proj[tau] = p.count(s_te)

            emit_proj(0)
            for tau in range(NTAU):
                pr = tau % 2
                gi = g_tau + tau
                if tau + 1 < NTAU:
                    emit_proj(tau + 1)
                p.wait("scalar", s_te, te_proj[tau])
                ACT(lambda e, b=pr: e.activation(
                    out=qraw[:, b * 512:(b + 1) * 512], in_=bank(b), func=AF.Copy),
                    inc=False)
                act_hist[gi] = ACT(lambda e, b=pr: e.activation(
                    out=sqbf[:, b * 512:(b + 1) * 512], in_=bank(b), func=AF.Square))
                p.wait("tensor", s_act, act_hist[gi])
                if gi >= 2:
                    p.wait("tensor", s_dve, dve_stage[gi - 2])
                TE(lambda e, hd_=hd, b=pr: e.matmul(
                    bank(2 + b), Rall[:, hd_ * 128:(hd_ + 1) * 128],
                    qraw[:, b * 512:(b + 1) * 512], start=True, stop=True))
                for i4 in range(4):
                    col = 4 * tau + i4
                    TE(lambda e, b=pr, cc=col: e.matmul(
                        PS[:, 2048 + cc:2048 + cc + 1],
                        sqbf[:, b * 512 + (cc % 4) * 128:b * 512 + ((cc % 4) + 1) * 128],
                        onescol[:], start=True, stop=True), inc=(i4 == 3))
                te_rope[gi] = p.count(s_te)
                p.wait("vector", s_te, te_rope[gi])
                dve_stage[gi] = DVE(lambda e, b=pr, t=tau: e.tensor_copy(
                    stg[:, t * 512:(t + 1) * 512], bank(2 + b)), inc=True)
            g_tau += NTAU
            p.wait("vector", s_te, te_rope[g_tau - 1])
            if hd >= 1:
                p.wait("vector", s_te, c_tr_hist[hd - 1])
                p.wait("vector", s_st, c_rst_hist[hd - 1])
            c_newt = newton_cols(PS[:, 2048:2080], 1.0 / DH, RMS_EPS,
                                 dbg=(u_dbg[hd] if DEBUG_SCRATCH else None),
                                 dbg2=(sd_dbg[hd] if DEBUG_SCRATCH else None),
                                 dbg3=(it_dbg[hd] if DEBUG_SCRATCH else None))
            if DEBUG_SCRATCH:
                p.wait("gpsimd", s_dve, c_newt)
                ST(lambda e, hd_=hd: e.dma_start(r2_dbg[hd_], rbuf[:, 0:32]))
            p.wait("tensor", s_dve, c_newt)
            c_tr = TE(lambda e: e.transpose(PS[0:32, 3584:3712], rbuf[:, 0:32],
                                            identf[:]), inc=True)
            c_tr_hist[hd] = c_tr
            p.wait("vector", s_te, c_tr)
            c_trc = DVE(lambda e: e.tensor_copy(rT[:], PS[0:32, 3584:3712]), inc=True)
            p.wait("gpsimd", s_dve, c_trc)
            c_rst = ST(lambda e, hd_=hd: e.dma_start(
                r_scr[hd_].rearrange("(a b) -> a b", a=32), rT[:]))
            c_rst_hist[hd] = c_rst
            p.wait("vector", s_dve, dve_stage[g_tau - 1])
            for tau in range(NTAU):
                pr = tau % 2
                gf = g_fin + tau
                if gf >= 2:
                    p.wait("sync", s_dve, dve_fin[gf - 2])
                p.wait("sync", s_st, c_rst)
                crr = LD(lambda e, hd_=hd, t=tau, b=pr: e.dma_start(
                    rbc[:, b * 512:(b + 1) * 512],
                    r_scr[hd_, t * 512:(t + 1) * 512].partition_broadcast(128)))
                p.wait("vector", s_ld, crr)
                if gf >= 2:
                    p.wait("vector", s_st, st_fin[gf - 2])
                dve_fin[gf] = DVE(lambda e, t=tau, b=pr: e.tensor_tensor(
                    qhat[:, b * 512:(b + 1) * 512], stg[:, t * 512:(t + 1) * 512],
                    rbc[:, b * 512:(b + 1) * 512], ALU.mult), inc=True)
                p.wait("gpsimd", s_dve, dve_fin[gf])
                ST(lambda e, h_=h, t=tau, b=pr, sc=scr: e.dma_start(
                    sc[h_][:, t * 512:(t + 1) * 512], qhat[:, b * 512:(b + 1) * 512]))
                st_fin[gf] = p.count(s_st)
            g_fin += NTAU
            te_hd_end[hd] = p.count(s_te)
        phase_barrier()

        tanh_hist = {}
        exp_hist = {}
        pv_hist = {}
        mask_hist = {}
        dve_on = {}
        st_on = {}
        st_rec = {}
        ld_rec = {}
        rec_hist = {}
        te_head_done = {}
        ld_head = {}
        gp_ = 0
        gJ = 0
        for h in range(H):
            hp = h % 2
            if h >= 2:
                p.wait("sync", s_te, te_head_done[h - 2])
            LD(lambda e, h_=h, b=hp: e.dma_start(
                A[:, b * 4096:(b + 1) * 4096], q_scr[h_]))
            LD(lambda e, h_=h, b=hp: e.dma_start(
                A[:, 8192 + b * 4096:8192 + (b + 1) * 4096], k_scr[h_]))
            LD(lambda e, h_=h, b=hp: e.dma_start(
                A[:, 16384 + b * 4096:16384 + (b + 1) * 4096], v_scr[h_]))
            ld_head[h] = p.count(s_ld)
            qb = A[:, hp * 4096:(hp + 1) * 4096]
            kb = A[:, 8192 + hp * 4096:8192 + (hp + 1) * 4096]
            vb = A[:, 16384 + hp * 4096:16384 + (hp + 1) * 4096]
            p.wait("tensor", s_ld, ld_head[h])
            for J in range(NTAU):
                npair = 2 * J + 2
                s_hist = {}

                def emit_S(m, J_=J):
                    pr = m % 2
                    g = gp_ + m
                    if g >= 2:
                        p.wait("tensor", s_act, tanh_hist[g - 2])
                    for ii, b in ((2 * m, 2 * pr), (2 * m + 1, 2 * pr + 1)):
                        TE(lambda e, i_=ii, b_=b, Jx=J_, q_=qb, k_=kb: e.matmul(
                            bank(b_), k_[:, i_ * 128:(i_ + 1) * 128],
                            q_[:, Jx * 512:(Jx + 1) * 512], start=True, stop=True),
                           inc=(b % 2 == 1))
                    s_hist[g] = p.count(s_te)

                emit_S(0)
                for m in range(npair):
                    pr = m % 2
                    g = gp_ + m
                    if m + 1 < npair:
                        emit_S(m + 1)
                    p.wait("scalar", s_te, s_hist[g])
                    if g >= 2:
                        p.wait("scalar", s_te, pv_hist[g - 2])
                    tanh_hist[g] = ACT(lambda e, b=pr: e.activation(
                        out=tS[:, b * 1024:(b + 1) * 1024],
                        in_=PS[:, b * 1024:(b + 1) * 1024], func=AF.Tanh, scale=SCL))
                    p.wait("scalar", s_act, tanh_hist[g])
                    exp_hist[g] = ACT(lambda e, b=pr: e.activation(
                        out=tP[:, b * 1024:(b + 1) * 1024],
                        in_=tS[:, b * 1024:(b + 1) * 1024], func=AF.Exp, scale=SOFTCAP))
                    diag = (m >= 2 * J)
                    use = tP
                    if diag:
                        d0 = 2 * m - 4 * J
                        p.wait("vector", s_act, exp_hist[g])
                        if g >= 2 and (g - 2) in mask_hist:
                            p.wait("vector", s_te, pv_hist[g - 2])
                        DVE(lambda e, b=pr, d=d0: e.tensor_tensor(
                            tPm[:, b * 1024:b * 1024 + 512],
                            tP[:, b * 1024:b * 1024 + 512],
                            maskb[:, 384 - 128 * d:896 - 128 * d], ALU.mult))
                        mask_hist[g] = DVE(lambda e, b=pr, d=d0 + 1: e.tensor_tensor(
                            tPm[:, b * 1024 + 512:b * 1024 + 1024],
                            tP[:, b * 1024 + 512:b * 1024 + 1024],
                            maskb[:, 384 - 128 * d:896 - 128 * d], ALU.mult), inc=True)
                        use = tPm
                    p.wait("tensor", s_act, exp_hist[g])
                    if diag:
                        p.wait("tensor", s_dve, mask_hist[g])
                    if m == 0:
                        if gJ >= 2:
                            p.wait("tensor", s_dve, dve_on[gJ - 2])
                            p.wait("tensor", s_dve, rec_hist[gJ - 2])
                    last = (m == npair - 1)
                    ob = 4 + (gJ % 2)
                    sb = 6 + (gJ % 2)
                    for half, ii in ((0, 2 * m), (1, 2 * m + 1)):
                        TE(lambda e, i_=ii, b=pr, hf=half, v_=vb, u_=use, m_=m,
                           ob_=ob: e.matmul(
                            bank(ob_), v_[:, i_ * 128:(i_ + 1) * 128],
                            u_[:, b * 1024 + hf * 512:b * 1024 + (hf + 1) * 512],
                            start=(m_ == 0 and hf == 0), stop=False))
                        TE(lambda e, i_=ii, b=pr, hf=half, u_=use, m_=m,
                           sb_=sb, lt=(last and half == 1): e.matmul(
                            bank(sb_, rows=1), onescol[:],
                            u_[:, b * 1024 + hf * 512:b * 1024 + (hf + 1) * 512],
                            start=(m_ == 0 and hf == 0), stop=lt),
                           inc=(half == 1))
                    pv_hist[g] = p.count(s_te)
                gp_ += npair
                c_attn = pv_hist[gp_ - 1]
                p.wait("vector", s_te, c_attn)
                if gJ >= 1:
                    p.wait("vector", s_st, st_rec[gJ - 1])
                rec_hist[gJ] = DVE(lambda e, sb=6 + (gJ % 2): e.reciprocal(
                    recrow[:], PS[0:1, sb * 512:sb * 512 + 512]), inc=True)
                p.wait("gpsimd", s_dve, rec_hist[gJ])
                if gJ >= 2:
                    p.wait("gpsimd", s_ld, ld_rec[gJ - 2])
                ST(lambda e, b=gJ % 2: e.dma_start(rec_scr[b].unsqueeze(0),
                                                   recrow[:]))
                st_rec[gJ] = p.count(s_st)
                p.wait("sync", s_st, st_rec[gJ])
                if gJ >= 1:
                    p.wait("sync", s_dve, dve_on[gJ - 1])
                ld_rec[gJ] = LD(lambda e, b=gJ % 2: e.dma_start(
                    bcsb[:], rec_scr[b].partition_broadcast(128)))
                p.wait("vector", s_ld, ld_rec[gJ])
                if gJ >= 2:
                    p.wait("vector", s_st, st_on[gJ - 2])
                dve_on[gJ] = DVE(lambda e, b=gJ % 2, ob=4 + (gJ % 2): e.tensor_tensor(
                    obf[:, b * 512:(b + 1) * 512], bank(ob), bcsb[:], ALU.mult),
                    inc=True)
                p.wait("gpsimd", s_dve, dve_on[gJ])
                ST(lambda e, h_=h, J_=J, b=gJ % 2: e.dma_start(
                    o_scr[J_][:, h_ * 512:(h_ + 1) * 512],
                    obf[:, b * 512:(b + 1) * 512]))
                st_on[gJ] = p.count(s_st)
                gJ += 1
            te_head_done[h] = p.count(s_te)
        phase_barrier()

        zview = A[:, 49152:65536].bitcast(F32)
        ln_state = {"i": 0, "ld": {}, "dve": {}, "st": {}, "stat": {}, "zlast": {},
                    "zall": {}, "mmlast": {}}

        def ln_block(tau, mm_emit, z_emit, store_emit):
            te_sums = {}
            dve_z = {}
            sq_hist = {}
            mm_hist = {}

            def emit_mms(c):
                if c >= 2:
                    p.wait("tensor", s_dve, dve_z[c - 2])
                mm_emit(c, c % 2)
                mm_hist[c] = p.count(s_te)

            emit_mms(0)
            for c in range(NK):
                b = c % 2
                if c + 1 < NK:
                    emit_mms(c + 1)
                p.wait("vector", s_te, mm_hist[c])
                z_emit(c, b)
                c_zf = p.count(s_dve)
                p.wait("vector", s_dve, c_zf)
                if c >= 2:
                    p.wait("vector", s_te, te_sums[c - 2])
                dve_z[c] = DVE(lambda e, cc=c, b_=b: e.tensor_copy(
                    zbf[:, b_ * 512:(b_ + 1) * 512],
                    zview[:, cc * 512:(cc + 1) * 512]), inc=True)
                p.wait("scalar", s_dve, dve_z[c])
                if c >= 2:
                    p.wait("scalar", s_te, te_sums[c - 2])
                sq_hist[c] = ACT(lambda e, cc=c, b_=b: e.activation(
                    out=sqz[:, b_ * 512:(b_ + 1) * 512],
                    in_=zview[:, cc * 512:(cc + 1) * 512], func=AF.Square))
                p.wait("tensor", s_act, sq_hist[c])
                if c == 0 and ln_state["i"] >= 1:
                    p.wait("tensor", s_dve, ln_state["stat"][ln_state["i"] - 1])
                TE(lambda e, cc=c, b_=b: e.matmul(
                    bank(2, rows=1), onescol[:], zbf[:, b_ * 512:(b_ + 1) * 512],
                    start=(cc == 0), stop=(cc == NK - 1)))
                TE(lambda e, cc=c, b_=b: e.matmul(
                    bank(3, rows=1), onescol[:], sqz[:, b_ * 512:(b_ + 1) * 512],
                    start=(cc == 0), stop=(cc == NK - 1)), inc=True)
                te_sums[c] = p.count(s_te)
            p.wait("vector", s_te, te_sums[NK - 1])
            if ln_state["i"] >= 1:
                p.wait("vector", s_st, ln_state["st"][ln_state["i"] - 1])
            DVE(lambda e: e.tensor_scalar(out=srow[0:1, 0:512], in0=PS[0:1, 1024:1536],
                                          scalar1=1.0 / D, scalar2=None, op0=ALU.mult))
            c_stat = DVE(lambda e: e.tensor_scalar(out=srow[0:1, 512:1024],
                                                   in0=PS[0:1, 1536:2048],
                                                   scalar1=1.0 / D, scalar2=None,
                                                   op0=ALU.mult), inc=True)
            p.wait("vector", s_dve, c_stat)
            ln_state["stat"][ln_state["i"]] = c_stat
            DVE(lambda e: e.tensor_tensor(srow[0:1, 1536:2048], srow[0:1, 0:512],
                                          srow[0:1, 0:512], ALU.mult))
            DVE(lambda e: e.tensor_tensor(srow[0:1, 512:1024], srow[0:1, 512:1024],
                                          srow[0:1, 1536:2048], ALU.subtract))
            newton_row(srow[0:1, 512:1024], recrow[0:1, :], LN_EPS)
            DVE(lambda e: e.tensor_tensor(srow[0:1, 0:512], srow[0:1, 0:512],
                                          recrow[0:1, :], ALU.mult))
            c_st2 = DVE(lambda e: e.tensor_scalar(out=srow[0:1, 0:512],
                                                  in0=srow[0:1, 0:512], scalar1=-1.0,
                                                  scalar2=None, op0=ALU.mult), inc=True)
            p.wait("gpsimd", s_dve, c_st2)
            lslot = ln_state["i"] % 2
            if ln_state["i"] >= 2:
                p.wait("gpsimd", s_ld, ln_state["ld"][ln_state["i"] - 2])
            ST(lambda e, b=lslot: e.dma_start(ln_scr[b, 0:512].unsqueeze(0),
                                              recrow[:]))
            c_lst = ST(lambda e, b=lslot: e.dma_start(ln_scr[b, 512:1024].unsqueeze(0),
                                                      srow[0:1, 0:512]))
            p.wait("sync", s_st, c_lst)
            if ln_state["i"] >= 1:
                p.wait("sync", s_dve, ln_state["dve"][ln_state["i"] - 1])
            LD(lambda e, b=lslot: e.dma_start(
                abc[:], ln_scr[b, 0:512].partition_broadcast(128)))
            c_lld = LD(lambda e, b=lslot: e.dma_start(
                bbc[:], ln_scr[b, 512:1024].partition_broadcast(128)))
            ln_state["ld"][ln_state["i"]] = c_lld
            ln_state["st"][ln_state["i"]] = c_lst
            p.wait("vector", s_ld, c_lld)
            for c in range(NK):
                b = c % 2
                if c >= 2:
                    p.wait("vector", s_st, p.count(s_st) - 16)
                DVE(lambda e, cc=c, b_=b: e.tensor_tensor(
                    y1t[:, b_ * 512:(b_ + 1) * 512],
                    zview[:, cc * 512:(cc + 1) * 512], abc[:], ALU.mult))
                c_y = DVE(lambda e, b_=b: e.tensor_tensor(
                    y1t[:, b_ * 512:(b_ + 1) * 512],
                    y1t[:, b_ * 512:(b_ + 1) * 512], bbc[:], ALU.add), inc=True)
                p.wait("gpsimd", s_dve, c_y)
                store_emit(c, b)
            ln_state["dve"][ln_state["i"]] = p.count(s_dve)
            ln_state["zlast"][ln_state["i"]] = dve_z[NK - 1]
            ln_state["zall"][ln_state["i"]] = dict(dve_z)
            ln_state["mmlast"][ln_state["i"]] = mm_hist[NK - 1]
            ln_state["i"] += 1

        dve_wo = {}
        for k in range(NK):
            par = k % 2
            if k >= 2:
                p.wait("sync", s_dve, dve_wo[k - 2])
            c = LD(lambda e, kk=k, pr=par: e.dma_start(
                FS[:, pr * 2048:(pr + 1) * 2048], wot[kk * 128:(kk + 1) * 128, :]))
            p.wait("vector", s_ld, c)
            dve_wo[k] = DVE(lambda e, kk=k, pr=par: e.tensor_copy(
                A[:, kk * 2048:(kk + 1) * 2048], FS[:, pr * 2048:(pr + 1) * 2048]),
                inc=True)
        p.wait("tensor", s_dve, dve_wo[NK - 1])
        for tau in range(NTAU):
            hp = tau % 2
            if tau >= 2:
                p.wait("sync", s_te, ln_state["mmlast"][ln_state["i"] - 1])
            LD(lambda e, t=tau, b=hp: e.dma_start(
                A[:, 32768 + b * 8192:32768 + (b + 1) * 8192], o_scr[t]))
            for g4 in range(4):
                if tau >= 1:
                    p.wait("sync", s_dve, ln_state["zall"][ln_state["i"] - 1][4 * g4 + 3])
                LD(lambda e, g=g4, t=tau: e.dma_start(
                    FS[:, g * 2048:(g + 1) * 2048].rearrange("p (k c) -> p k c", c=512),
                    xt.rearrange("(k p) f -> p k f", p=128)
                      [:, 4 * g:4 * g + 4, t * 512:(t + 1) * 512]))
            c_lds = p.count(s_ld)
            p.wait("tensor", s_ld, c_lds)
            p.wait("vector", s_ld, c_lds)

            def pc_mm(c, b, hp_=hp):
                for hc2 in range(NK):
                    TE(lambda e, cc=c, b_=b, h_=hc2, bb=hp_: e.matmul(
                        bank(b_), A[:, h_ * 2048 + cc * 128:h_ * 2048 + (cc + 1) * 128],
                        A[:, 32768 + bb * 8192 + h_ * 512:
                          32768 + bb * 8192 + (h_ + 1) * 512],
                        start=(h_ == 0), stop=(h_ == NK - 1)), inc=(hc2 == NK - 1))

            def pc_z(c, b):
                DVE(lambda e, cc=c, b_=b: e.tensor_tensor(
                    zview[:, cc * 512:(cc + 1) * 512], bank(b_),
                    FS[:, cc * 512:(cc + 1) * 512], ALU.add), inc=True)

            def pc_store(c, b, t=tau):
                ST(lambda e, cc=c, b_=b, t_=t: e.dma_start(
                    y1_scr[cc][:, t_ * 512:(t_ + 1) * 512],
                    y1t[:, b_ * 512:(b_ + 1) * 512]))

            ln_block(tau, pc_mm, pc_z, pc_store)
        phase_barrier()

        for k in range(NK):
            LD(lambda e, kk=k: e.dma_start(A[:, kk * 2048:(kk + 1) * 2048], wt_scr[kk]))
        c_wt = p.count(s_ld)
        p.wait("tensor", s_ld, c_wt)
        y1bf = A[:, 32768:40960]
        for tau in range(NTAU):
            if tau >= 1:
                p.wait("sync", s_te, ln_state["mmlast"][ln_state["i"] - 1])
            for g4 in range(4):
                if tau >= 1:
                    p.wait("sync", s_dve, ln_state["zall"][ln_state["i"] - 1][4 * g4 + 3])
                LD(lambda e, g=g4, t=tau: e.dma_start(
                    FS[:, g * 2048:(g + 1) * 2048].rearrange("p (k c) -> p k c", c=512),
                    y1_scr.ap_or_self() if False else y1_scr
                    .rearrange("k p f -> p k f")[:, 4 * g:4 * g + 4,
                                                 t * 512:(t + 1) * 512]))
            c_lds = p.count(s_ld)
            p.wait("vector", s_ld, c_lds)
            for k in range(NK):
                DVE(lambda e, kk=k: e.tensor_copy(
                    y1bf[:, kk * 512:(kk + 1) * 512], FS[:, kk * 512:(kk + 1) * 512]),
                    inc=(kk := k) == NK - 1)
            c_cast = p.count(s_dve)
            p.wait("tensor", s_dve, c_cast)

            def pd_mm(c, b):
                for k2 in range(NK):
                    TE(lambda e, cc=c, b_=b, kk=k2: e.matmul(
                        bank(b_), A[:, kk * 2048 + cc * 128:kk * 2048 + (cc + 1) * 128],
                        y1bf[:, kk * 512:(kk + 1) * 512],
                        start=(kk == 0), stop=(kk == NK - 1)), inc=(k2 == NK - 1))

            def pd_z(c, b):
                DVE(lambda e, cc=c, b_=b: e.scalar_tensor_tensor(
                    out=zview[:, cc * 512:(cc + 1) * 512], in0=bank(b_),
                    scalar=cf[:, 34 + cc:35 + cc],
                    in1=FS[:, cc * 512:(cc + 1) * 512], op0=ALU.add, op1=ALU.add),
                    inc=True)

            def pd_store(c, b, t=tau):
                ST(lambda e, cc=c, b_=b, t_=t: e.dma_start(
                    yt[cc * 128:(cc + 1) * 128, t_ * 512:(t_ + 1) * 512],
                    y1t[:, b_ * 512:(b_ + 1) * 512]))

            ln_block(tau, pd_mm, pd_z, pd_store)

        p.wait("gpsimd", s_st, p.count(s_st))
        p.wait("gpsimd", s_ld, p.count(s_ld))

        block = es.enter_context(nc.Block())

        @block.sync
        def _(eng):
            for f in p.ops["sync"]:
                f(eng)

        @block.tensor
        def _(eng):
            for f in p.ops["tensor"]:
                f(eng)

        @block.scalar
        def _(eng):
            for f in p.ops["scalar"]:
                f(eng)

        @block.vector
        def _(eng):
            for f in p.ops["vector"]:
                f(eng)

        @block.gpsimd
        def _(eng):
            for f in p.ops["gpsimd"]:
                f(eng)

    return nc


def _perm():
    pr = np.arange(DH).reshape(DH // 2, 2).T.reshape(-1)
    return np.concatenate([h * DH + pr for h in range(H)])


def kernel(L0_attn_x, L0_attn_wq, L0_attn_wk, L0_attn_wv, L0_attn_wo,
           L0_attn_q_norm_w, L0_attn_k_norm_w, L0_attn_rope_cos, L0_attn_rope_sin,
           L0_ffn_weight, L0_ffn_bias):
    x = np.asarray(L0_attn_x, dtype=np.float32)
    wq = np.asarray(L0_attn_wq, dtype=np.float32)
    wk = np.asarray(L0_attn_wk, dtype=np.float32)
    wv = np.asarray(L0_attn_wv, dtype=np.float32)
    wo = np.asarray(L0_attn_wo, dtype=np.float32)
    qnw = np.asarray(L0_attn_q_norm_w, dtype=np.float32)
    knw = np.asarray(L0_attn_k_norm_w, dtype=np.float32)
    cos = np.asarray(L0_attn_rope_cos, dtype=np.float32)
    sin = np.asarray(L0_attn_rope_sin, dtype=np.float32)
    wf = np.asarray(L0_ffn_weight, dtype=np.float32)
    bias = np.asarray(L0_ffn_bias, dtype=np.float32)

    perm = _perm()
    prd = perm[:DH]
    wqt = np.ascontiguousarray(wq[perm, :].T)
    wkt = np.ascontiguousarray(wk[perm, :].T)
    wvt = np.ascontiguousarray(wv.T)
    wot = np.ascontiguousarray(wo.T)
    wft = np.ascontiguousarray(wf.T)

    cfin = np.zeros((128, 50), dtype=np.float32)
    cfin[:, 0] = qnw[prd]
    cfin[:, 1] = knw[prd]
    cosT = np.ascontiguousarray(cos[:H].T)
    sinT = np.ascontiguousarray(sin[:H].T)
    cfin[0:64, 2:18] = cosT
    cfin[64:128, 2:18] = cosT
    cfin[0:64, 18:34] = sinT
    cfin[64:128, 18:34] = sinT
    cfin[:, 34:50] = bias.reshape(NK, 128).T

    jj = np.arange(896)[None, :]
    pp = np.arange(128)[:, None]
    maskin = (pp <= jj - 384).astype(ml_dtypes.bfloat16)
    identin = np.eye(128, dtype=np.float32)
    maskwin = np.zeros((128, 128), dtype=np.float32)
    maskwin[np.arange(64), np.arange(64) + 64] = 1.0
    maskwin[np.arange(64) + 64, np.arange(64)] = -1.0
    onesfin = np.ones((1, 128), dtype=np.float32)
    onesbin = np.ones((128, 1), dtype=ml_dtypes.bfloat16)

    nc = build_nc()
    in_maps = []
    for b in range(8):
        in_maps.append({
            "xt": np.ascontiguousarray(x[b].T),
            "wqt": wqt, "wkt": wkt, "wvt": wvt, "wot": wot, "wft": wft,
            "cfin": cfin, "maskin": maskin, "identin": identin,
            "maskwin": maskwin, "onesfin": onesfin, "onesbin": onesbin,
        })
    res = run_bass_kernel_spmd(nc, in_maps, core_ids=list(range(8)))
    global _last_results
    if DEBUG_SCRATCH:
        _last_results = res.results
    out = np.stack([np.asarray(r["yt"]).T for r in res.results], axis=0)
    return np.ascontiguousarray(out.astype(np.float32))


# version 10
# speedup vs baseline: 1.0633x; 1.0335x over previous
import numpy as np
import ml_dtypes
import concourse.bass as bass
import concourse.mybir as mybir
from concourse.bass_utils import run_bass_kernel_spmd

F32 = mybir.dt.float32
BF16 = mybir.dt.bfloat16
I32 = mybir.dt.int32
AF = mybir.ActivationFunctionType
ALU = mybir.AluOpType

T = 4096
D = 2048
H = 16
DH = 128
NK = D // 128
NTAU = T // 512
NST = T // 128
RMS_EPS = 1e-6
LN_EPS = 1e-5
SOFTCAP = 50.0
SCL = (DH ** -0.5) / SOFTCAP
MAGIC = 0x5F3759DF

ENGS = ("sync", "tensor", "scalar", "vector", "gpsimd")
DEBUG_SCRATCH = False
_last_results = None


class Prog:
    def __init__(self):
        self.ops = {e: [] for e in ENGS}
        self.cnt = {}

    def wait(self, eng, sem, val):
        if val and val > 0:
            self.ops[eng].append(lambda e, s=sem, v=val: e.wait_ge(s, v))

    def do(self, eng, fn, inc=None, n=1):
        if inc is not None:
            self.cnt[inc] = self.cnt.get(inc, 0) + n
            tot = self.cnt[inc]
            self.ops[eng].append(lambda e, f=fn, s=inc, k=n: f(e).then_inc(s, k))
            return tot
        self.ops[eng].append(fn)
        return None

    def count(self, sem):
        return self.cnt.get(sem, 0)


def build_nc():
    nc = bass.Bass("TRN2", target_bir_lowering=False)

    xt = nc.dram_tensor("xt", [D, T], F32, kind="ExternalInput").ap()
    wqt = nc.dram_tensor("wqt", [D, D], F32, kind="ExternalInput").ap()
    wkt = nc.dram_tensor("wkt", [D, D], F32, kind="ExternalInput").ap()
    wvt = nc.dram_tensor("wvt", [D, D], F32, kind="ExternalInput").ap()
    wot = nc.dram_tensor("wot", [D, D], F32, kind="ExternalInput").ap()
    wft = nc.dram_tensor("wft", [D, D], F32, kind="ExternalInput").ap()
    cfin = nc.dram_tensor("cfin", [128, 50], F32, kind="ExternalInput").ap()
    maskin = nc.dram_tensor("maskin", [128, 896], BF16, kind="ExternalInput").ap()
    identin = nc.dram_tensor("identin", [128, 128], F32, kind="ExternalInput").ap()
    maskwin = nc.dram_tensor("maskwin", [128, 128], F32, kind="ExternalInput").ap()
    onesfin = nc.dram_tensor("onesfin", [1, 128], F32, kind="ExternalInput").ap()
    onesbin = nc.dram_tensor("onesbin", [128, 1], BF16, kind="ExternalInput").ap()
    yt = nc.dram_tensor("yt", [D, T], F32, kind="ExternalOutput").ap()

    kw = {"kind": "ExternalOutput"} if DEBUG_SCRATCH else {}
    q_scr = nc.dram_tensor("q_scr", [H, 128, T], BF16, **kw).ap()
    k_scr = nc.dram_tensor("k_scr", [H, 128, T], BF16, **kw).ap()
    v_scr = nc.dram_tensor("v_scr", [H, 128, T], BF16, **kw).ap()
    o_scr = nc.dram_tensor("o_scr", [NTAU, 128, H * 512], BF16, **kw).ap()
    y1_scr = nc.dram_tensor("y1_scr", [NK, 128, T], F32, **kw).ap()
    wt_scr = nc.dram_tensor("wt_scr", [NK, 128, D], BF16, **kw).ap()
    r_scr = nc.dram_tensor("r_scr", [2 * H, T], F32, **kw).ap()
    at_scr = nc.dram_tensor("at_scr", [128, 64], F32).ap()
    thr_scr = nc.dram_tensor("thr_scr", [2048], F32, **kw).ap()
    rec_scr = nc.dram_tensor("rec_scr", [2, 512], F32).ap()
    ln_scr = nc.dram_tensor("ln_scr", [2, 1024], F32).ap()
    u_dbg = nc.dram_tensor("u_dbg", [2 * H, 128, 32], F32, **kw).ap()
    r2_dbg = nc.dram_tensor("r2_dbg", [2 * H, 128, 32], F32, **kw).ap()
    sd_dbg = nc.dram_tensor("sd_dbg", [2 * H, 128, 32], F32, **kw).ap()
    it_dbg = nc.dram_tensor("it_dbg", [2 * H, 128, 32], F32, **kw).ap()
    z_dbg = nc.dram_tensor("z_dbg", [128, 8192], F32, **kw).ap()
    osb_dbg = nc.dram_tensor("osb_dbg", [128, 8192], F32, **kw).ap()
    xf_dbg = nc.dram_tensor("xf_dbg", [128, 8192], F32, **kw).ap()
    rall_dbg = nc.dram_tensor("rall_dbg", [128, 4096], BF16, **kw).ap()

    p = Prog()
    from contextlib import ExitStack
    es = ExitStack()
    with es:
        A = es.enter_context(nc.sbuf_tensor("A", [128, 65536], BF16))
        FS = es.enter_context(nc.sbuf_tensor("FS", [128, 8192], F32))
        Rall = es.enter_context(nc.sbuf_tensor("Rall", [128, 32 * 128], BF16))
        cf = es.enter_context(nc.sbuf_tensor("cf", [128, 50], F32))
        maskb = es.enter_context(nc.sbuf_tensor("maskb", [128, 896], BF16))
        identf = es.enter_context(nc.sbuf_tensor("identf", [128, 128], F32))
        maskw = es.enter_context(nc.sbuf_tensor("maskw", [128, 128], F32))
        onesrow = es.enter_context(nc.sbuf_tensor("onesrow", [1, 128], F32))
        onescol = es.enter_context(nc.sbuf_tensor("onescol", [128, 1], BF16))
        onescolf = es.enter_context(nc.sbuf_tensor("onescolf", [128, 1], F32))
        Atile = es.enter_context(nc.sbuf_tensor("Atile", [128, 64], F32))
        rtmp1 = es.enter_context(nc.sbuf_tensor("rtmp1", [128, 128], BF16))
        rtmp2 = es.enter_context(nc.sbuf_tensor("rtmp2", [128, 128], BF16))
        rbuf = es.enter_context(nc.sbuf_tensor("rbuf", [128, 32], F32))
        ntb = es.enter_context(nc.sbuf_tensor("ntb", [128, 32], F32))
        hhb = es.enter_context(nc.sbuf_tensor("hhb", [128, 32], F32))
        rT = es.enter_context(nc.sbuf_tensor("rT", [32, 128], F32))
        rbc = es.enter_context(nc.sbuf_tensor("rbc", [128, 1024], F32))
        abc = es.enter_context(nc.sbuf_tensor("abc", [128, 512], F32))
        bbc = es.enter_context(nc.sbuf_tensor("bbc", [128, 512], F32))
        qraw = es.enter_context(nc.sbuf_tensor("qraw", [128, 1024], BF16))
        sqbf = es.enter_context(nc.sbuf_tensor("sqbf", [128, 1024], BF16))
        qhat = es.enter_context(nc.sbuf_tensor("qhat", [128, 1024], BF16))
        recrow = es.enter_context(nc.sbuf_tensor("recrow", [1, 512], F32))
        bcsb = es.enter_context(nc.sbuf_tensor("bcsb", [128, 512], F32))
        zbf = es.enter_context(nc.sbuf_tensor("zbf", [128, 1024], BF16))
        sqz = es.enter_context(nc.sbuf_tensor("sqz", [128, 1024], BF16))
        y1t = es.enter_context(nc.sbuf_tensor("y1t", [128, 1024], F32))
        srow = es.enter_context(nc.sbuf_tensor("srow", [1, 2048], F32))
        PS = es.enter_context(nc.psum_tensor("PS", [128, 4096], F32))

        s_ld = es.enter_context(nc.semaphore("s_ld"))
        s_st = es.enter_context(nc.semaphore("s_st"))
        s_te = es.enter_context(nc.semaphore("s_te"))
        s_act = es.enter_context(nc.semaphore("s_act"))
        s_dve = es.enter_context(nc.semaphore("s_dve"))

        def bank(i, rows=128, width=512):
            return PS[0:rows, 512 * i: 512 * i + width]

        tS = FS[:, 0:2048]
        tP = FS[:, 2048:3072].bitcast(BF16)
        tPm = FS[:, 3072:4096].bitcast(BF16)
        obf = FS[:, 4096:4608].bitcast(BF16)
        gtv = A[:, 40960:43008]
        ltv = A[:, 43008:45056]
        absf = A[:, 45056:49152].bitcast(F32)
        vevac = FS[:, 5120:6144].bitcast(BF16)
        throw_ = FS[0:1, 6144:8192]

        LD = lambda fn: p.do("sync", fn, inc=s_ld, n=16)
        ST = lambda fn: p.do("gpsimd", fn, inc=s_st, n=16)
        TE = lambda fn, inc=False: p.do("tensor", fn, inc=s_te if inc else None)
        ACT = lambda fn, inc=True: p.do("scalar", fn, inc=s_act if inc else None)
        DVE = lambda fn, inc=False: p.do("vector", fn, inc=s_dve if inc else None)

        def newton_cols(src_ap, scale_mean, eps, dbg=None, dbg2=None, dbg3=None):
            def F(fn):
                c = DVE(fn, inc=True)
                p.wait("vector", s_dve, c)
                return c
            c_u = F(lambda e, s=src_ap, sm=scale_mean, ep=eps: e.tensor_scalar(
                out=ntb[:, 0:32], in0=s, scalar1=sm, scalar2=ep,
                op0=ALU.mult, op1=ALU.add))
            if dbg is not None:
                p.wait("gpsimd", s_dve, c_u)
                ST(lambda e, d=dbg: e.dma_start(d, ntb[:, 0:32]))
            F(lambda e: e.tensor_scalar(
                out=rbuf[:, 0:32].bitcast(I32), in0=ntb[:, 0:32].bitcast(I32),
                scalar1=1, scalar2=None, op0=ALU.arith_shift_right))
            F(lambda e: e.tensor_scalar(
                out=rbuf[:, 0:32].bitcast(I32), in0=rbuf[:, 0:32].bitcast(I32),
                scalar1=-1, scalar2=MAGIC, op0=ALU.mult, op1=ALU.add))
            for it_ in range(4):
                F(lambda e: e.tensor_tensor(hhb[:, 0:32], rbuf[:, 0:32],
                                            rbuf[:, 0:32], ALU.mult))
                F(lambda e: e.tensor_tensor(hhb[:, 0:32], hhb[:, 0:32],
                                            ntb[:, 0:32], ALU.mult))
                F(lambda e: e.tensor_scalar(out=hhb[:, 0:32], in0=hhb[:, 0:32],
                                            scalar1=-0.5, scalar2=1.5,
                                            op0=ALU.mult, op1=ALU.add))
                F(lambda e: e.tensor_tensor(rbuf[:, 0:32], rbuf[:, 0:32],
                                            hhb[:, 0:32], ALU.mult))
            return F(lambda e: e.tensor_copy(rbuf[:, 0:32], rbuf[:, 0:32]))

        def newton_row(src_ap, dst_ap, eps, w=512):
            u = srow[0:1, 512:512 + w]
            t = srow[0:1, 1024:1024 + w]
            def F(fn):
                c = DVE(fn, inc=True)
                p.wait("vector", s_dve, c)
                return c
            F(lambda e, s=src_ap, uu=u, ep=eps: e.tensor_scalar(
                out=uu, in0=s, scalar1=1.0, scalar2=ep, op0=ALU.mult, op1=ALU.add))
            F(lambda e, uu=u, y=dst_ap: e.tensor_scalar(
                out=y.bitcast(I32), in0=uu.bitcast(I32), scalar1=1, scalar2=None,
                op0=ALU.arith_shift_right))
            F(lambda e, y=dst_ap: e.tensor_scalar(
                out=y.bitcast(I32), in0=y.bitcast(I32), scalar1=-1, scalar2=MAGIC,
                op0=ALU.mult, op1=ALU.add))
            for _ in range(4):
                F(lambda e, y=dst_ap, tt=t: e.tensor_tensor(tt, y, y, ALU.mult))
                F(lambda e, uu=u, tt=t: e.tensor_tensor(tt, tt, uu, ALU.mult))
                F(lambda e, tt=t: e.tensor_scalar(out=tt, in0=tt, scalar1=-0.5,
                                                  scalar2=1.5, op0=ALU.mult,
                                                  op1=ALU.add))
                F(lambda e, y=dst_ap, tt=t: e.tensor_tensor(y, y, tt, ALU.mult))

        def phase_barrier():
            snap = dict(p.cnt)
            for eng in ENGS:
                for sem in (s_ld, s_st, s_te, s_act, s_dve):
                    p.wait(eng, sem, snap.get(sem, 0))

        LD(lambda e: e.dma_start(cf[:], cfin))
        LD(lambda e: e.dma_start(maskb[:], maskin))
        LD(lambda e: e.dma_start(identf[:], identin))
        LD(lambda e: e.dma_start(maskw[:], maskwin))
        LD(lambda e: e.dma_start(onesrow[:], onesfin))
        LD(lambda e: e.dma_start(onescol[:], onesbin))
        c_consts = LD(lambda e: e.dma_start(
            onescolf[:], onesfin.rearrange("o f -> f o")))

        p.wait("vector", s_ld, c_consts)
        DVE(lambda e: e.tensor_scalar(out=Atile[:, 0:16], in0=cf[:, 2:18],
                                      scalar1=cf[:, 0:1], scalar2=None, op0=ALU.mult))
        DVE(lambda e: e.tensor_scalar(out=Atile[:, 16:32], in0=cf[:, 18:34],
                                      scalar1=cf[:, 0:1], scalar2=None, op0=ALU.mult))
        DVE(lambda e: e.tensor_scalar(out=Atile[:, 32:48], in0=cf[:, 2:18],
                                      scalar1=cf[:, 1:2], scalar2=None, op0=ALU.mult))
        c_at = DVE(lambda e: e.tensor_scalar(out=Atile[:, 48:64], in0=cf[:, 18:34],
                                             scalar1=cf[:, 1:2], scalar2=None,
                                             op0=ALU.mult), inc=True)
        p.wait("gpsimd", s_dve, c_at)
        c_atst = ST(lambda e: e.dma_start(at_scr, Atile[:]))
        p.wait("sync", s_st, c_atst)
        c_atld = LD(lambda e: e.dma_start(Atile[:], at_scr))
        p.wait("vector", s_ld, c_atld)
        for hd in range(2 * H):
            base = 0 if hd < H else 32
            hc = hd % H
            DVE(lambda e, b=base, h_=hc: e.tensor_scalar(
                out=rtmp1[:], in0=identf[:], scalar1=Atile[:, b + h_:b + h_ + 1],
                scalar2=None, op0=ALU.mult))
            DVE(lambda e, b=base, h_=hc: e.tensor_scalar(
                out=rtmp2[:], in0=maskw[:], scalar1=Atile[:, b + 16 + h_:b + 17 + h_],
                scalar2=None, op0=ALU.mult))
            DVE(lambda e, hd_=hd: e.tensor_tensor(
                Rall[:, hd_ * 128:(hd_ + 1) * 128], rtmp1[:], rtmp2[:], ALU.add))

        if DEBUG_SCRATCH:
            c_rall = DVE(lambda e: e.tensor_copy(rtmp1[:, 0:1], rtmp1[:, 0:1]),
                         inc=True)
            p.wait("gpsimd", s_dve, c_rall)
            ST(lambda e: e.dma_start(rall_dbg, Rall[:]))
        te_abs = {}
        act_abs = {}
        res_hist = {}
        for k in range(NK):
            par = k % 2
            if k >= 2:
                p.wait("sync", s_act, act_abs[k - 2])
            c = LD(lambda e, kk=k, pr=par: e.dma_start(
                FS[:, pr * 2048:(pr + 1) * 2048], wft[kk * 128:(kk + 1) * 128, :]))
            p.wait("scalar", s_ld, c)
            if k >= 1:
                p.wait("scalar", s_te, te_abs[k - 1])
                p.wait("scalar", s_dve, res_hist[k - 1])
            act_abs[k] = ACT(lambda e, pr=par: e.activation(
                out=absf[:, 0:2048], in_=FS[:, pr * 2048:(pr + 1) * 2048],
                func=AF.Abs))
            p.wait("vector", s_act, act_abs[k])
            if k >= 1:
                p.wait("vector", s_te, te_abs[k - 1])
            c_hi = DVE(lambda e: e.tensor_copy(gtv[:, 0:2048], absf[:, 0:2048]),
                       inc=True)
            p.wait("vector", s_dve, c_hi)
            c_res = DVE(lambda e: e.tensor_tensor(ltv[:, 0:2048], absf[:, 0:2048],
                                                  gtv[:, 0:2048], ALU.subtract),
                        inc=True)
            res_hist[k] = c_res
            p.wait("tensor", s_dve, c_res)
            for g in range(4):
                TE(lambda e, gg=g, kk=k: e.matmul(
                    bank(gg, rows=1), onescol[:], gtv[:, gg * 512:(gg + 1) * 512],
                    start=(kk == 0), stop=(kk == NK - 1)), inc=False)
                TE(lambda e, gg=g, kk=k: e.matmul(
                    bank(4 + gg, rows=1), onescol[:], ltv[:, gg * 512:(gg + 1) * 512],
                    start=(kk == 0), stop=(kk == NK - 1)), inc=(g == 3))
            te_abs[k] = p.count(s_te)
        p.wait("vector", s_te, te_abs[NK - 1])
        thrbc = A[:, 57344:61440].bitcast(F32)
        nthrbc = A[:, 61440:65536].bitcast(F32)
        for g in range(4):
            DVE(lambda e, gg=g: e.tensor_scalar(
                out=srow[0:1, gg * 512:(gg + 1) * 512],
                in0=PS[0:1, gg * 512:(gg + 1) * 512],
                scalar1=0.7 / D, scalar2=None, op0=ALU.mult))
        c_s1 = DVE(lambda e: e.tensor_copy(srow[0:1, 0:1], srow[0:1, 0:1]), inc=True)
        p.wait("vector", s_dve, c_s1)
        for g in range(4):
            DVE(lambda e, gg=g: e.scalar_tensor_tensor(
                out=throw_[0:1, gg * 512:(gg + 1) * 512],
                in0=PS[0:1, 2048 + gg * 512:2048 + (gg + 1) * 512],
                scalar=0.7 / D, in1=srow[0:1, gg * 512:(gg + 1) * 512],
                op0=ALU.mult, op1=ALU.add))
        c_thr = DVE(lambda e: e.tensor_copy(srow[0:1, 0:1], srow[0:1, 0:1]), inc=True)
        p.wait("gpsimd", s_dve, c_thr)
        c_thrst = ST(lambda e: e.dma_start(thr_scr.rearrange("(o f) -> o f", o=1),
                                           throw_[0:1, :]))
        p.wait("sync", s_st, c_thrst)
        c_thrld = LD(lambda e: e.dma_start(thrbc[:, 0:2048],
                                           thr_scr.partition_broadcast(128)))
        p.wait("vector", s_ld, c_thrld)
        c_nthr = DVE(lambda e: e.tensor_scalar(out=nthrbc[:, 0:2048],
                                               in0=thrbc[:, 0:2048], scalar1=-1.0,
                                               scalar2=None, op0=ALU.mult), inc=True)
        wt_tile = FS[:, 4096:6144].bitcast(BF16)
        dve_q = {}
        for k in range(NK):
            par = k % 2
            if k >= 2:
                p.wait("sync", s_dve, dve_q[k - 2])
            c = LD(lambda e, kk=k, pr=par: e.dma_start(
                FS[:, pr * 2048:(pr + 1) * 2048], wft[kk * 128:(kk + 1) * 128, :]))
            p.wait("vector", s_ld, c)
            if k == 0:
                p.wait("vector", s_dve, c_nthr)
            DVE(lambda e, pr=par: e.tensor_tensor(
                gtv[:, 0:2048], FS[:, pr * 2048:(pr + 1) * 2048],
                thrbc[:, 0:2048], ALU.is_gt))
            DVE(lambda e, pr=par: e.tensor_tensor(
                ltv[:, 0:2048], FS[:, pr * 2048:(pr + 1) * 2048],
                nthrbc[:, 0:2048], ALU.is_lt))
            if k >= 2:
                p.wait("vector", s_st, p.count(s_st) - 16)
            dve_q[k] = DVE(lambda e, pr=par: e.tensor_tensor(
                wt_tile[:, pr * 2048:(pr + 1) * 2048], gtv[:, 0:2048],
                ltv[:, 0:2048], ALU.subtract), inc=True)
            p.wait("gpsimd", s_dve, dve_q[k])
            ST(lambda e, kk=k, pr=par: e.dma_start(
                wt_scr[kk], wt_tile[:, pr * 2048:(pr + 1) * 2048]))
        p.wait("sync", s_dve, dve_q[NK - 1])
        p.wait("sync", s_st, p.count(s_st))
        dve_x = {}
        for k in range(NK):
            par = k % 2
            if k >= 2:
                p.wait("sync", s_dve, dve_x[k - 2])
            c = LD(lambda e, kk=k, pr=par: e.dma_start(
                FS[:, pr * 4096:(pr + 1) * 4096], xt[kk * 128:(kk + 1) * 128, :]))
            p.wait("vector", s_ld, c)
            dve_x[k] = DVE(lambda e, kk=k, pr=par: e.tensor_copy(
                A[:, kk * 4096:(kk + 1) * 4096], FS[:, pr * 4096:(pr + 1) * 4096]),
                inc=True)
        phase_barrier()

        wvq = FS[:, 0:4096].bitcast(BF16)
        wvf = FS[:, 4096:5120]
        st_evac = {}
        dve_evac = {}
        gidx = 0
        dve_wv_all = {}
        for g4 in range(4):
            dve_wv = {}
            for k in range(NK):
                par = k % 2
                prev = dve_wv_all.get(g4 * NK + k - 2)
                if prev:
                    p.wait("sync", s_dve, prev)
                c = LD(lambda e, kk=k, g=g4, pr=par: e.dma_start(
                    wvf[:, pr * 512:(pr + 1) * 512],
                    wvt[kk * 128:(kk + 1) * 128, g * 512:(g + 1) * 512]))
                p.wait("vector", s_ld, c)
                if g4 > 0 and k == 0:
                    p.wait("vector", s_te, p.count(s_te))
                dve_wv[k] = DVE(lambda e, kk=k, pr=par: e.tensor_copy(
                    wvq[:, kk * 512:(kk + 1) * 512], wvf[:, pr * 512:(pr + 1) * 512]),
                    inc=True)
                dve_wv_all[g4 * NK + k] = dve_wv[k]
            p.wait("tensor", s_dve, dve_wv[NK - 1])
            for oct_ in range(4):
                if gidx >= 8:
                    p.wait("tensor", s_dve, dve_evac[gidx - 1])
                for k in range(NK):
                    for s8 in range(8):
                        st_ = oct_ * 8 + s8
                        TE(lambda e, kk=k, s=st_, b=s8: e.matmul(
                            bank(b), A[:, kk * 4096 + s * 128:kk * 4096 + (s + 1) * 128],
                            wvq[:, kk * 512:(kk + 1) * 512],
                            start=(kk == 0), stop=(kk == NK - 1)),
                           inc=(k == NK - 1 and s8 == 7))
                c_vmm = p.count(s_te)
                p.wait("vector", s_te, c_vmm)
                for s8 in range(8):
                    st_ = oct_ * 8 + s8
                    if gidx >= 2:
                        p.wait("vector", s_st, st_evac[gidx - 2])
                    dve_evac[gidx] = DVE(lambda e, b=s8: e.tensor_copy(
                        vevac[:, (b % 2) * 1024:(b % 2) * 1024 + 512], bank(b)), inc=True)
                    p.wait("gpsimd", s_dve, dve_evac[gidx])
                    for j in range(4):
                        ST(lambda e, g=g4, s=st_, b=s8, jj=j: e.dma_start(
                            v_scr[4 * g + jj, :, s * 128:(s + 1) * 128],
                            vevac[:, (b % 2) * 1024 + jj * 128:
                                  (b % 2) * 1024 + (jj + 1) * 128]))
                    st_evac[gidx] = p.count(s_st)
                    gidx += 1
        phase_barrier()

        wpf = FS[:, 0:2048]
        wpb = FS[:, 2048:4096].bitcast(BF16)
        stgA = FS[:, 4096:6144].bitcast(BF16)
        stgB = FS[:, 6144:8192].bitcast(BF16)
        act_hist = {}
        dve_stage = {}
        te_rope = {}
        te_bc = {}
        dve_fin = {}
        st_fin = {}
        te_hd_end = {}
        cwb_hist = {}
        c_tr_hist = {}
        c_rst_hist = {}
        g_tau = 0
        g_fin = 0
        pending_tail = [None]

        def emit_pa_tail(hd, h, proj, scr, stg, gt, gf0):
            p.wait("vector", s_te, te_rope[gt + NTAU - 1])
            if hd >= 1:
                p.wait("vector", s_te, c_tr_hist[hd - 1])
                p.wait("vector", s_st, c_rst_hist[hd - 1])
            uo = 2048 + 32 * (hd % 2)
            c_newt = newton_cols(PS[:, uo:uo + 32], 1.0 / DH, RMS_EPS)
            p.wait("tensor", s_dve, c_newt)
            c_tr = TE(lambda e: e.transpose(PS[0:32, 3584:3712], rbuf[:, 0:32],
                                            identf[:]), inc=True)
            c_tr_hist[hd] = c_tr
            p.wait("vector", s_te, c_tr)
            c_trc = DVE(lambda e: e.tensor_copy(rT[:], PS[0:32, 3584:3712]), inc=True)
            p.wait("gpsimd", s_dve, c_trc)
            c_rst = ST(lambda e, hd_=hd: e.dma_start(
                r_scr[hd_].rearrange("(a b) -> a b", a=32), rT[:]))
            c_rst_hist[hd] = c_rst
            p.wait("vector", s_dve, dve_stage[gt + NTAU - 1])
            for tau in range(NTAU):
                pr = tau % 2
                gf = gf0 + tau
                if gf >= 2:
                    p.wait("sync", s_dve, dve_fin[gf - 2])
                p.wait("sync", s_st, c_rst)
                crr = LD(lambda e, hd_=hd, t=tau, b=pr: e.dma_start(
                    rbc[:, b * 512:(b + 1) * 512],
                    r_scr[hd_, t * 512:(t + 1) * 512].partition_broadcast(128)))
                p.wait("vector", s_ld, crr)
                if gf >= 2:
                    p.wait("vector", s_st, st_fin[gf - 2])
                dve_fin[gf] = DVE(lambda e, t=tau, b=pr, sg=stg: e.tensor_tensor(
                    qhat[:, b * 512:(b + 1) * 512], sg[:, t * 512:(t + 1) * 512],
                    rbc[:, b * 512:(b + 1) * 512], ALU.mult), inc=True)
                p.wait("gpsimd", s_dve, dve_fin[gf])
                ST(lambda e, h_=h, t=tau, b=pr, sc=scr: e.dma_start(
                    sc[h_][:, t * 512:(t + 1) * 512], qhat[:, b * 512:(b + 1) * 512]))
                st_fin[gf] = p.count(s_st)
            te_hd_end[hd] = p.count(s_te)

        for hd in range(2 * H):
            proj = 0 if hd < H else 1
            h = hd % H
            wsrc = wqt if proj == 0 else wkt
            scr = q_scr if proj == 0 else k_scr
            wslot = hd % 2
            stg = stgA if hd % 2 == 0 else stgB
            if hd >= 1:
                p.wait("sync", s_dve, cwb_hist[hd - 1])
            cw = LD(lambda e, w=wsrc, h_=h: e.dma_start(
                wpf[:, 0:2048].rearrange("p (k c) -> p k c", c=128),
                w[:, h_ * 128:(h_ + 1) * 128].rearrange("(k p) c -> p k c", p=128)))
            p.wait("vector", s_ld, cw)
            if hd >= 2:
                p.wait("vector", s_te, te_hd_end[hd - 2])
            cwb = DVE(lambda e, sl=wslot: e.tensor_copy(
                wpb[:, sl * 2048:(sl + 1) * 2048], wpf[:, 0:2048]), inc=True)
            cwb_hist[hd] = cwb
            p.wait("tensor", s_dve, cwb)
            te_proj = {}

            def emit_proj(tau, wslot=wslot, g_tau0=g_tau):
                pr = tau % 2
                gi = g_tau0 + tau
                if gi >= 2:
                    p.wait("tensor", s_act, act_hist[gi - 2])
                for k in range(NK):
                    TE(lambda e, kk=k, sl=wslot, t=tau, b=pr: e.matmul(
                        bank(b), wpb[:, sl * 2048 + kk * 128:sl * 2048 + (kk + 1) * 128],
                        A[:, kk * 4096 + t * 512:kk * 4096 + (t + 1) * 512],
                        start=(kk == 0), stop=(kk == NK - 1)), inc=(k == NK - 1))
                te_proj[tau] = p.count(s_te)

            emit_proj(0)
            for tau in range(NTAU):
                pr = tau % 2
                gi = g_tau + tau
                if tau + 1 < NTAU:
                    emit_proj(tau + 1)
                p.wait("scalar", s_te, te_proj[tau])
                ACT(lambda e, b=pr: e.activation(
                    out=qraw[:, b * 512:(b + 1) * 512], in_=bank(b), func=AF.Copy),
                    inc=False)
                act_hist[gi] = ACT(lambda e, b=pr: e.activation(
                    out=sqbf[:, b * 512:(b + 1) * 512], in_=bank(b), func=AF.Square))
                p.wait("tensor", s_act, act_hist[gi])
                if gi >= 2:
                    p.wait("tensor", s_dve, dve_stage[gi - 2])
                TE(lambda e, hd_=hd, b=pr: e.matmul(
                    bank(2 + b), Rall[:, hd_ * 128:(hd_ + 1) * 128],
                    qraw[:, b * 512:(b + 1) * 512], start=True, stop=True))
                if tau == 0 and pending_tail[0] is not None:
                    pass
                uoff = 2048 + 32 * (hd % 2)
                for i4 in range(4):
                    col = 4 * tau + i4
                    TE(lambda e, b=pr, cc=col, uo=uoff: e.matmul(
                        PS[:, uo + cc:uo + cc + 1],
                        sqbf[:, b * 512 + (cc % 4) * 128:b * 512 + ((cc % 4) + 1) * 128],
                        onescol[:], start=True, stop=True), inc=(i4 == 3))
                te_rope[gi] = p.count(s_te)
                p.wait("vector", s_te, te_rope[gi])
                dve_stage[gi] = DVE(lambda e, b=pr, t=tau, sg=stg: e.tensor_copy(
                    sg[:, t * 512:(t + 1) * 512], bank(2 + b)), inc=True)
                if tau == 0 and pending_tail[0] is not None:
                    pending_tail[0]()
                    pending_tail[0] = None
            args = (hd, h, proj, scr, stg, g_tau, g_fin)
            pending_tail[0] = (lambda a=args: emit_pa_tail(*a))
            g_tau += NTAU
            g_fin += NTAU
        pending_tail[0]()
        pending_tail[0] = None
        phase_barrier()

        tanh_hist = {}
        exp_hist = {}
        pv_hist = {}
        mask_hist = {}
        dve_on = {}
        st_on = {}
        st_rec = {}
        ld_rec = {}
        rec_hist = {}
        te_head_done = {}
        ld_head = {}
        gp_ = 0
        gJ = 0
        for h in range(H):
            hp = h % 2
            if h >= 2:
                p.wait("sync", s_te, te_head_done[h - 2])
            LD(lambda e, h_=h, b=hp: e.dma_start(
                A[:, b * 4096:(b + 1) * 4096], q_scr[h_]))
            LD(lambda e, h_=h, b=hp: e.dma_start(
                A[:, 8192 + b * 4096:8192 + (b + 1) * 4096], k_scr[h_]))
            LD(lambda e, h_=h, b=hp: e.dma_start(
                A[:, 16384 + b * 4096:16384 + (b + 1) * 4096], v_scr[h_]))
            ld_head[h] = p.count(s_ld)
            qb = A[:, hp * 4096:(hp + 1) * 4096]
            kb = A[:, 8192 + hp * 4096:8192 + (hp + 1) * 4096]
            vb = A[:, 16384 + hp * 4096:16384 + (hp + 1) * 4096]
            p.wait("tensor", s_ld, ld_head[h])
            for J in range(NTAU):
                npair = 2 * J + 2
                s_hist = {}

                def emit_S(m, J_=J):
                    pr = m % 2
                    g = gp_ + m
                    if g >= 2:
                        p.wait("tensor", s_act, tanh_hist[g - 2])
                    for ii, b in ((2 * m, 2 * pr), (2 * m + 1, 2 * pr + 1)):
                        TE(lambda e, i_=ii, b_=b, Jx=J_, q_=qb, k_=kb: e.matmul(
                            bank(b_), k_[:, i_ * 128:(i_ + 1) * 128],
                            q_[:, Jx * 512:(Jx + 1) * 512], start=True, stop=True),
                           inc=(b % 2 == 1))
                    s_hist[g] = p.count(s_te)

                emit_S(0)
                for m in range(npair):
                    pr = m % 2
                    g = gp_ + m
                    if m + 1 < npair:
                        emit_S(m + 1)
                    p.wait("scalar", s_te, s_hist[g])
                    if g >= 2:
                        p.wait("scalar", s_te, pv_hist[g - 2])
                    tanh_hist[g] = ACT(lambda e, b=pr: e.activation(
                        out=tS[:, b * 1024:(b + 1) * 1024],
                        in_=PS[:, b * 1024:(b + 1) * 1024], func=AF.Tanh, scale=SCL))
                    p.wait("scalar", s_act, tanh_hist[g])
                    exp_hist[g] = ACT(lambda e, b=pr: e.activation(
                        out=tP[:, b * 1024:(b + 1) * 1024],
                        in_=tS[:, b * 1024:(b + 1) * 1024], func=AF.Exp, scale=SOFTCAP))
                    diag = (m >= 2 * J)
                    use = tP
                    if diag:
                        d0 = 2 * m - 4 * J
                        p.wait("vector", s_act, exp_hist[g])
                        if g >= 2 and (g - 2) in mask_hist:
                            p.wait("vector", s_te, pv_hist[g - 2])
                        DVE(lambda e, b=pr, d=d0: e.tensor_tensor(
                            tPm[:, b * 1024:b * 1024 + 512],
                            tP[:, b * 1024:b * 1024 + 512],
                            maskb[:, 384 - 128 * d:896 - 128 * d], ALU.mult))
                        mask_hist[g] = DVE(lambda e, b=pr, d=d0 + 1: e.tensor_tensor(
                            tPm[:, b * 1024 + 512:b * 1024 + 1024],
                            tP[:, b * 1024 + 512:b * 1024 + 1024],
                            maskb[:, 384 - 128 * d:896 - 128 * d], ALU.mult), inc=True)
                        use = tPm
                    p.wait("tensor", s_act, exp_hist[g])
                    if diag:
                        p.wait("tensor", s_dve, mask_hist[g])
                    if m == 0:
                        if gJ >= 2:
                            p.wait("tensor", s_dve, dve_on[gJ - 2])
                            p.wait("tensor", s_dve, rec_hist[gJ - 2])
                    last = (m == npair - 1)
                    ob = 4 + (gJ % 2)
                    sb = 6 + (gJ % 2)
                    for half, ii in ((0, 2 * m), (1, 2 * m + 1)):
                        TE(lambda e, i_=ii, b=pr, hf=half, v_=vb, u_=use, m_=m,
                           ob_=ob: e.matmul(
                            bank(ob_), v_[:, i_ * 128:(i_ + 1) * 128],
                            u_[:, b * 1024 + hf * 512:b * 1024 + (hf + 1) * 512],
                            start=(m_ == 0 and hf == 0), stop=False))
                    for half in (0, 1):
                        TE(lambda e, b=pr, hf=half, u_=use, m_=m,
                           sb_=sb, lt=(last and half == 1): e.matmul(
                            bank(sb_, rows=1), onescol[:],
                            u_[:, b * 1024 + hf * 512:b * 1024 + (hf + 1) * 512],
                            start=(m_ == 0 and hf == 0), stop=lt),
                           inc=(half == 1))
                    pv_hist[g] = p.count(s_te)
                gp_ += npair
                c_attn = pv_hist[gp_ - 1]
                p.wait("vector", s_te, c_attn)
                if gJ >= 1:
                    p.wait("vector", s_st, st_rec[gJ - 1])
                rec_hist[gJ] = DVE(lambda e, sb=6 + (gJ % 2): e.reciprocal(
                    recrow[:], PS[0:1, sb * 512:sb * 512 + 512]), inc=True)
                p.wait("gpsimd", s_dve, rec_hist[gJ])
                if gJ >= 2:
                    p.wait("gpsimd", s_ld, ld_rec[gJ - 2])
                ST(lambda e, b=gJ % 2: e.dma_start(rec_scr[b].unsqueeze(0),
                                                   recrow[:]))
                st_rec[gJ] = p.count(s_st)
                p.wait("sync", s_st, st_rec[gJ])
                if gJ >= 1:
                    p.wait("sync", s_dve, dve_on[gJ - 1])
                ld_rec[gJ] = LD(lambda e, b=gJ % 2: e.dma_start(
                    bcsb[:], rec_scr[b].partition_broadcast(128)))
                p.wait("vector", s_ld, ld_rec[gJ])
                if gJ >= 2:
                    p.wait("vector", s_st, st_on[gJ - 2])
                dve_on[gJ] = DVE(lambda e, b=gJ % 2, ob=4 + (gJ % 2): e.tensor_tensor(
                    obf[:, b * 512:(b + 1) * 512], bank(ob), bcsb[:], ALU.mult),
                    inc=True)
                p.wait("gpsimd", s_dve, dve_on[gJ])
                ST(lambda e, h_=h, J_=J, b=gJ % 2: e.dma_start(
                    o_scr[J_][:, h_ * 512:(h_ + 1) * 512],
                    obf[:, b * 512:(b + 1) * 512]))
                st_on[gJ] = p.count(s_st)
                gJ += 1
            te_head_done[h] = p.count(s_te)
        phase_barrier()

        zview = A[:, 49152:65536].bitcast(F32)
        ln_state = {"i": 0, "ld": {}, "dve": {}, "st": {}, "stat": {}, "zlast": {},
                    "zall": {}, "mmlast": {}, "gpst": {}}

        def ln_block(tau, mm_emit, z_emit, store_emit):
            te_sums = {}
            dve_z = {}
            sq_hist = {}
            mm_hist = {}

            def emit_mms(c):
                if c >= 2:
                    p.wait("tensor", s_dve, dve_z[c - 2])
                mm_emit(c, c % 2)
                mm_hist[c] = p.count(s_te)

            emit_mms(0)
            for c in range(NK):
                b = c % 2
                if c + 1 < NK:
                    emit_mms(c + 1)
                p.wait("vector", s_te, mm_hist[c])
                if c == 0 and ln_state["i"] >= 1:
                    p.wait("vector", s_st, ln_state["gpst"][ln_state["i"] - 1])
                z_emit(c, b)
                c_zf = p.count(s_dve)
                dve_z[c] = c_zf
                p.wait("scalar", s_dve, dve_z[c])
                if c >= 2:
                    p.wait("scalar", s_te, te_sums[c - 2])
                ACT(lambda e, cc=c, b_=b: e.activation(
                    out=zbf[:, b_ * 512:(b_ + 1) * 512],
                    in_=zview[:, cc * 512:(cc + 1) * 512], func=AF.Copy), inc=False)
                sq_hist[c] = ACT(lambda e, cc=c, b_=b: e.activation(
                    out=sqz[:, b_ * 512:(b_ + 1) * 512],
                    in_=zview[:, cc * 512:(cc + 1) * 512], func=AF.Square))
                p.wait("tensor", s_act, sq_hist[c])
                if c == 0 and ln_state["i"] >= 1:
                    p.wait("tensor", s_dve, ln_state["stat"][ln_state["i"] - 1])
                TE(lambda e, cc=c, b_=b: e.matmul(
                    bank(2, rows=1), onescol[:], zbf[:, b_ * 512:(b_ + 1) * 512],
                    start=(cc == 0), stop=(cc == NK - 1)))
                TE(lambda e, cc=c, b_=b: e.matmul(
                    bank(3, rows=1), onescol[:], sqz[:, b_ * 512:(b_ + 1) * 512],
                    start=(cc == 0), stop=(cc == NK - 1)), inc=True)
                te_sums[c] = p.count(s_te)
            p.wait("vector", s_te, te_sums[NK - 1])
            if ln_state["i"] >= 1:
                p.wait("vector", s_st, ln_state["st"][ln_state["i"] - 1])
            DVE(lambda e: e.tensor_scalar(out=srow[0:1, 0:512], in0=PS[0:1, 1024:1536],
                                          scalar1=1.0 / D, scalar2=None, op0=ALU.mult))
            c_stat = DVE(lambda e: e.tensor_scalar(out=srow[0:1, 512:1024],
                                                   in0=PS[0:1, 1536:2048],
                                                   scalar1=1.0 / D, scalar2=None,
                                                   op0=ALU.mult), inc=True)
            p.wait("vector", s_dve, c_stat)
            ln_state["stat"][ln_state["i"]] = c_stat
            DVE(lambda e: e.tensor_tensor(srow[0:1, 1536:2048], srow[0:1, 0:512],
                                          srow[0:1, 0:512], ALU.mult))
            DVE(lambda e: e.tensor_tensor(srow[0:1, 512:1024], srow[0:1, 512:1024],
                                          srow[0:1, 1536:2048], ALU.subtract))
            newton_row(srow[0:1, 512:1024], recrow[0:1, :], LN_EPS)
            DVE(lambda e: e.tensor_tensor(srow[0:1, 0:512], srow[0:1, 0:512],
                                          recrow[0:1, :], ALU.mult))
            c_st2 = DVE(lambda e: e.tensor_scalar(out=srow[0:1, 0:512],
                                                  in0=srow[0:1, 0:512], scalar1=-1.0,
                                                  scalar2=None, op0=ALU.mult), inc=True)
            p.wait("gpsimd", s_dve, c_st2)
            lslot = ln_state["i"] % 2
            if ln_state["i"] >= 2:
                p.wait("gpsimd", s_ld, ln_state["ld"][ln_state["i"] - 2])
            ST(lambda e, b=lslot: e.dma_start(ln_scr[b, 0:512].unsqueeze(0),
                                              recrow[:]))
            c_lst = ST(lambda e, b=lslot: e.dma_start(ln_scr[b, 512:1024].unsqueeze(0),
                                                      srow[0:1, 0:512]))
            p.wait("sync", s_st, c_lst)
            if ln_state["i"] >= 1:
                p.wait("sync", s_dve, ln_state["dve"][ln_state["i"] - 1])
                p.wait("sync", s_st, ln_state["gpst"][ln_state["i"] - 1])
            LD(lambda e, b=lslot: e.dma_start(
                abc[:], ln_scr[b, 0:512].partition_broadcast(128)))
            c_lld = LD(lambda e, b=lslot: e.dma_start(
                bbc[:], ln_scr[b, 512:1024].partition_broadcast(128)))
            ln_state["ld"][ln_state["i"]] = c_lld
            ln_state["st"][ln_state["i"]] = c_lst
            p.wait("vector", s_ld, c_lld)
            p.wait("gpsimd", s_ld, c_lld)
            for c in range(NK):
                b = c % 2
                if b == 0:
                    if c >= 2:
                        p.wait("vector", s_st, p.count(s_st) - 16)
                    DVE(lambda e, cc=c, b_=b: e.tensor_tensor(
                        y1t[:, b_ * 512:(b_ + 1) * 512],
                        zview[:, cc * 512:(cc + 1) * 512], abc[:], ALU.mult))
                    c_y = DVE(lambda e, b_=b: e.tensor_tensor(
                        y1t[:, b_ * 512:(b_ + 1) * 512],
                        y1t[:, b_ * 512:(b_ + 1) * 512], bbc[:], ALU.add), inc=True)
                    p.wait("gpsimd", s_dve, c_y)
                    store_emit(c, b)
                else:
                    if c >= 2:
                        p.wait("gpsimd", s_st, p.count(s_st) - 16)
                    p.do("gpsimd", lambda e, cc=c, b_=b: e.tensor_tensor(
                        y1t[:, b_ * 512:(b_ + 1) * 512],
                        zview[:, cc * 512:(cc + 1) * 512], abc[:], ALU.mult))
                    p.do("gpsimd", lambda e, cc=c, b_=b: e.tensor_tensor(
                        y1t[:, b_ * 512:(b_ + 1) * 512],
                        y1t[:, b_ * 512:(b_ + 1) * 512], bbc[:], ALU.add))
                    store_emit(c, b)
            ln_state["dve"][ln_state["i"]] = p.count(s_dve)
            ln_state["gpst"][ln_state["i"]] = p.count(s_st)
            ln_state["zlast"][ln_state["i"]] = dve_z[NK - 1]
            ln_state["zall"][ln_state["i"]] = dict(dve_z)
            ln_state["mmlast"][ln_state["i"]] = mm_hist[NK - 1]
            ln_state["i"] += 1

        dve_wo = {}
        for k in range(NK):
            par = k % 2
            if k >= 2:
                p.wait("sync", s_dve, dve_wo[k - 2])
            c = LD(lambda e, kk=k, pr=par: e.dma_start(
                FS[:, pr * 2048:(pr + 1) * 2048], wot[kk * 128:(kk + 1) * 128, :]))
            p.wait("vector", s_ld, c)
            dve_wo[k] = DVE(lambda e, kk=k, pr=par: e.tensor_copy(
                A[:, kk * 2048:(kk + 1) * 2048], FS[:, pr * 2048:(pr + 1) * 2048]),
                inc=True)
        p.wait("tensor", s_dve, dve_wo[NK - 1])
        for tau in range(NTAU):
            hp = tau % 2
            if tau >= 2:
                p.wait("sync", s_te, ln_state["mmlast"][ln_state["i"] - 1])
            LD(lambda e, t=tau, b=hp: e.dma_start(
                A[:, 32768 + b * 8192:32768 + (b + 1) * 8192], o_scr[t]))
            for g4 in range(4):
                if tau >= 1:
                    p.wait("sync", s_dve, ln_state["zall"][ln_state["i"] - 1][4 * g4 + 3])
                LD(lambda e, g=g4, t=tau: e.dma_start(
                    FS[:, g * 2048:(g + 1) * 2048].rearrange("p (k c) -> p k c", c=512),
                    xt.rearrange("(k p) f -> p k f", p=128)
                      [:, 4 * g:4 * g + 4, t * 512:(t + 1) * 512]))
            c_lds = p.count(s_ld)
            p.wait("tensor", s_ld, c_lds)
            p.wait("vector", s_ld, c_lds)

            def pc_mm(c, b, hp_=hp):
                for hc2 in range(NK):
                    TE(lambda e, cc=c, b_=b, h_=hc2, bb=hp_: e.matmul(
                        bank(b_), A[:, h_ * 2048 + cc * 128:h_ * 2048 + (cc + 1) * 128],
                        A[:, 32768 + bb * 8192 + h_ * 512:
                          32768 + bb * 8192 + (h_ + 1) * 512],
                        start=(h_ == 0), stop=(h_ == NK - 1)), inc=(hc2 == NK - 1))

            def pc_z(c, b):
                DVE(lambda e, cc=c, b_=b: e.tensor_tensor(
                    zview[:, cc * 512:(cc + 1) * 512], bank(b_),
                    FS[:, cc * 512:(cc + 1) * 512], ALU.add), inc=True)

            def pc_store(c, b, t=tau):
                ST(lambda e, cc=c, b_=b, t_=t: e.dma_start(
                    y1_scr[cc][:, t_ * 512:(t_ + 1) * 512],
                    y1t[:, b_ * 512:(b_ + 1) * 512]))

            ln_block(tau, pc_mm, pc_z, pc_store)
        phase_barrier()

        for k in range(NK):
            LD(lambda e, kk=k: e.dma_start(A[:, kk * 2048:(kk + 1) * 2048], wt_scr[kk]))
        c_wt = p.count(s_ld)
        p.wait("tensor", s_ld, c_wt)
        y1bf = A[:, 32768:40960]
        for tau in range(NTAU):
            if tau >= 1:
                p.wait("sync", s_te, ln_state["mmlast"][ln_state["i"] - 1])
            for g4 in range(4):
                if tau >= 1:
                    p.wait("sync", s_dve, ln_state["zall"][ln_state["i"] - 1][4 * g4 + 3])
                LD(lambda e, g=g4, t=tau: e.dma_start(
                    FS[:, g * 2048:(g + 1) * 2048].rearrange("p (k c) -> p k c", c=512),
                    y1_scr.ap_or_self() if False else y1_scr
                    .rearrange("k p f -> p k f")[:, 4 * g:4 * g + 4,
                                                 t * 512:(t + 1) * 512]))
            c_lds = p.count(s_ld)
            p.wait("vector", s_ld, c_lds)
            for k in range(NK):
                DVE(lambda e, kk=k: e.tensor_copy(
                    y1bf[:, kk * 512:(kk + 1) * 512], FS[:, kk * 512:(kk + 1) * 512]),
                    inc=(kk := k) == NK - 1)
            c_cast = p.count(s_dve)
            p.wait("tensor", s_dve, c_cast)

            def pd_mm(c, b):
                for k2 in range(NK):
                    TE(lambda e, cc=c, b_=b, kk=k2: e.matmul(
                        bank(b_), A[:, kk * 2048 + cc * 128:kk * 2048 + (cc + 1) * 128],
                        y1bf[:, kk * 512:(kk + 1) * 512],
                        start=(kk == 0), stop=(kk == NK - 1)), inc=(k2 == NK - 1))

            def pd_z(c, b):
                DVE(lambda e, cc=c, b_=b: e.scalar_tensor_tensor(
                    out=zview[:, cc * 512:(cc + 1) * 512], in0=bank(b_),
                    scalar=cf[:, 34 + cc:35 + cc],
                    in1=FS[:, cc * 512:(cc + 1) * 512], op0=ALU.add, op1=ALU.add),
                    inc=True)

            def pd_store(c, b, t=tau):
                ST(lambda e, cc=c, b_=b, t_=t: e.dma_start(
                    yt[cc * 128:(cc + 1) * 128, t_ * 512:(t_ + 1) * 512],
                    y1t[:, b_ * 512:(b_ + 1) * 512]))

            ln_block(tau, pd_mm, pd_z, pd_store)

        p.wait("gpsimd", s_st, p.count(s_st))
        p.wait("gpsimd", s_ld, p.count(s_ld))

        block = es.enter_context(nc.Block())

        @block.sync
        def _(eng):
            for f in p.ops["sync"]:
                f(eng)

        @block.tensor
        def _(eng):
            for f in p.ops["tensor"]:
                f(eng)

        @block.scalar
        def _(eng):
            for f in p.ops["scalar"]:
                f(eng)

        @block.vector
        def _(eng):
            for f in p.ops["vector"]:
                f(eng)

        @block.gpsimd
        def _(eng):
            for f in p.ops["gpsimd"]:
                f(eng)

    return nc


def _perm():
    pr = np.arange(DH).reshape(DH // 2, 2).T.reshape(-1)
    return np.concatenate([h * DH + pr for h in range(H)])


def kernel(L0_attn_x, L0_attn_wq, L0_attn_wk, L0_attn_wv, L0_attn_wo,
           L0_attn_q_norm_w, L0_attn_k_norm_w, L0_attn_rope_cos, L0_attn_rope_sin,
           L0_ffn_weight, L0_ffn_bias):
    x = np.asarray(L0_attn_x, dtype=np.float32)
    wq = np.asarray(L0_attn_wq, dtype=np.float32)
    wk = np.asarray(L0_attn_wk, dtype=np.float32)
    wv = np.asarray(L0_attn_wv, dtype=np.float32)
    wo = np.asarray(L0_attn_wo, dtype=np.float32)
    qnw = np.asarray(L0_attn_q_norm_w, dtype=np.float32)
    knw = np.asarray(L0_attn_k_norm_w, dtype=np.float32)
    cos = np.asarray(L0_attn_rope_cos, dtype=np.float32)
    sin = np.asarray(L0_attn_rope_sin, dtype=np.float32)
    wf = np.asarray(L0_ffn_weight, dtype=np.float32)
    bias = np.asarray(L0_ffn_bias, dtype=np.float32)

    perm = _perm()
    prd = perm[:DH]
    wqt = np.ascontiguousarray(wq[perm, :].T)
    wkt = np.ascontiguousarray(wk[perm, :].T)
    wvt = np.ascontiguousarray(wv.T)
    wot = np.ascontiguousarray(wo.T)
    wft = np.ascontiguousarray(wf.T)

    cfin = np.zeros((128, 50), dtype=np.float32)
    cfin[:, 0] = qnw[prd]
    cfin[:, 1] = knw[prd]
    cosT = np.ascontiguousarray(cos[:H].T)
    sinT = np.ascontiguousarray(sin[:H].T)
    cfin[0:64, 2:18] = cosT
    cfin[64:128, 2:18] = cosT
    cfin[0:64, 18:34] = sinT
    cfin[64:128, 18:34] = sinT
    cfin[:, 34:50] = bias.reshape(NK, 128).T

    jj = np.arange(896)[None, :]
    pp = np.arange(128)[:, None]
    maskin = (pp <= jj - 384).astype(ml_dtypes.bfloat16)
    identin = np.eye(128, dtype=np.float32)
    maskwin = np.zeros((128, 128), dtype=np.float32)
    maskwin[np.arange(64), np.arange(64) + 64] = 1.0
    maskwin[np.arange(64) + 64, np.arange(64)] = -1.0
    onesfin = np.ones((1, 128), dtype=np.float32)
    onesbin = np.ones((128, 1), dtype=ml_dtypes.bfloat16)

    nc = build_nc()
    in_maps = []
    for b in range(8):
        in_maps.append({
            "xt": np.ascontiguousarray(x[b].T),
            "wqt": wqt, "wkt": wkt, "wvt": wvt, "wot": wot, "wft": wft,
            "cfin": cfin, "maskin": maskin, "identin": identin,
            "maskwin": maskwin, "onesfin": onesfin, "onesbin": onesbin,
        })
    res = run_bass_kernel_spmd(nc, in_maps, core_ids=list(range(8)))
    global _last_results
    if DEBUG_SCRATCH:
        _last_results = res.results
    out = np.stack([np.asarray(r["yt"]).T for r in res.results], axis=0)
    return np.ascontiguousarray(out.astype(np.float32))


# version 14
# speedup vs baseline: 1.1457x; 1.0775x over previous
import numpy as np
import ml_dtypes
import concourse.bass as bass
import concourse.mybir as mybir
from concourse.bass_utils import run_bass_kernel_spmd

F32 = mybir.dt.float32
BF16 = mybir.dt.bfloat16
I32 = mybir.dt.int32
AF = mybir.ActivationFunctionType
ALU = mybir.AluOpType

T = 4096
D = 2048
H = 16
DH = 128
NK = D // 128
NTAU = T // 512
NST = T // 128
RMS_EPS = 1e-6
LN_EPS = 1e-5
SOFTCAP = 50.0
SCL = (DH ** -0.5) / SOFTCAP
MAGIC = 0x5F3759DF

ENGS = ("sync", "tensor", "scalar", "vector", "gpsimd")
DEBUG_SCRATCH = False
_last_results = None


class Prog:
    def __init__(self):
        self.ops = {e: [] for e in ENGS}
        self.cnt = {}

    def wait(self, eng, sem, val):
        if val and val > 0:
            self.ops[eng].append(lambda e, s=sem, v=val: e.wait_ge(s, v))

    def do(self, eng, fn, inc=None, n=1):
        if inc is not None:
            self.cnt[inc] = self.cnt.get(inc, 0) + n
            tot = self.cnt[inc]
            self.ops[eng].append(lambda e, f=fn, s=inc, k=n: f(e).then_inc(s, k))
            return tot
        self.ops[eng].append(fn)
        return None

    def count(self, sem):
        return self.cnt.get(sem, 0)


def build_nc():
    nc = bass.Bass("TRN2", target_bir_lowering=False)

    xt = nc.dram_tensor("xt", [D, T], F32, kind="ExternalInput").ap()
    wqt = nc.dram_tensor("wqt", [D, D], F32, kind="ExternalInput").ap()
    wkt = nc.dram_tensor("wkt", [D, D], F32, kind="ExternalInput").ap()
    wvt = nc.dram_tensor("wvt", [D, D], F32, kind="ExternalInput").ap()
    wot = nc.dram_tensor("wot", [D, D], F32, kind="ExternalInput").ap()
    wft = nc.dram_tensor("wft", [D, D], F32, kind="ExternalInput").ap()
    cfin = nc.dram_tensor("cfin", [128, 50], F32, kind="ExternalInput").ap()
    maskin = nc.dram_tensor("maskin", [128, 896], BF16, kind="ExternalInput").ap()
    identin = nc.dram_tensor("identin", [128, 128], F32, kind="ExternalInput").ap()
    maskwin = nc.dram_tensor("maskwin", [128, 128], F32, kind="ExternalInput").ap()
    onesfin = nc.dram_tensor("onesfin", [1, 128], F32, kind="ExternalInput").ap()
    onesbin = nc.dram_tensor("onesbin", [128, 1], BF16, kind="ExternalInput").ap()
    yt = nc.dram_tensor("yt", [D, T], F32, kind="ExternalOutput").ap()

    kw = {"kind": "ExternalOutput"} if DEBUG_SCRATCH else {}
    q_scr = nc.dram_tensor("q_scr", [H, 128, T], BF16, **kw).ap()
    k_scr = nc.dram_tensor("k_scr", [H, 128, T], BF16, **kw).ap()
    v_scr = nc.dram_tensor("v_scr", [H, 128, T], BF16, **kw).ap()
    o_scr = nc.dram_tensor("o_scr", [NTAU, 128, H * 512], BF16, **kw).ap()
    y1_scr = nc.dram_tensor("y1_scr", [NK, 128, T], F32, **kw).ap()
    wt_scr = nc.dram_tensor("wt_scr", [NK, 128, D], BF16, **kw).ap()
    r_scr = nc.dram_tensor("r_scr", [2 * H, T], F32, **kw).ap()
    at_scr = nc.dram_tensor("at_scr", [128, 64], F32).ap()
    thr_scr = nc.dram_tensor("thr_scr", [2048], F32, **kw).ap()
    rec_scr = nc.dram_tensor("rec_scr", [2, 512], F32).ap()
    ln_scr = nc.dram_tensor("ln_scr", [2, 1024], F32).ap()
    u_dbg = nc.dram_tensor("u_dbg", [2 * H, 128, 32], F32, **kw).ap()
    r2_dbg = nc.dram_tensor("r2_dbg", [2 * H, 128, 32], F32, **kw).ap()
    sd_dbg = nc.dram_tensor("sd_dbg", [2 * H, 128, 32], F32, **kw).ap()
    it_dbg = nc.dram_tensor("it_dbg", [2 * H, 128, 32], F32, **kw).ap()
    z_dbg = nc.dram_tensor("z_dbg", [128, 8192], F32, **kw).ap()
    osb_dbg = nc.dram_tensor("osb_dbg", [128, 8192], F32, **kw).ap()
    xf_dbg = nc.dram_tensor("xf_dbg", [128, 8192], F32, **kw).ap()
    rall_dbg = nc.dram_tensor("rall_dbg", [128, 4096], BF16, **kw).ap()

    p = Prog()
    from contextlib import ExitStack
    es = ExitStack()
    with es:
        A = es.enter_context(nc.sbuf_tensor("A", [128, 65536], BF16))
        FS = es.enter_context(nc.sbuf_tensor("FS", [128, 8192], F32))
        Rall = es.enter_context(nc.sbuf_tensor("Rall", [128, 32 * 128], BF16))
        cf = es.enter_context(nc.sbuf_tensor("cf", [128, 50], F32))
        maskb = es.enter_context(nc.sbuf_tensor("maskb", [128, 896], BF16))
        identf = es.enter_context(nc.sbuf_tensor("identf", [128, 128], F32))
        maskw = es.enter_context(nc.sbuf_tensor("maskw", [128, 128], F32))
        onesrow = es.enter_context(nc.sbuf_tensor("onesrow", [1, 128], F32))
        onescol = es.enter_context(nc.sbuf_tensor("onescol", [128, 1], BF16))
        onescolf = es.enter_context(nc.sbuf_tensor("onescolf", [128, 1], F32))
        Atile = es.enter_context(nc.sbuf_tensor("Atile", [128, 64], F32))
        rtmp1 = es.enter_context(nc.sbuf_tensor("rtmp1", [128, 128], BF16))
        rtmp2 = es.enter_context(nc.sbuf_tensor("rtmp2", [128, 128], BF16))
        rbuf = es.enter_context(nc.sbuf_tensor("rbuf", [128, 32], F32))
        ntb = es.enter_context(nc.sbuf_tensor("ntb", [128, 32], F32))
        hhb = es.enter_context(nc.sbuf_tensor("hhb", [128, 32], F32))
        rT = es.enter_context(nc.sbuf_tensor("rT", [32, 128], F32))
        rbc = es.enter_context(nc.sbuf_tensor("rbc", [128, 1024], F32))
        abc = es.enter_context(nc.sbuf_tensor("abc", [128, 512], F32))
        bbc = es.enter_context(nc.sbuf_tensor("bbc", [128, 512], F32))
        qraw = es.enter_context(nc.sbuf_tensor("qraw", [128, 1024], BF16))
        sqbf = es.enter_context(nc.sbuf_tensor("sqbf", [128, 1024], BF16))
        qhat = es.enter_context(nc.sbuf_tensor("qhat", [128, 1024], BF16))
        recrow = es.enter_context(nc.sbuf_tensor("recrow", [1, 512], F32))
        bcsb = es.enter_context(nc.sbuf_tensor("bcsb", [128, 512], F32))
        zbf = es.enter_context(nc.sbuf_tensor("zbf", [128, 1024], BF16))
        sqz = es.enter_context(nc.sbuf_tensor("sqz", [128, 1024], BF16))
        y1t = es.enter_context(nc.sbuf_tensor("y1t", [128, 1024], F32))
        srow = es.enter_context(nc.sbuf_tensor("srow", [1, 2048], F32))
        PS = es.enter_context(nc.psum_tensor("PS", [128, 4096], F32))

        s_ld = es.enter_context(nc.semaphore("s_ld"))
        s_st = es.enter_context(nc.semaphore("s_st"))
        s_te = es.enter_context(nc.semaphore("s_te"))
        s_act = es.enter_context(nc.semaphore("s_act"))
        s_dve = es.enter_context(nc.semaphore("s_dve"))
        s_gp = es.enter_context(nc.semaphore("s_gp"))

        def bank(i, rows=128, width=512):
            return PS[0:rows, 512 * i: 512 * i + width]

        tS = FS[:, 0:2048]
        tP = FS[:, 2048:3072].bitcast(BF16)
        tPm = FS[:, 3072:4096].bitcast(BF16)
        obf = FS[:, 4096:4608].bitcast(BF16)
        gtv = A[:, 40960:43008]
        ltv = A[:, 43008:45056]
        absf = A[:, 45056:49152].bitcast(F32)
        vevac = FS[:, 5120:6144].bitcast(BF16)
        throw_ = FS[0:1, 6144:8192]

        LD = lambda fn: p.do("sync", fn, inc=s_ld, n=16)
        ST = lambda fn: p.do("gpsimd", fn, inc=s_st, n=16)
        TE = lambda fn, inc=False: p.do("tensor", fn, inc=s_te if inc else None)
        ACT = lambda fn, inc=True: p.do("scalar", fn, inc=s_act if inc else None)
        DVE = lambda fn, inc=False: p.do("vector", fn, inc=s_dve if inc else None)

        def newton_cols(src_ap, scale_mean, eps, dbg=None, dbg2=None, dbg3=None):
            def F(fn):
                c = DVE(fn, inc=True)
                p.wait("vector", s_dve, c)
                return c
            c_u = F(lambda e, s=src_ap, sm=scale_mean, ep=eps: e.tensor_scalar(
                out=ntb[:, 0:32], in0=s, scalar1=sm, scalar2=ep,
                op0=ALU.mult, op1=ALU.add))
            if dbg is not None:
                p.wait("gpsimd", s_dve, c_u)
                ST(lambda e, d=dbg: e.dma_start(d, ntb[:, 0:32]))
            F(lambda e: e.tensor_scalar(
                out=rbuf[:, 0:32].bitcast(I32), in0=ntb[:, 0:32].bitcast(I32),
                scalar1=1, scalar2=None, op0=ALU.arith_shift_right))
            F(lambda e: e.tensor_scalar(
                out=rbuf[:, 0:32].bitcast(I32), in0=rbuf[:, 0:32].bitcast(I32),
                scalar1=-1, scalar2=MAGIC, op0=ALU.mult, op1=ALU.add))
            for it_ in range(4):
                F(lambda e: e.tensor_tensor(hhb[:, 0:32], rbuf[:, 0:32],
                                            rbuf[:, 0:32], ALU.mult))
                F(lambda e: e.tensor_tensor(hhb[:, 0:32], hhb[:, 0:32],
                                            ntb[:, 0:32], ALU.mult))
                F(lambda e: e.tensor_scalar(out=hhb[:, 0:32], in0=hhb[:, 0:32],
                                            scalar1=-0.5, scalar2=1.5,
                                            op0=ALU.mult, op1=ALU.add))
                F(lambda e: e.tensor_tensor(rbuf[:, 0:32], rbuf[:, 0:32],
                                            hhb[:, 0:32], ALU.mult))
            return F(lambda e: e.tensor_copy(rbuf[:, 0:32], rbuf[:, 0:32]))

        def newton_row(src_ap, dst_ap, eps, w=512):
            u = srow[0:1, 512:512 + w]
            t = srow[0:1, 1024:1024 + w]
            def F(fn):
                c = DVE(fn, inc=True)
                p.wait("vector", s_dve, c)
                return c
            F(lambda e, s=src_ap, uu=u, ep=eps: e.tensor_scalar(
                out=uu, in0=s, scalar1=1.0, scalar2=ep, op0=ALU.mult, op1=ALU.add))
            F(lambda e, uu=u, y=dst_ap: e.tensor_scalar(
                out=y.bitcast(I32), in0=uu.bitcast(I32), scalar1=1, scalar2=None,
                op0=ALU.arith_shift_right))
            F(lambda e, y=dst_ap: e.tensor_scalar(
                out=y.bitcast(I32), in0=y.bitcast(I32), scalar1=-1, scalar2=MAGIC,
                op0=ALU.mult, op1=ALU.add))
            for _ in range(4):
                F(lambda e, y=dst_ap, tt=t: e.tensor_tensor(tt, y, y, ALU.mult))
                F(lambda e, uu=u, tt=t: e.tensor_tensor(tt, tt, uu, ALU.mult))
                F(lambda e, tt=t: e.tensor_scalar(out=tt, in0=tt, scalar1=-0.5,
                                                  scalar2=1.5, op0=ALU.mult,
                                                  op1=ALU.add))
                F(lambda e, y=dst_ap, tt=t: e.tensor_tensor(y, y, tt, ALU.mult))

        def phase_barrier():
            snap = dict(p.cnt)
            for eng in ENGS:
                for sem in (s_ld, s_st, s_te, s_act, s_dve):
                    p.wait(eng, sem, snap.get(sem, 0))

        LD(lambda e: e.dma_start(cf[:], cfin))
        LD(lambda e: e.dma_start(maskb[:], maskin))
        LD(lambda e: e.dma_start(identf[:], identin))
        LD(lambda e: e.dma_start(maskw[:], maskwin))
        LD(lambda e: e.dma_start(onesrow[:], onesfin))
        LD(lambda e: e.dma_start(onescol[:], onesbin))
        c_consts = LD(lambda e: e.dma_start(
            onescolf[:], onesfin.rearrange("o f -> f o")))

        p.wait("vector", s_ld, c_consts)
        DVE(lambda e: e.tensor_scalar(out=Atile[:, 0:16], in0=cf[:, 2:18],
                                      scalar1=cf[:, 0:1], scalar2=None, op0=ALU.mult))
        DVE(lambda e: e.tensor_scalar(out=Atile[:, 16:32], in0=cf[:, 18:34],
                                      scalar1=cf[:, 0:1], scalar2=None, op0=ALU.mult))
        DVE(lambda e: e.tensor_scalar(out=Atile[:, 32:48], in0=cf[:, 2:18],
                                      scalar1=cf[:, 1:2], scalar2=None, op0=ALU.mult))
        c_at = DVE(lambda e: e.tensor_scalar(out=Atile[:, 48:64], in0=cf[:, 18:34],
                                             scalar1=cf[:, 1:2], scalar2=None,
                                             op0=ALU.mult), inc=True)
        p.wait("gpsimd", s_dve, c_at)
        c_atst = ST(lambda e: e.dma_start(at_scr, Atile[:]))
        p.wait("sync", s_st, c_atst)
        c_atld = LD(lambda e: e.dma_start(Atile[:], at_scr))
        p.wait("vector", s_ld, c_atld)
        for hd in range(2 * H):
            base = 0 if hd < H else 32
            hc = hd % H
            DVE(lambda e, b=base, h_=hc: e.tensor_scalar(
                out=rtmp1[:], in0=identf[:], scalar1=Atile[:, b + h_:b + h_ + 1],
                scalar2=None, op0=ALU.mult))
            DVE(lambda e, b=base, h_=hc: e.tensor_scalar(
                out=rtmp2[:], in0=maskw[:], scalar1=Atile[:, b + 16 + h_:b + 17 + h_],
                scalar2=None, op0=ALU.mult))
            DVE(lambda e, hd_=hd: e.tensor_tensor(
                Rall[:, hd_ * 128:(hd_ + 1) * 128], rtmp1[:], rtmp2[:], ALU.add))

        if DEBUG_SCRATCH:
            c_rall = DVE(lambda e: e.tensor_copy(rtmp1[:, 0:1], rtmp1[:, 0:1]),
                         inc=True)
            p.wait("gpsimd", s_dve, c_rall)
            ST(lambda e: e.dma_start(rall_dbg, Rall[:]))
        dve_x = {}

        def emit_x(k):
            if k >= 1:
                p.wait("sync", s_dve, dve_x[k - 1])
            c = LD(lambda e, kk=k: e.dma_start(
                FS[:, 4096:8192], xt[kk * 128:(kk + 1) * 128, :]))
            p.wait("vector", s_ld, c)
            dve_x[k] = DVE(lambda e, kk=k: e.tensor_copy(
                A[:, kk * 4096:(kk + 1) * 4096], FS[:, 4096:8192]), inc=True)

        te_abs = {}
        act_abs = {}
        res_hist = {}
        for k in range(NK):
            par = k % 2
            if k >= 2:
                p.wait("sync", s_act, act_abs[k - 2])
            c = LD(lambda e, kk=k, pr=par: e.dma_start(
                FS[:, pr * 2048:(pr + 1) * 2048], wft[kk * 128:(kk + 1) * 128, :]))
            p.wait("scalar", s_ld, c)
            if k >= 1:
                p.wait("scalar", s_te, te_abs[k - 1])
                p.wait("scalar", s_dve, res_hist[k - 1])
            act_abs[k] = ACT(lambda e, pr=par: e.activation(
                out=absf[:, 0:2048], in_=FS[:, pr * 2048:(pr + 1) * 2048],
                func=AF.Abs))
            p.wait("vector", s_act, act_abs[k])
            if k >= 1:
                p.wait("vector", s_te, te_abs[k - 1])
            c_hi = DVE(lambda e: e.tensor_copy(gtv[:, 0:2048], absf[:, 0:2048]),
                       inc=True)
            p.wait("vector", s_dve, c_hi)
            c_res = DVE(lambda e: e.tensor_tensor(ltv[:, 0:2048], absf[:, 0:2048],
                                                  gtv[:, 0:2048], ALU.subtract),
                        inc=True)
            res_hist[k] = c_res
            p.wait("tensor", s_dve, c_res)
            for g in range(4):
                TE(lambda e, gg=g, kk=k: e.matmul(
                    bank(gg, rows=1), onescol[:], gtv[:, gg * 512:(gg + 1) * 512],
                    start=(kk == 0), stop=(kk == NK - 1)), inc=False)
                TE(lambda e, gg=g, kk=k: e.matmul(
                    bank(4 + gg, rows=1), onescol[:], ltv[:, gg * 512:(gg + 1) * 512],
                    start=(kk == 0), stop=(kk == NK - 1)), inc=(g == 3))
            te_abs[k] = p.count(s_te)
            if k < 8:
                emit_x(k)
        p.wait("vector", s_te, te_abs[NK - 1])
        thrbc = A[:, 57344:61440].bitcast(F32)
        nthrbc = A[:, 61440:65536].bitcast(F32)
        for g in range(4):
            DVE(lambda e, gg=g: e.tensor_scalar(
                out=srow[0:1, gg * 512:(gg + 1) * 512],
                in0=PS[0:1, gg * 512:(gg + 1) * 512],
                scalar1=0.7 / D, scalar2=None, op0=ALU.mult))
        c_s1 = DVE(lambda e: e.tensor_copy(srow[0:1, 0:1], srow[0:1, 0:1]), inc=True)
        p.wait("vector", s_dve, c_s1)
        for g in range(4):
            DVE(lambda e, gg=g: e.scalar_tensor_tensor(
                out=throw_[0:1, gg * 512:(gg + 1) * 512],
                in0=PS[0:1, 2048 + gg * 512:2048 + (gg + 1) * 512],
                scalar=0.7 / D, in1=srow[0:1, gg * 512:(gg + 1) * 512],
                op0=ALU.mult, op1=ALU.add))
        c_thr = DVE(lambda e: e.tensor_copy(srow[0:1, 0:1], srow[0:1, 0:1]), inc=True)
        p.wait("gpsimd", s_dve, c_thr)
        c_thrst = ST(lambda e: e.dma_start(thr_scr.rearrange("(o f) -> o f", o=1),
                                           throw_[0:1, :]))
        p.wait("sync", s_st, c_thrst)
        c_thrld = LD(lambda e: e.dma_start(thrbc[:, 0:2048],
                                           thr_scr.partition_broadcast(128)))
        p.wait("vector", s_ld, c_thrld)
        c_nthr = DVE(lambda e: e.tensor_scalar(out=nthrbc[:, 0:2048],
                                               in0=thrbc[:, 0:2048], scalar1=-1.0,
                                               scalar2=None, op0=ALU.mult), inc=True)
        wt_slotA = rbc[:].bitcast(BF16)
        wt_slotB = y1t[:].bitcast(BF16)
        dve_q = {}
        for k in range(NK):
            par = k % 2
            if k >= 2:
                p.wait("sync", s_dve, dve_q[k - 2])
            c = LD(lambda e, kk=k, pr=par: e.dma_start(
                FS[:, pr * 2048:(pr + 1) * 2048], wft[kk * 128:(kk + 1) * 128, :]))
            p.wait("vector", s_ld, c)
            if k == 0:
                p.wait("vector", s_dve, c_nthr)
            DVE(lambda e, pr=par: e.tensor_tensor(
                gtv[:, 0:2048], FS[:, pr * 2048:(pr + 1) * 2048],
                thrbc[:, 0:2048], ALU.is_gt))
            DVE(lambda e, pr=par: e.tensor_tensor(
                ltv[:, 0:2048], FS[:, pr * 2048:(pr + 1) * 2048],
                nthrbc[:, 0:2048], ALU.is_lt))
            if k >= 2:
                p.wait("vector", s_st, p.count(s_st) - 16)
            dve_q[k] = DVE(lambda e, w=(wt_slotA if par == 0 else wt_slotB):
                           e.tensor_tensor(w[:, 0:2048], gtv[:, 0:2048],
                                           ltv[:, 0:2048], ALU.subtract), inc=True)
            p.wait("gpsimd", s_dve, dve_q[k])
            ST(lambda e, kk=k, w=(wt_slotA if par == 0 else wt_slotB):
               e.dma_start(wt_scr[kk], w[:, 0:2048]))
            if k < 2:
                emit_x(8 + k)
        for k in range(10, NK):
            emit_x(k)
        phase_barrier()

        wvq = FS[:, 0:4096].bitcast(BF16)
        wvf = FS[:, 4096:5120]
        st_evac = {}
        dve_evac = {}
        gidx = 0
        dve_wv_all = {}
        for g4 in range(4):
            dve_wv = {}
            for k in range(NK):
                par = k % 2
                prev = dve_wv_all.get(g4 * NK + k - 2)
                if prev:
                    p.wait("sync", s_dve, prev)
                c = LD(lambda e, kk=k, g=g4, pr=par: e.dma_start(
                    wvf[:, pr * 512:(pr + 1) * 512],
                    wvt[kk * 128:(kk + 1) * 128, g * 512:(g + 1) * 512]))
                p.wait("vector", s_ld, c)
                if g4 > 0 and k == 0:
                    p.wait("vector", s_te, p.count(s_te))
                dve_wv[k] = DVE(lambda e, kk=k, pr=par: e.tensor_copy(
                    wvq[:, kk * 512:(kk + 1) * 512], wvf[:, pr * 512:(pr + 1) * 512]),
                    inc=True)
                dve_wv_all[g4 * NK + k] = dve_wv[k]
            p.wait("tensor", s_dve, dve_wv[NK - 1])
            for q8 in range(8):
                hb = (q8 % 2) * 4
                if gidx >= 8:
                    p.wait("tensor", s_dve, dve_evac[gidx - 5])
                for k in range(NK):
                    for s4 in range(4):
                        st_ = q8 * 4 + s4
                        TE(lambda e, kk=k, s=st_, b=hb + s4: e.matmul(
                            bank(b), A[:, kk * 4096 + s * 128:kk * 4096 + (s + 1) * 128],
                            wvq[:, kk * 512:(kk + 1) * 512],
                            start=(kk == 0), stop=(kk == NK - 1)),
                           inc=(k == NK - 1 and s4 == 3))
                c_vmm = p.count(s_te)
                p.wait("vector", s_te, c_vmm)
                for s4 in range(4):
                    st_ = q8 * 4 + s4
                    if gidx >= 2:
                        p.wait("vector", s_st, st_evac[gidx - 2])
                    dve_evac[gidx] = DVE(lambda e, b=hb + s4: e.tensor_copy(
                        vevac[:, (b % 2) * 1024:(b % 2) * 1024 + 512], bank(b)), inc=True)
                    p.wait("gpsimd", s_dve, dve_evac[gidx])
                    for j in range(4):
                        ST(lambda e, g=g4, s=st_, b=hb + s4, jj=j: e.dma_start(
                            v_scr[4 * g + jj, :, s * 128:(s + 1) * 128],
                            vevac[:, (b % 2) * 1024 + jj * 128:
                                  (b % 2) * 1024 + (jj + 1) * 128]))
                    st_evac[gidx] = p.count(s_st)
                    gidx += 1
        phase_barrier()

        wpf = FS[:, 0:2048]
        wpb = FS[:, 2048:4096].bitcast(BF16)
        stgA = FS[:, 4096:6144].bitcast(BF16)
        stgB = FS[:, 6144:8192].bitcast(BF16)
        act_hist = {}
        dve_stage = {}
        te_rope = {}
        te_bc = {}
        dve_fin = {}
        st_fin = {}
        te_hd_end = {}
        cwb_hist = {}
        c_tr_hist = {}
        c_rst_hist = {}
        g_tau = 0
        g_fin = 0
        pending_tail = [None]

        def emit_pa_tail(hd, h, proj, scr, stg, gt, gf0):
            p.wait("vector", s_te, te_rope[gt + NTAU - 1])
            if hd >= 1:
                p.wait("vector", s_te, c_tr_hist[hd - 1])
                p.wait("vector", s_st, c_rst_hist[hd - 1])
            uo = 2048 + 32 * (hd % 2)
            c_newt = newton_cols(PS[:, uo:uo + 32], 1.0 / DH, RMS_EPS)
            p.wait("tensor", s_dve, c_newt)
            c_tr = TE(lambda e: e.transpose(PS[0:32, 3584:3712], rbuf[:, 0:32],
                                            identf[:]), inc=True)
            c_tr_hist[hd] = c_tr
            p.wait("vector", s_te, c_tr)
            c_trc = DVE(lambda e: e.tensor_copy(rT[:], PS[0:32, 3584:3712]), inc=True)
            p.wait("gpsimd", s_dve, c_trc)
            c_rst = ST(lambda e, hd_=hd: e.dma_start(
                r_scr[hd_].rearrange("(a b) -> a b", a=32), rT[:]))
            c_rst_hist[hd] = c_rst
            p.wait("vector", s_dve, dve_stage[gt + NTAU - 1])
            for tau in range(NTAU):
                pr = tau % 2
                gf = gf0 + tau
                if gf >= 2:
                    p.wait("sync", s_dve, dve_fin[gf - 2])
                p.wait("sync", s_st, c_rst)
                crr = LD(lambda e, hd_=hd, t=tau, b=pr: e.dma_start(
                    rbc[:, b * 512:(b + 1) * 512],
                    r_scr[hd_, t * 512:(t + 1) * 512].partition_broadcast(128)))
                p.wait("vector", s_ld, crr)
                if gf >= 2:
                    p.wait("vector", s_st, st_fin[gf - 2])
                dve_fin[gf] = DVE(lambda e, t=tau, b=pr, sg=stg: e.tensor_tensor(
                    qhat[:, b * 512:(b + 1) * 512], sg[:, t * 512:(t + 1) * 512],
                    rbc[:, b * 512:(b + 1) * 512], ALU.mult), inc=True)
                p.wait("gpsimd", s_dve, dve_fin[gf])
                ST(lambda e, h_=h, t=tau, b=pr, sc=scr: e.dma_start(
                    sc[h_][:, t * 512:(t + 1) * 512], qhat[:, b * 512:(b + 1) * 512]))
                st_fin[gf] = p.count(s_st)
            te_hd_end[hd] = p.count(s_te)

        for hd in range(2 * H):
            proj = 0 if hd < H else 1
            h = hd % H
            wsrc = wqt if proj == 0 else wkt
            scr = q_scr if proj == 0 else k_scr
            wslot = hd % 2
            stg = stgA if hd % 2 == 0 else stgB
            if hd >= 1:
                p.wait("sync", s_dve, cwb_hist[hd - 1])
            cw = LD(lambda e, w=wsrc, h_=h: e.dma_start(
                wpf[:, 0:2048].rearrange("p (k c) -> p k c", c=128),
                w[:, h_ * 128:(h_ + 1) * 128].rearrange("(k p) c -> p k c", p=128)))
            p.wait("vector", s_ld, cw)
            if hd >= 2:
                p.wait("vector", s_te, te_hd_end[hd - 2])
            cwb = DVE(lambda e, sl=wslot: e.tensor_copy(
                wpb[:, sl * 2048:(sl + 1) * 2048], wpf[:, 0:2048]), inc=True)
            cwb_hist[hd] = cwb
            p.wait("tensor", s_dve, cwb)
            te_proj = {}

            def emit_proj(tau, wslot=wslot, g_tau0=g_tau):
                pr = tau % 2
                gi = g_tau0 + tau
                if gi >= 2:
                    p.wait("tensor", s_act, act_hist[gi - 2])
                for k in range(NK):
                    TE(lambda e, kk=k, sl=wslot, t=tau, b=pr: e.matmul(
                        bank(b), wpb[:, sl * 2048 + kk * 128:sl * 2048 + (kk + 1) * 128],
                        A[:, kk * 4096 + t * 512:kk * 4096 + (t + 1) * 512],
                        start=(kk == 0), stop=(kk == NK - 1)), inc=(k == NK - 1))
                te_proj[tau] = p.count(s_te)

            emit_proj(0)
            for tau in range(NTAU):
                pr = tau % 2
                gi = g_tau + tau
                if tau + 1 < NTAU:
                    emit_proj(tau + 1)
                p.wait("scalar", s_te, te_proj[tau])
                ACT(lambda e, b=pr: e.activation(
                    out=qraw[:, b * 512:(b + 1) * 512], in_=bank(b), func=AF.Copy),
                    inc=False)
                act_hist[gi] = ACT(lambda e, b=pr: e.activation(
                    out=sqbf[:, b * 512:(b + 1) * 512], in_=bank(b), func=AF.Square))
                p.wait("tensor", s_act, act_hist[gi])
                if gi >= 2:
                    p.wait("tensor", s_dve, dve_stage[gi - 2])
                TE(lambda e, hd_=hd, b=pr: e.matmul(
                    bank(2 + b), Rall[:, hd_ * 128:(hd_ + 1) * 128],
                    qraw[:, b * 512:(b + 1) * 512], start=True, stop=True))
                if tau == 0 and pending_tail[0] is not None:
                    pass
                uoff = 2048 + 32 * (hd % 2)
                for i4 in range(4):
                    col = 4 * tau + i4
                    TE(lambda e, b=pr, cc=col, uo=uoff: e.matmul(
                        PS[:, uo + cc:uo + cc + 1],
                        sqbf[:, b * 512 + (cc % 4) * 128:b * 512 + ((cc % 4) + 1) * 128],
                        onescol[:], start=True, stop=True), inc=(i4 == 3))
                te_rope[gi] = p.count(s_te)
                p.wait("vector", s_te, te_rope[gi])
                dve_stage[gi] = DVE(lambda e, b=pr, t=tau, sg=stg: e.tensor_copy(
                    sg[:, t * 512:(t + 1) * 512], bank(2 + b)), inc=True)
                if tau == 0 and pending_tail[0] is not None:
                    pending_tail[0]()
                    pending_tail[0] = None
            args = (hd, h, proj, scr, stg, g_tau, g_fin)
            pending_tail[0] = (lambda a=args: emit_pa_tail(*a))
            g_tau += NTAU
            g_fin += NTAU
        pending_tail[0]()
        pending_tail[0] = None
        phase_barrier()

        tanh_hist = {}
        exp_hist = {}
        pv_hist = {}
        mask_hist = {}
        dve_on = {}
        st_on = {}
        st_rec = {}
        ld_rec = {}
        rec_hist = {}
        te_head_done = {}
        ld_head = {}
        gp_ = 0
        gJ = 0
        for h in range(H):
            hp = h % 2
            if h >= 2:
                p.wait("sync", s_te, te_head_done[h - 2])
            LD(lambda e, h_=h, b=hp: e.dma_start(
                A[:, b * 4096:(b + 1) * 4096], q_scr[h_]))
            LD(lambda e, h_=h, b=hp: e.dma_start(
                A[:, 8192 + b * 4096:8192 + (b + 1) * 4096], k_scr[h_]))
            LD(lambda e, h_=h, b=hp: e.dma_start(
                A[:, 16384 + b * 4096:16384 + (b + 1) * 4096], v_scr[h_]))
            ld_head[h] = p.count(s_ld)
            qb = A[:, hp * 4096:(hp + 1) * 4096]
            kb = A[:, 8192 + hp * 4096:8192 + (hp + 1) * 4096]
            vb = A[:, 16384 + hp * 4096:16384 + (hp + 1) * 4096]
            p.wait("tensor", s_ld, ld_head[h])
            for J in range(NTAU):
                npair = 2 * J + 2
                s_hist = {}

                def emit_S(m, J_=J):
                    pr = m % 2
                    g = gp_ + m
                    if g >= 2:
                        p.wait("tensor", s_act, tanh_hist[g - 2])
                    for ii, b in ((2 * m, 2 * pr), (2 * m + 1, 2 * pr + 1)):
                        TE(lambda e, i_=ii, b_=b, Jx=J_, q_=qb, k_=kb: e.matmul(
                            bank(b_), k_[:, i_ * 128:(i_ + 1) * 128],
                            q_[:, Jx * 512:(Jx + 1) * 512], start=True, stop=True),
                           inc=(b % 2 == 1))
                    s_hist[g] = p.count(s_te)

                emit_S(0)
                for m in range(npair):
                    pr = m % 2
                    g = gp_ + m
                    if m + 1 < npair:
                        emit_S(m + 1)
                    p.wait("scalar", s_te, s_hist[g])
                    if g >= 2:
                        p.wait("scalar", s_te, pv_hist[g - 2])
                    tanh_hist[g] = ACT(lambda e, b=pr: e.activation(
                        out=tS[:, b * 1024:(b + 1) * 1024],
                        in_=PS[:, b * 1024:(b + 1) * 1024], func=AF.Tanh, scale=SCL))
                    p.wait("scalar", s_act, tanh_hist[g])
                    exp_hist[g] = ACT(lambda e, b=pr: e.activation(
                        out=tP[:, b * 1024:(b + 1) * 1024],
                        in_=tS[:, b * 1024:(b + 1) * 1024], func=AF.Exp, scale=SOFTCAP))
                    diag = (m >= 2 * J)
                    use = tP
                    if diag:
                        d0 = 2 * m - 4 * J
                        p.wait("vector", s_act, exp_hist[g])
                        if g >= 2 and (g - 2) in mask_hist:
                            p.wait("vector", s_te, pv_hist[g - 2])
                        DVE(lambda e, b=pr, d=d0: e.tensor_tensor(
                            tPm[:, b * 1024:b * 1024 + 512],
                            tP[:, b * 1024:b * 1024 + 512],
                            maskb[:, 384 - 128 * d:896 - 128 * d], ALU.mult))
                        mask_hist[g] = DVE(lambda e, b=pr, d=d0 + 1: e.tensor_tensor(
                            tPm[:, b * 1024 + 512:b * 1024 + 1024],
                            tP[:, b * 1024 + 512:b * 1024 + 1024],
                            maskb[:, 384 - 128 * d:896 - 128 * d], ALU.mult), inc=True)
                        use = tPm
                    p.wait("tensor", s_act, exp_hist[g])
                    if diag:
                        p.wait("tensor", s_dve, mask_hist[g])
                    if m == 0:
                        if gJ >= 2:
                            p.wait("tensor", s_dve, dve_on[gJ - 2])
                            p.wait("tensor", s_dve, rec_hist[gJ - 2])
                    last = (m == npair - 1)
                    ob = 4 + (gJ % 2)
                    sb = 6 + (gJ % 2)
                    for half, ii in ((0, 2 * m), (1, 2 * m + 1)):
                        TE(lambda e, i_=ii, b=pr, hf=half, v_=vb, u_=use, m_=m,
                           ob_=ob: e.matmul(
                            bank(ob_), v_[:, i_ * 128:(i_ + 1) * 128],
                            u_[:, b * 1024 + hf * 512:b * 1024 + (hf + 1) * 512],
                            start=(m_ == 0 and hf == 0), stop=False))
                    for half in (0, 1):
                        TE(lambda e, b=pr, hf=half, u_=use, m_=m,
                           sb_=sb, lt=(last and half == 1): e.matmul(
                            bank(sb_, rows=1), onescol[:],
                            u_[:, b * 1024 + hf * 512:b * 1024 + (hf + 1) * 512],
                            start=(m_ == 0 and hf == 0), stop=lt),
                           inc=(half == 1))
                    pv_hist[g] = p.count(s_te)
                gp_ += npair
                c_attn = pv_hist[gp_ - 1]
                p.wait("vector", s_te, c_attn)
                if gJ >= 1:
                    p.wait("vector", s_st, st_rec[gJ - 1])
                rec_hist[gJ] = DVE(lambda e, sb=6 + (gJ % 2): e.reciprocal(
                    recrow[:], PS[0:1, sb * 512:sb * 512 + 512]), inc=True)
                p.wait("gpsimd", s_dve, rec_hist[gJ])
                if gJ >= 2:
                    p.wait("gpsimd", s_ld, ld_rec[gJ - 2])
                ST(lambda e, b=gJ % 2: e.dma_start(rec_scr[b].unsqueeze(0),
                                                   recrow[:]))
                st_rec[gJ] = p.count(s_st)
                p.wait("sync", s_st, st_rec[gJ])
                if gJ >= 1:
                    p.wait("sync", s_dve, dve_on[gJ - 1])
                ld_rec[gJ] = LD(lambda e, b=gJ % 2: e.dma_start(
                    bcsb[:], rec_scr[b].partition_broadcast(128)))
                p.wait("vector", s_ld, ld_rec[gJ])
                if gJ >= 2:
                    p.wait("vector", s_st, st_on[gJ - 2])
                dve_on[gJ] = DVE(lambda e, b=gJ % 2, ob=4 + (gJ % 2): e.tensor_tensor(
                    obf[:, b * 512:(b + 1) * 512], bank(ob), bcsb[:], ALU.mult),
                    inc=True)
                p.wait("gpsimd", s_dve, dve_on[gJ])
                ST(lambda e, h_=h, J_=J, b=gJ % 2: e.dma_start(
                    o_scr[J_][:, h_ * 512:(h_ + 1) * 512],
                    obf[:, b * 512:(b + 1) * 512]))
                st_on[gJ] = p.count(s_st)
                gJ += 1
            te_head_done[h] = p.count(s_te)
        phase_barrier()

        zview = A[:, 49152:65536].bitcast(F32)
        ln_state = {"i": 0, "ld": {}, "dve": {}, "st": {}, "stat": {}, "zlast": {},
                    "zall": {}, "mmlast": {}, "gpst": {}, "stc": {}}

        def ln_block(tau, mm_emit, z_emit, store_emit):
            te_sums = {}
            dve_z = {}
            sq_hist = {}
            mm_hist = {}

            def emit_mms(c):
                if c >= 2:
                    p.wait("tensor", s_dve, dve_z[c - 2])
                mm_emit(c, c % 2)
                mm_hist[c] = p.count(s_te)

            emit_mms(0)
            for c in range(NK):
                b = c % 2
                if c + 1 < NK:
                    emit_mms(c + 1)
                p.wait("vector", s_te, mm_hist[c])
                if ln_state["i"] >= 1:
                    p.wait("vector", s_st, ln_state["stc"][ln_state["i"] - 1][c])
                z_emit(c, b)
                c_zf = p.count(s_dve)
                dve_z[c] = c_zf
                p.wait("scalar", s_dve, dve_z[c])
                if c >= 2:
                    p.wait("scalar", s_te, te_sums[c - 2])
                ACT(lambda e, cc=c, b_=b: e.activation(
                    out=zbf[:, b_ * 512:(b_ + 1) * 512],
                    in_=zview[:, cc * 512:(cc + 1) * 512], func=AF.Copy), inc=False)
                sq_hist[c] = ACT(lambda e, cc=c, b_=b: e.activation(
                    out=sqz[:, b_ * 512:(b_ + 1) * 512],
                    in_=zview[:, cc * 512:(cc + 1) * 512], func=AF.Square))
                p.wait("tensor", s_act, sq_hist[c])
                if c == 0 and ln_state["i"] >= 1:
                    p.wait("tensor", s_dve, ln_state["stat"][ln_state["i"] - 1])
                TE(lambda e, cc=c, b_=b: e.matmul(
                    bank(2, rows=1), onescol[:], zbf[:, b_ * 512:(b_ + 1) * 512],
                    start=(cc == 0), stop=(cc == NK - 1)))
                TE(lambda e, cc=c, b_=b: e.matmul(
                    bank(3, rows=1), onescol[:], sqz[:, b_ * 512:(b_ + 1) * 512],
                    start=(cc == 0), stop=(cc == NK - 1)), inc=True)
                te_sums[c] = p.count(s_te)
            p.wait("vector", s_te, te_sums[NK - 1])
            if ln_state["i"] >= 1:
                p.wait("vector", s_st, ln_state["st"][ln_state["i"] - 1])
            DVE(lambda e: e.tensor_scalar(out=srow[0:1, 0:512], in0=PS[0:1, 1024:1536],
                                          scalar1=1.0 / D, scalar2=None, op0=ALU.mult))
            c_stat = DVE(lambda e: e.tensor_scalar(out=srow[0:1, 512:1024],
                                                   in0=PS[0:1, 1536:2048],
                                                   scalar1=1.0 / D, scalar2=None,
                                                   op0=ALU.mult), inc=True)
            p.wait("vector", s_dve, c_stat)
            ln_state["stat"][ln_state["i"]] = c_stat
            DVE(lambda e: e.tensor_tensor(srow[0:1, 1536:2048], srow[0:1, 0:512],
                                          srow[0:1, 0:512], ALU.mult))
            DVE(lambda e: e.tensor_tensor(srow[0:1, 512:1024], srow[0:1, 512:1024],
                                          srow[0:1, 1536:2048], ALU.subtract))
            newton_row(srow[0:1, 512:1024], recrow[0:1, :], LN_EPS)
            DVE(lambda e: e.tensor_tensor(srow[0:1, 0:512], srow[0:1, 0:512],
                                          recrow[0:1, :], ALU.mult))
            c_st2 = DVE(lambda e: e.tensor_scalar(out=srow[0:1, 0:512],
                                                  in0=srow[0:1, 0:512], scalar1=-1.0,
                                                  scalar2=None, op0=ALU.mult), inc=True)
            p.wait("gpsimd", s_dve, c_st2)
            lslot = ln_state["i"] % 2
            if ln_state["i"] >= 2:
                p.wait("gpsimd", s_ld, ln_state["ld"][ln_state["i"] - 2])
            ST(lambda e, b=lslot: e.dma_start(ln_scr[b, 0:512].unsqueeze(0),
                                              recrow[:]))
            c_lst = ST(lambda e, b=lslot: e.dma_start(ln_scr[b, 512:1024].unsqueeze(0),
                                                      srow[0:1, 0:512]))
            p.wait("sync", s_st, c_lst)
            if ln_state["i"] >= 1:
                p.wait("sync", s_dve, ln_state["dve"][ln_state["i"] - 1])
                p.wait("sync", s_st, ln_state["gpst"][ln_state["i"] - 1])
            LD(lambda e, b=lslot: e.dma_start(
                abc[:], ln_scr[b, 0:512].partition_broadcast(128)))
            c_lld = LD(lambda e, b=lslot: e.dma_start(
                bbc[:], ln_scr[b, 512:1024].partition_broadcast(128)))
            ln_state["ld"][ln_state["i"]] = c_lld
            ln_state["st"][ln_state["i"]] = c_lst
            p.wait("gpsimd", s_ld, c_lld)
            stc = {}
            for c in range(NK):
                b = c % 2
                if c >= 2:
                    p.wait("gpsimd", s_st, stc[c - 2])
                c_n1 = p.do("gpsimd", lambda e, cc=c, b_=b: e.tensor_tensor(
                    y1t[:, b_ * 512:(b_ + 1) * 512],
                    zview[:, cc * 512:(cc + 1) * 512], abc[:], ALU.mult),
                    inc=s_gp)
                p.wait("gpsimd", s_gp, c_n1)
                p.do("gpsimd", lambda e, cc=c, b_=b: e.tensor_tensor(
                    y1t[:, b_ * 512:(b_ + 1) * 512],
                    y1t[:, b_ * 512:(b_ + 1) * 512], bbc[:], ALU.add))
                store_emit(c, b)
                stc[c] = p.count(s_st)
            ln_state["dve"][ln_state["i"]] = p.count(s_dve)
            ln_state["gpst"][ln_state["i"]] = p.count(s_st)
            ln_state["stc"][ln_state["i"]] = dict(stc)
            ln_state["zlast"][ln_state["i"]] = dve_z[NK - 1]
            ln_state["zall"][ln_state["i"]] = dict(dve_z)
            ln_state["mmlast"][ln_state["i"]] = mm_hist[NK - 1]
            ln_state["i"] += 1

        dve_wo = {}
        for k in range(NK):
            par = k % 2
            if k >= 2:
                p.wait("sync", s_dve, dve_wo[k - 2])
            c = LD(lambda e, kk=k, pr=par: e.dma_start(
                FS[:, pr * 2048:(pr + 1) * 2048], wot[kk * 128:(kk + 1) * 128, :]))
            p.wait("vector", s_ld, c)
            dve_wo[k] = DVE(lambda e, kk=k, pr=par: e.tensor_copy(
                A[:, kk * 2048:(kk + 1) * 2048], FS[:, pr * 2048:(pr + 1) * 2048]),
                inc=True)
        p.wait("tensor", s_dve, dve_wo[NK - 1])
        for tau in range(NTAU):
            hp = tau % 2
            if tau >= 2:
                p.wait("sync", s_te, ln_state["mmlast"][ln_state["i"] - 1])
            LD(lambda e, t=tau, b=hp: e.dma_start(
                A[:, 32768 + b * 8192:32768 + (b + 1) * 8192], o_scr[t]))
            for g4 in range(4):
                if tau >= 1:
                    p.wait("sync", s_dve, ln_state["zall"][ln_state["i"] - 1][4 * g4 + 3])
                LD(lambda e, g=g4, t=tau: e.dma_start(
                    FS[:, g * 2048:(g + 1) * 2048].rearrange("p (k c) -> p k c", c=512),
                    xt.rearrange("(k p) f -> p k f", p=128)
                      [:, 4 * g:4 * g + 4, t * 512:(t + 1) * 512]))
            c_lds = p.count(s_ld)
            p.wait("tensor", s_ld, c_lds)
            p.wait("vector", s_ld, c_lds)

            def pc_mm(c, b, hp_=hp):
                for hc2 in range(NK):
                    TE(lambda e, cc=c, b_=b, h_=hc2, bb=hp_: e.matmul(
                        bank(b_), A[:, h_ * 2048 + cc * 128:h_ * 2048 + (cc + 1) * 128],
                        A[:, 32768 + bb * 8192 + h_ * 512:
                          32768 + bb * 8192 + (h_ + 1) * 512],
                        start=(h_ == 0), stop=(h_ == NK - 1)), inc=(hc2 == NK - 1))

            def pc_z(c, b):
                DVE(lambda e, cc=c, b_=b: e.tensor_tensor(
                    zview[:, cc * 512:(cc + 1) * 512], bank(b_),
                    FS[:, cc * 512:(cc + 1) * 512], ALU.add), inc=True)

            def pc_store(c, b, t=tau):
                ST(lambda e, cc=c, b_=b, t_=t: e.dma_start(
                    y1_scr[cc][:, t_ * 512:(t_ + 1) * 512],
                    y1t[:, b_ * 512:(b_ + 1) * 512]))

            ln_block(tau, pc_mm, pc_z, pc_store)
        phase_barrier()

        for k in range(NK):
            LD(lambda e, kk=k: e.dma_start(A[:, kk * 2048:(kk + 1) * 2048], wt_scr[kk]))
        c_wt = p.count(s_ld)
        p.wait("tensor", s_ld, c_wt)
        y1bf = A[:, 32768:40960]
        for tau in range(NTAU):
            if tau >= 1:
                p.wait("sync", s_te, ln_state["mmlast"][ln_state["i"] - 1])
            for g4 in range(4):
                if tau >= 1:
                    p.wait("sync", s_dve, ln_state["zall"][ln_state["i"] - 1][4 * g4 + 3])
                LD(lambda e, g=g4, t=tau: e.dma_start(
                    FS[:, g * 2048:(g + 1) * 2048].rearrange("p (k c) -> p k c", c=512),
                    y1_scr.ap_or_self() if False else y1_scr
                    .rearrange("k p f -> p k f")[:, 4 * g:4 * g + 4,
                                                 t * 512:(t + 1) * 512]))
            c_lds = p.count(s_ld)
            p.wait("scalar", s_ld, c_lds)
            if tau >= 1:
                p.wait("scalar", s_te, ln_state["mmlast"][ln_state["i"] - 1])
            for k in range(NK):
                ACT(lambda e, kk=k: e.activation(
                    out=y1bf[:, kk * 512:(kk + 1) * 512],
                    in_=FS[:, kk * 512:(kk + 1) * 512], func=AF.Copy),
                    inc=(k == NK - 1))
            c_cast = p.count(s_act)
            p.wait("tensor", s_act, c_cast)

            def pd_mm(c, b):
                for k2 in range(NK):
                    TE(lambda e, cc=c, b_=b, kk=k2: e.matmul(
                        bank(b_), A[:, kk * 2048 + cc * 128:kk * 2048 + (cc + 1) * 128],
                        y1bf[:, kk * 512:(kk + 1) * 512],
                        start=(kk == 0), stop=(kk == NK - 1)), inc=(k2 == NK - 1))

            def pd_z(c, b):
                DVE(lambda e, cc=c, b_=b: e.scalar_tensor_tensor(
                    out=zview[:, cc * 512:(cc + 1) * 512], in0=bank(b_),
                    scalar=cf[:, 34 + cc:35 + cc],
                    in1=FS[:, cc * 512:(cc + 1) * 512], op0=ALU.add, op1=ALU.add),
                    inc=True)

            def pd_store(c, b, t=tau):
                ST(lambda e, cc=c, b_=b, t_=t: e.dma_start(
                    yt[cc * 128:(cc + 1) * 128, t_ * 512:(t_ + 1) * 512],
                    y1t[:, b_ * 512:(b_ + 1) * 512]))

            ln_block(tau, pd_mm, pd_z, pd_store)

        p.wait("gpsimd", s_st, p.count(s_st))
        p.wait("gpsimd", s_ld, p.count(s_ld))

        block = es.enter_context(nc.Block())

        @block.sync
        def _(eng):
            for f in p.ops["sync"]:
                f(eng)

        @block.tensor
        def _(eng):
            for f in p.ops["tensor"]:
                f(eng)

        @block.scalar
        def _(eng):
            for f in p.ops["scalar"]:
                f(eng)

        @block.vector
        def _(eng):
            for f in p.ops["vector"]:
                f(eng)

        @block.gpsimd
        def _(eng):
            for f in p.ops["gpsimd"]:
                f(eng)

    return nc


def _perm():
    pr = np.arange(DH).reshape(DH // 2, 2).T.reshape(-1)
    return np.concatenate([h * DH + pr for h in range(H)])


def kernel(L0_attn_x, L0_attn_wq, L0_attn_wk, L0_attn_wv, L0_attn_wo,
           L0_attn_q_norm_w, L0_attn_k_norm_w, L0_attn_rope_cos, L0_attn_rope_sin,
           L0_ffn_weight, L0_ffn_bias):
    x = np.asarray(L0_attn_x, dtype=np.float32)
    wq = np.asarray(L0_attn_wq, dtype=np.float32)
    wk = np.asarray(L0_attn_wk, dtype=np.float32)
    wv = np.asarray(L0_attn_wv, dtype=np.float32)
    wo = np.asarray(L0_attn_wo, dtype=np.float32)
    qnw = np.asarray(L0_attn_q_norm_w, dtype=np.float32)
    knw = np.asarray(L0_attn_k_norm_w, dtype=np.float32)
    cos = np.asarray(L0_attn_rope_cos, dtype=np.float32)
    sin = np.asarray(L0_attn_rope_sin, dtype=np.float32)
    wf = np.asarray(L0_ffn_weight, dtype=np.float32)
    bias = np.asarray(L0_ffn_bias, dtype=np.float32)

    perm = _perm()
    prd = perm[:DH]
    wqt = np.ascontiguousarray(wq[perm, :].T)
    wkt = np.ascontiguousarray(wk[perm, :].T)
    wvt = np.ascontiguousarray(wv.T)
    wot = np.ascontiguousarray(wo.T)
    wft = np.ascontiguousarray(wf.T)

    cfin = np.zeros((128, 50), dtype=np.float32)
    cfin[:, 0] = qnw[prd]
    cfin[:, 1] = knw[prd]
    cosT = np.ascontiguousarray(cos[:H].T)
    sinT = np.ascontiguousarray(sin[:H].T)
    cfin[0:64, 2:18] = cosT
    cfin[64:128, 2:18] = cosT
    cfin[0:64, 18:34] = sinT
    cfin[64:128, 18:34] = sinT
    cfin[:, 34:50] = bias.reshape(NK, 128).T

    jj = np.arange(896)[None, :]
    pp = np.arange(128)[:, None]
    maskin = (pp <= jj - 384).astype(ml_dtypes.bfloat16)
    identin = np.eye(128, dtype=np.float32)
    maskwin = np.zeros((128, 128), dtype=np.float32)
    maskwin[np.arange(64), np.arange(64) + 64] = 1.0
    maskwin[np.arange(64) + 64, np.arange(64)] = -1.0
    onesfin = np.ones((1, 128), dtype=np.float32)
    onesbin = np.ones((128, 1), dtype=ml_dtypes.bfloat16)

    nc = build_nc()
    in_maps = []
    for b in range(8):
        in_maps.append({
            "xt": np.ascontiguousarray(x[b].T),
            "wqt": wqt, "wkt": wkt, "wvt": wvt, "wot": wot, "wft": wft,
            "cfin": cfin, "maskin": maskin, "identin": identin,
            "maskwin": maskwin, "onesfin": onesfin, "onesbin": onesbin,
        })
    res = run_bass_kernel_spmd(nc, in_maps, core_ids=list(range(8)))
    global _last_results
    if DEBUG_SCRATCH:
        _last_results = res.results
    out = np.stack([np.asarray(r["yt"]).T for r in res.results], axis=0)
    return np.ascontiguousarray(out.astype(np.float32))
